# version 04
# baseline (speedup 1.0000x reference)
import numpy as np
import concourse.bass as bass
import concourse.mybir as mybir
import bass_rust as _br
from concourse.bass_utils import run_bass_kernel_spmd
RED = _br.ReduceOp

F32 = mybir.dt.float32
BF16 = mybir.dt.bfloat16
I32 = mybir.dt.int32
AF = mybir.ActivationFunctionType
ALU = mybir.AluOpType
AX = mybir.AxisListType

CONST_VALUES = (1e-5, 1e-6, -1.0, 0.125, 1.0, 0.5, 2.0, 0.0)

NCORES = 8
NTOK = 2048
NT = 16
D = 1024
FC = 8
GAMMA = 0.96875
BIGNEG = -float(2 ** 30)

BUILD_SUBLAYERS = (1, 2, 3, 4, 5, 6, 7, 8)
DEBUG_TAP = None
DEBUG_STOP = None

ENGINES = ("sync", "scalar", "vector", "tensor", "gpsimd")


class Ctx:

    def __init__(self, nc, block):
        self.nc = nc
        self.block = block
        self.sems = {}
        self.cnt = {}
        self.dma_sems = []
        self.phase_i = 0

    def add_sem(self, name, handle, dma=False):
        self.sems[name] = handle
        self.cnt[name] = 0
        if dma:
            self.dma_sems.append(name)

    def emit(self, **fns):
        for en in ENGINES:
            if en in fns:
                getattr(self.block, en)(fns[en])

    def barrier(self):
        self.phase_i += 1
        waits = [(self.sems[s], self.cnt[s])
                 for s in ("te", "ve", "ae", "ge", "dsy", "dsc", "dve", "dgp",
                           "cc", "wl0", "wl1", "st0", "st1", "st2", "st3")
                 if self.cnt.get(s)]

        def body(eng):
            for h, v in waits:
                eng.wait_ge(h, v)

        for en in ENGINES:
            getattr(self.block, en)(body)



def build_program():
    nc = bass.Bass("TRN2", target_bir_lowering=False, debug=False,
                   dynamic_dma_scratch_size=16384)

    dram = {}

    def din(name, shape, dtype=F32):
        dram[name] = nc.dram_tensor(name, shape, dtype, kind="ExternalInput").ap()
        return dram[name]

    def dint(name, shape, dtype=F32):
        dram[name] = nc.dram_tensor(name, shape, dtype).ap()
        return dram[name]

    din("x0", [NTOK, D])
    din("win_t", [D, 4384])
    din("wout_t", [2048, D])
    din("wg_t", [D, 4096])
    din("wu_t", [D, 4096])
    din("wd_t", [4096, D])
    din("wq_t", [D, D]); din("wk_t", [D, D]); din("wv_t", [D, D]); din("wo_t", [D, D])
    din("l0fw_t", [D, D])
    din("ww_t", [D, 3072]); din("outw_t", [D, D])
    din("retq", [D, D]); din("retk", [D, D]); din("retv", [D, D])
    din("wr_t", [D, D]); din("wk2_t", [D, D]); din("wv2_t", [D, D])
    din("wln1_b", [128, D])
    din("wmnorm_b", [128, 2048])
    din("wln2_b", [128, D])
    din("bconv_fm", [128, 18])
    din("wconv_fm", [128, 18 * 4])
    din("dtbias_fm", [32, 1])
    din("alog_fm", [32, 1])
    din("d_bcast", [128, 32])
    din("bq_b", [128, D]); din("bk_b", [128, D]); din("bv_b", [128, D]); din("bo_b", [128, D])
    din("l0fb_b", [128, D])
    din("wb_b", [128, 3072]); din("outb_b", [128, D])
    din("mur_b", [128, D]); din("muk_b", [128, D])
    din("mur_fm", [128, 8]); din("muk_fm", [128, 8])
    din("wb_fm", [128, 24])
    din("bq_fm", [128, 8]); din("bk_fm", [128, 8])
    din("ropePT", [128, 128])
    din("cosfm", [128, NTOK]); din("sinfm", [128, NTOK])
    din("k5idx", [4 * 8 * 128, 1], I32)
    din("v5idx", [32 * 128, 1], I32)
    din("k3idx", [4 * 8 * 128, 1], I32)
    din("v3idx", [32 * 128, 1], I32)
    din("ident", [128, 128])
    din("ones_row", [1, 128])
    din("ones_col", [128, 1])
    din("tril01", [128, 128])
    din("triu_big", [128, 128])
    din("dtile", [128, 128])
    din("gp_col", [128, 1])
    din("gk_col", [128, 1])
    din("rope_cos", [NTOK, 64])
    din("rope_sin", [NTOK, 64])
    din("mrow", [16, 4224], BF16)
    din("ones_bf", [1, 128], BF16)
    din("flags", [128, 4])
    din("constvals", [128, len(CONST_VALUES)])
    din("kvidx", [32 * 128, 1], I32)
    din("mbidx", [128, 1], I32)
    din("cvidx", [4, 1], I32)
    din("retidx", [1024, 1], I32)
    din("rwidx", [128, 1], I32)
    din("gtcol", [128, 16])
    din("triu01", [128, 128])
    din("tril_big", [128, 128])
    din("dtbias_b", [128, 32])
    din("alog_b", [128, 32])
    din("wln1_fm", [128, 8])
    din("wln2_fm", [128, 8])
    din("wmnorm_fm", [128, 16])
    din("cvidx2", [2304, 1], I32)
    din("wgu_t", [1024, 8192])

    y_out = nc.dram_tensor("y", [NTOK, D], F32, kind="ExternalOutput").ap()
    if DEBUG_TAP == "multi":
        dram["dbgy"] = nc.dram_tensor("dbgy", [2 * NTOK, D], F32, kind="ExternalOutput").ap()

    dint("d_z", [NTOK, 2048])
    dint("d_xbc", [18, 128, 2051])
    dint("d_ypart", [NTOK, 2048])
    dint("d_qfm", [FC, 128, NTOK])
    dint("d_kfm", [FC, 128, NTOK])
    dint("d_vtm", [NTOK, D])
    dint("d_aout", [NTOK, D])
    dint("ag_max1_i", [1, 1]); dint("ag_max1_o", [8, 1])
    dint("ag_max2_i", [1, 1]); dint("ag_max2_o", [8, 1])
    dint("ag_kv5_i", [4096, 1024]); dint("ag_kv5_o", [8 * 4096, 1024])
    dint("ag_cv_i", [2304, 4]); dint("ag_cv_o", [8 * 2304, 4])
    dint("d_act", [4096, 2048])
    dint("ag_mb_i", [128, 2048]); dint("ag_mb_o", [8 * 128, 2048])
    dint("ag_ret_i", [1024, D]); dint("ag_ret_o", [8 * 1024, D])
    dint("ag_rw1_i", [128, D]); dint("ag_rw1_o", [8 * 128, D])
    dint("ag_rw2_i", [128, D]); dint("ag_rw2_o", [8 * 128, D])

    from contextlib import ExitStack
    with ExitStack() as stack:
        ent = stack.enter_context
        xres = ent(nc.sbuf_tensor("xres", [128, NT, D], F32))
        big0 = ent(nc.sbuf_tensor("big0", [128, 8192], F32))
        big1 = ent(nc.sbuf_tensor("big1", [128, 16384], F32))
        wbuf = ent(nc.sbuf_tensor("wbuf", [128, 2, 2048], F32))
        stage = ent(nc.sbuf_tensor("stage", [128, 2, 1024], F32))
        stat = ent(nc.sbuf_tensor("stat", [128, NT, 8], F32))
        smal = ent(nc.sbuf_tensor("smal", [128, 256], F32))
        cons = ent(nc.sbuf_tensor("cons", [128, 768], F32))
        cbf = ent(nc.sbuf_tensor("cbf", [128, 128], BF16))
        mbf = ent(nc.sbuf_tensor("mbf", [1, 2, 512], BF16))
        mrow_sb = ent(nc.sbuf_tensor("mrow_sb", [1, 4352], BF16))
        sidx = ent(nc.sbuf_tensor("sidx", [128, 64], I32))
        dtb = ent(nc.sbuf_tensor("dtb", [128, 512], F32))
        msc = ent(nc.sbuf_tensor("msc", [128, 512], F32))
        qkbias = ent(nc.sbuf_tensor("qkbias", [128, 16], F32))
        ps = [ent(nc.psum_tensor("ps%d" % i, [128, 512], F32)) for i in range(8)]
        ps0, ps1, ps2, ps3, ps4, ps5, ps6, ps7 = ps
        s_bar = ent(nc.semaphore("bar"))
        s_dsy = ent(nc.semaphore("dsy"))
        s_dsc = ent(nc.semaphore("dsc"))
        s_dve = ent(nc.semaphore("dve"))
        s_dgp = ent(nc.semaphore("dgp"))
        s_te = ent(nc.semaphore("te"))
        s_ve = ent(nc.semaphore("ve"))
        s_ae = ent(nc.semaphore("ae"))
        s_ge = ent(nc.semaphore("ge"))
        s_cc = ent(nc.semaphore("cc"))
        s_wl0 = ent(nc.semaphore("wl0"))
        s_wl1 = ent(nc.semaphore("wl1"))
        s_st = [ent(nc.semaphore("st%d" % i)) for i in range(4)]
        block = ent(nc.Block())
        ctx = Ctx(nc, block)
        ctx.add_sem("bar", s_bar)
        ctx.add_sem("dsy", s_dsy, dma=True)
        ctx.add_sem("dsc", s_dsc, dma=True)
        ctx.add_sem("dve", s_dve, dma=True)
        ctx.add_sem("dgp", s_dgp, dma=True)
        ctx.add_sem("te", s_te)
        ctx.add_sem("ve", s_ve)
        ctx.add_sem("ae", s_ae)
        ctx.add_sem("ge", s_ge)
        ctx.add_sem("cc", s_cc)
        ctx.add_sem("wl0", s_wl0, dma=True)
        ctx.add_sem("wl1", s_wl1, dma=True)
        for i in range(4):
            ctx.add_sem("st%d" % i, s_st[i], dma=True)

        for i, v in enumerate(CONST_VALUES):
            nc.const_aps.register(F32, v, smal[:, 192 + i:193 + i])

        psum = [ps0, ps1, ps2, ps3, ps4, ps5, ps6, ps7]
        sb = dict(xres=xres, big0=big0, big1=big1, wbuf=wbuf, stage=stage, stat=stat,
                  smal=smal, cons=cons, cbf=cbf, mbf=mbf, mrow_sb=mrow_sb,
                  sidx=sidx, qkbias=qkbias, dtb=dtb, msc=msc)

        emit_whole_program(ctx, dram, sb, psum, y_out)

    return nc



def load_consts(ctx, dram, sb):
    cons, cbf, smal = sb["cons"], sb["cbf"], sb["smal"]
    dsy = ctx.sems["dsy"]
    n = 0

    def f(sync):
        nonlocal n
        for dst, src in (
            (cons[:, 0:128], dram["ident"][:, :]),
            (cons[:, 128:256], dram["tril01"][:, :]),
            (cons[:, 256:384], dram["triu_big"][:, :]),
            (cons[:, 384:512], dram["dtile"][:, :]),
            (cons[:, 512:640], dram["triu01"][:, :]),
            (cons[:, 640:768], dram["tril_big"][:, :]),
            (cbf[0:1, 0:128], dram["ones_bf"][:, :]),
            (smal[:, 0:4], dram["flags"][:, :]),
            (smal[:, 4:5], dram["gp_col"][:, :]),
            (smal[:, 5:6], dram["gk_col"][:, :]),
            (smal[0:1, 64:192], dram["ones_row"][:, :]),
            (smal[:, 6:7], dram["ones_col"][:, :]),
            (smal[:, 192:192 + len(CONST_VALUES)], dram["constvals"][:, :]),
        ):
            sync.dma_start(out=dst, in_=src).then_inc(dsy, 16)
            n += 1

    ctx.emit(sync=f)
    ctx.cnt["dsy"] += 16 * n
    ctx.barrier()


def load_x0(ctx, dram, sb):
    xres = sb["xres"]
    dsy = ctx.sems["dsy"]

    def f(sync):
        for tt in range(NT):
            sync.dma_start(out=xres[:, tt, :],
                           in_=dram["x0"][tt * 128:(tt + 1) * 128, :]).then_inc(dsy, 16)
    ctx.emit(sync=f)
    ctx.cnt["dsy"] += 16 * NT
    ctx.barrier()


def ln_stats(ctx, sb):
    xres, stat, big1 = sb["xres"], sb["stat"], sb["big1"]
    ae, ve = ctx.sems["ae"], ctx.sems["ve"]

    a0, v0 = ctx.cnt["ae"], ctx.cnt["ve"]

    def fa(scalar):
        for tt in range(NT):
            scalar.activation(out=big1[:, tt * 1024:(tt + 1) * 1024],
                              in_=xres[:, tt, :], func=AF.Square).then_inc(ae, 1)

    def fv(vector):
        v = v0
        for tt in range(NT):
            vector.tensor_reduce(out=stat[:, tt, 4:5], in_=xres[:, tt, :],
                                 axis=AX.X, op=ALU.add).then_inc(ve, 1); v += 1
            vector.wait_ge(ae, a0 + tt + 1)
            vector.tensor_reduce(out=stat[:, tt, 5:6], in_=big1[:, tt * 1024:(tt + 1) * 1024],
                                 axis=AX.X, op=ALU.add).then_inc(ve, 1); v += 1
        vector.wait_ge(ve, v)
        for tt in range(NT):
            vector.tensor_scalar(out=stat[:, tt, 0:1], in0=stat[:, tt, 4:5],
                                 scalar1=1.0 / D, scalar2=None, op0=ALU.mult).then_inc(ve, 1); v += 1
            vector.tensor_scalar(out=stat[:, tt, 6:7], in0=stat[:, tt, 5:6],
                                 scalar1=1.0 / D, scalar2=None, op0=ALU.mult).then_inc(ve, 1); v += 1
            vector.wait_ge(ve, v)
            vector.scalar_tensor_tensor(out=stat[:, tt, 3:4], in0=stat[:, tt, 0:1],
                                        scalar=stat[:, tt, 0:1], in1=stat[:, tt, 6:7],
                                        op0=ALU.mult, op1=ALU.subtract).then_inc(ve, 1); v += 1

    ctx.cnt["ae"] += NT
    ctx.cnt["ve"] += 5 * NT
    ctx.emit(scalar=fa, vector=fv)
    ctx.barrier()

    a0, v0 = ctx.cnt["ae"], ctx.cnt["ve"]

    def fa2(scalar):
        for tt in range(NT):
            scalar.activation(out=stat[:, tt, 7:8], in_=stat[:, tt, 3:4],
                              func=AF.Sqrt, scale=-1.0, bias=1e-5).then_inc(ae, 1)

    def fv2(vector):
        v = v0
        for tt in range(NT):
            vector.wait_ge(ae, a0 + tt + 1)
            vector.reciprocal(out=stat[:, tt, 1:2], in_=stat[:, tt, 7:8]).then_inc(ve, 1); v += 1
            vector.wait_ge(ve, v)
            vector.scalar_tensor_tensor(out=stat[:, tt, 2:3], in0=stat[:, tt, 0:1],
                                        scalar=-1.0, in1=stat[:, tt, 1:2],
                                        op0=ALU.mult, op1=ALU.mult).then_inc(ve, 1); v += 1

    ctx.cnt["ae"] += NT
    ctx.cnt["ve"] += 2 * NT
    ctx.emit(scalar=fa2, vector=fv2)
    ctx.barrier()


def transpose_to_fm(ctx, sb, psum, src_fn, dst_fn, n_fc=FC, nt=NT):
    cons = sb["cons"]
    te, ve, ae = ctx.sems["te"], ctx.sems["ve"], ctx.sems["ae"]
    t0, v0, a0 = ctx.cnt["te"], ctx.cnt["ve"], ctx.cnt["ae"]
    nops = nt * n_fc
    banks = [4, 5, 6, 7]

    def evac_done_wait(tensor, j):
        if j % 2 == 0:
            tensor.wait_ge(ve, v0 + j // 2 + 1)
        else:
            tensor.wait_ge(ae, a0 + (j + 1) // 2)

    def ft(tensor):
        tensor.drain()
        for k in range(nops):
            if k >= 4:
                evac_done_wait(tensor, k - 4)
            tt, fc = divmod(k, n_fc)
            tensor.transpose(out=psum[banks[k % 4]][:, 0:128],
                             in_=src_fn(tt)[:, fc * 128:(fc + 1) * 128],
                             identity=cons[:, 0:128]).then_inc(te, 1)

    def fv(vector):
        for k in range(0, nops, 2):
            tt, fc = divmod(k, n_fc)
            vector.wait_ge(te, t0 + k + 1)
            vector.tensor_copy(dst_fn(fc, tt), psum[banks[k % 4]][:, 0:128]).then_inc(ve, 1)

    def fs(scalar):
        for k in range(1, nops, 2):
            tt, fc = divmod(k, n_fc)
            scalar.wait_ge(te, t0 + k + 1)
            scalar.copy(out=dst_fn(fc, tt), in_=psum[banks[k % 4]][:, 0:128]).then_inc(ae, 1)

    ctx.cnt["te"] += nops
    ctx.cnt["ve"] += (nops + 1) // 2
    ctx.cnt["ae"] += nops // 2
    ctx.emit(tensor=ft, vector=fv, scalar=fs)
    ctx.barrier()


def stream_xnorm_transpose(ctx, sb, psum, dst_fn, tiles, scale_fn=None):
    xres, stat, stage, cons = sb["xres"], sb["stat"], sb["stage"], sb["cons"]
    te, ve, ae = ctx.sems["te"], ctx.sems["ve"], ctx.sems["ae"]
    t0, v0, a0 = ctx.cnt["te"], ctx.cnt["ve"], ctx.cnt["ae"]
    nt = len(tiles)

    def xt_done(i):
        return v0 + 1 if i == 0 else v0 + 5 * i - 3

    def ev_evac_done(t, m):
        if t <= nt - 2:
            return v0 + 5 * t + 3 + m
        return v0 + 5 * (nt - 1) + 2 + m

    def fxt(vector):
        def evacs(i):
            for m in range(4):
                fc = 2 * m
                k = i * 8 + fc
                vector.wait_ge(te, t0 + k + 1)
                if scale_fn is None:
                    vector.tensor_copy(dst_fn(fc, i), psum[4 + k % 4][:, 0:128]).then_inc(ve, 1)
                else:
                    vector.tensor_scalar(out=dst_fn(fc, i), in0=psum[4 + k % 4][:, 0:128],
                                         scalar1=scale_fn(fc), scalar2=None,
                                         op0=ALU.mult).then_inc(ve, 1)
        for i, tt in enumerate(tiles):
            slot = i % 2
            if i >= 2:
                vector.wait_ge(te, t0 + 8 * (i - 1))
            vector.tensor_scalar(out=stage[:, slot, :], in0=xres[:, tt, :],
                                 scalar1=stat[:, tt, 0:1], scalar2=stat[:, tt, 1:2],
                                 op0=ALU.subtract, op1=ALU.mult).then_inc(ve, 1)
            if i >= 1:
                evacs(i - 1)
        evacs(nt - 1)

    def fxT(tensor):
        tensor.drain()
        for i in range(nt):
            slot = i % 2
            tensor.wait_ge(ve, xt_done(i))
            for fc in range(8):
                k = i * 8 + fc
                if k >= 4:
                    j = k - 4
                    tj, fcj = divmod(j, 8)
                    if fcj % 2 == 0:
                        tensor.wait_ge(ve, ev_evac_done(tj, fcj // 2))
                    else:
                        tensor.wait_ge(ae, a0 + 4 * tj + (fcj + 1) // 2)
                tensor.transpose(out=psum[4 + k % 4][:, 0:128],
                                 in_=stage[:, slot, fc * 128:(fc + 1) * 128],
                                 identity=cons[:, 0:128]).then_inc(te, 1)

    def fxA(scalar):
        for k in range(1, nt * 8, 2):
            i, fc = divmod(k, 8)
            scalar.wait_ge(te, t0 + k + 1)
            if scale_fn is None:
                scalar.copy(out=dst_fn(fc, i), in_=psum[4 + k % 4][:, 0:128]).then_inc(ae, 1)
            else:
                scalar.activation(out=dst_fn(fc, i), in_=psum[4 + k % 4][:, 0:128],
                                  func=AF.Copy, scale=scale_fn(fc)).then_inc(ae, 1)

    ctx.cnt["ve"] += nt + 4 * nt
    ctx.cnt["ae"] += 4 * nt
    ctx.cnt["te"] += 8 * nt
    ctx.emit(vector=fxt, tensor=fxT, scalar=fxA)
    ctx.barrier()


def stream_load_transpose(ctx, sb, psum, src_dram_fn, dst_fn, nt, n_fc=FC):
    stage, cons = sb["stage"], sb["cons"]
    te, ve, ae = ctx.sems["te"], ctx.sems["ve"], ctx.sems["ae"]
    stm = [ctx.sems["st0"], ctx.sems["st1"]]
    t0, v0, a0 = ctx.cnt["te"], ctx.cnt["ve"], ctx.cnt["ae"]
    s0 = [ctx.cnt["st0"], ctx.cnt["st1"]]

    def odd_evac_done(t, m):
        return a0 + (n_fc // 2) * t + m + 1

    def fa(scalar):
        scalar.wait_ge(te, t0)
        for tt in range(nt):
            slot = tt % 2
            if tt >= 1:
                for m in range(n_fc // 2):
                    fc = 2 * m + 1
                    k = (tt - 1) * n_fc + fc
                    scalar.wait_ge(te, t0 + k + 1)
                    scalar.copy(out=dst_fn(fc, tt - 1),
                                in_=psum[4 + k % 4][:, 0:128]).then_inc(ae, 1)
            if tt >= 2:
                scalar.wait_ge(te, t0 + n_fc * (tt - 1))
            scalar.dma_start(out=stage[:, slot, 0:n_fc * 128],
                             in_=src_dram_fn(tt)).then_inc(stm[slot], 16)
        for m in range(n_fc // 2):
            fc = 2 * m + 1
            k = (nt - 1) * n_fc + fc
            scalar.wait_ge(te, t0 + k + 1)
            scalar.copy(out=dst_fn(fc, nt - 1),
                        in_=psum[4 + k % 4][:, 0:128]).then_inc(ae, 1)

    def ft(tensor):
        tensor.drain()
        for tt in range(nt):
            slot = tt % 2
            tensor.wait_ge(stm[slot], s0[slot] + 16 * (tt // 2 + 1))
            for fc in range(n_fc):
                k = tt * n_fc + fc
                if k >= 4:
                    j = k - 4
                    tj, fcj = divmod(j, n_fc)
                    if fcj % 2 == 0:
                        tensor.wait_ge(ve, v0 + (n_fc // 2) * tj + fcj // 2 + 1)
                    else:
                        tensor.wait_ge(ae, odd_evac_done(tj, (fcj - 1) // 2))
                tensor.transpose(out=psum[4 + k % 4][:, 0:128],
                                 in_=stage[:, slot, fc * 128:(fc + 1) * 128],
                                 identity=cons[:, 0:128]).then_inc(te, 1)

    def fv(vector):
        for k in range(0, nt * n_fc, 2):
            tt, fc = divmod(k, n_fc)
            vector.wait_ge(te, t0 + k + 1)
            vector.tensor_copy(dst_fn(fc, tt), psum[4 + k % 4][:, 0:128]).then_inc(ve, 1)

    for slot in range(2):
        ctx.cnt["st%d" % slot] += 16 * ((nt + 1 - slot) // 2)
    ctx.cnt["te"] += nt * n_fc
    ctx.cnt["ve"] += nt * n_fc // 2
    ctx.cnt["ae"] += nt * n_fc // 2
    ctx.emit(scalar=fa, tensor=ft, vector=fv)
    ctx.barrier()


def allgather(ctx, gpsimd, name):
    cc = ctx.sems["cc"]
    gpsimd.collective_compute(
        "AllGather", ALU.bypass, replica_groups=[list(range(NCORES))],
        ins=[ctx.dram[name + "_i"].opt()], outs=[ctx.dram[name + "_o"].opt()]).then_inc(cc, 1)
    ctx.cnt["cc"] += 1



def emit_quant_linear(ctx, dram, sb, psum, use_idx):
    xres, big0, big1, wbuf, stat, smal, stage = (
        sb["xres"], sb["big0"], sb["big1"], sb["wbuf"], sb["stat"], sb["smal"], sb["stage"])
    cons = sb["cons"]
    dsy, dve, dgp = ctx.sems["dsy"], ctx.sems["dve"], ctx.sems["dgp"]
    te, ve, ae, ge, cc = (ctx.sems[k] for k in ("te", "ve", "ae", "ge", "cc"))

    ln_stats(ctx, sb)

    v0 = ctx.cnt["ve"]

    def fv(vector):
        v = v0
        for tt in range(NT):
            slot = tt % 2
            if tt >= 2:
                vector.wait_ge(ve, v - 3)
            vector.tensor_scalar(out=stage[:, slot, :], in0=xres[:, tt, :],
                                 scalar1=stat[:, tt, 0:1], scalar2=stat[:, tt, 1:2],
                                 op0=ALU.subtract, op1=ALU.mult).then_inc(ve, 1); v += 1
            vector.wait_ge(ve, v)
            vector.tensor_reduce(out=smal[:, 16 + tt:17 + tt], in_=stage[:, slot, :],
                                 axis=AX.X, op=ALU.max, apply_absolute_value=True).then_inc(ve, 1); v += 1
        vector.wait_ge(ve, v)
        vector.tensor_reduce(out=smal[:, 32:33], in_=smal[:, 16:32],
                             axis=AX.X, op=ALU.max).then_inc(ve, 1)
    ctx.cnt["ve"] += 2 * NT + 1
    ctx.emit(vector=fv)
    ctx.barrier()

    if use_idx == 1:
        wl = [ctx.sems["wl0"], ctx.sems["wl1"]]
        for half in range(2):
            w0 = [ctx.cnt["wl0"], ctx.cnt["wl1"]]

            def fsy(sync, half=half, tw0=ctx.cnt["te"]):
                sync.wait_ge(te, tw0)
                for r in range(2):
                    sync.dma_start(out=wbuf[:, r, :].rearrange("p (a o) -> p a o", a=2),
                                   in_=dram["l0fw_t"][(2 * half + r) * 256:(2 * half + r + 1) * 256, :]
                                   .rearrange("(a p) o -> p a o", p=128)).then_inc(wl[r], 16)

            a0 = ctx.cnt["ae"]

            def fa(scalar, w0=w0, a0=a0):
                for r in range(2):
                    scalar.wait_ge(wl[r], w0[r] + 16)
                    scalar.activation(out=big1[:, r * 2048:(r + 1) * 2048],
                                      in_=wbuf[:, r, :], func=AF.Abs).then_inc(ae, 1)

            def fvw(vector, half=half, a0=a0):
                for r in range(2):
                    vector.wait_ge(ae, a0 + r + 1)
                    vector.tensor_reduce(out=smal[:, 40 + 2 * half + r:41 + 2 * half + r],
                                         in_=big1[:, r * 2048:(r + 1) * 2048],
                                         axis=AX.X, op=ALU.add).then_inc(ve, 1)
            ctx.cnt["wl0"] += 16
            ctx.cnt["wl1"] += 16
            ctx.cnt["ae"] += 2
            ctx.cnt["ve"] += 2
            ctx.emit(sync=fsy, scalar=fa, vector=fvw)
            ctx.barrier()

        def fvs(vector):
            vector.tensor_reduce(out=smal[:, 44:45], in_=smal[:, 40:44],
                                 axis=AX.X, op=ALU.add).then_inc(ve, 1)
        ctx.cnt["ve"] += 1
        ctx.emit(vector=fvs)
        ctx.barrier()

        t0w, v0w = ctx.cnt["te"], ctx.cnt["ve"]

        def fg(tensor):
            tensor.matmul(psum[0][0:1, 0:1], lhsT=smal[:, 6:7], rhs=smal[:, 44:45],
                          start=True, stop=True).then_inc(te, 1)

        def fgv(vector):
            vector.wait_ge(te, t0w + 1)
            vector.tensor_copy(smal[0:1, 46:47], psum[0][0:1, 0:1]).then_inc(ve, 1)

        def fg2b(tensor):
            tensor.wait_ge(ve, v0w + 1)
            tensor.matmul(psum[0][:, 0:1], lhsT=smal[0:1, 64:192], rhs=smal[0:1, 46:47],
                          start=True, stop=True).then_inc(te, 1)

        def fgv2(vector):
            vector.wait_ge(te, t0w + 2)
            vector.tensor_copy(smal[:, 45:46], psum[0][:, 0:1]).then_inc(ve, 1)
        ctx.cnt["te"] += 2
        ctx.cnt["ve"] += 2
        ctx.emit(tensor=fg, vector=fgv)
        ctx.emit(tensor=fg2b, vector=fgv2)
        ctx.barrier()

    agi = dram["ag_max%d_i" % use_idx]
    ago = dram["ag_max%d_o" % use_idx]
    g0, p0, c0 = ctx.cnt["ge"], ctx.cnt["dgp"], ctx.cnt["cc"]

    def fg2(gpsimd):
        gpsimd.tensor_reduce(out=smal[0:1, 33:34], in_=smal[:, 32:33],
                             axis=AX.C, op=ALU.max).then_inc(ge, 1)
        gpsimd.wait_ge(ge, g0 + 1)
        gpsimd.dma_start(out=agi[0:1, 0:1], in_=smal[0:1, 33:34]).then_inc(dgp, 16)
        gpsimd.wait_ge(dgp, p0 + 16)
        gpsimd.collective_compute(
            "AllGather", ALU.bypass, replica_groups=[list(range(NCORES))],
            ins=[agi.opt()], outs=[ago.opt()]).then_inc(cc, 1)
        gpsimd.wait_ge(cc, c0 + 1)
        gpsimd.dma_start(out=smal[0:8, 36:37], in_=ago[:, 0:1]).then_inc(dgp, 16)
        gpsimd.wait_ge(dgp, p0 + 32)
        gpsimd.tensor_reduce(out=smal[0:1, 37:38], in_=smal[0:8, 36:37],
                             axis=AX.C, op=ALU.max).then_inc(ge, 1)
    ctx.cnt["ge"] += 2
    ctx.cnt["dgp"] += 32
    ctx.cnt["cc"] += 1
    ctx.emit(gpsimd=fg2)
    ctx.barrier()

    t0b, v0b = ctx.cnt["te"], ctx.cnt["ve"]

    def fbc(tensor):
        tensor.matmul(psum[0][:, 0:1], lhsT=smal[0:1, 64:192], rhs=smal[0:1, 37:38],
                      start=True, stop=True).then_inc(te, 1)

    def fbcv(vector):
        vector.wait_ge(te, t0b + 1)
        vector.tensor_copy(smal[:, 34:35], psum[0][:, 0:1]).then_inc(ve, 1)
    ctx.cnt["te"] += 1
    ctx.cnt["ve"] += 1
    ctx.emit(tensor=fbc, vector=fbcv)
    ctx.barrier()

    v0 = ctx.cnt["ve"]

    def fv3(vector):
        v = v0
        vector.tensor_scalar(out=smal[:, 49:50], in0=smal[:, 34:35],
                             scalar1=1.0 / 127.0, scalar2=None, op0=ALU.mult).then_inc(ve, 1); v += 1
        vector.tensor_scalar(out=smal[:, 50:51], in0=smal[:, 45:46],
                             scalar1=1.0 / (1024.0 * 1024.0), scalar2=None, op0=ALU.mult).then_inc(ve, 1); v += 1
        vector.wait_ge(ve, v)
        vector.tensor_scalar(out=smal[:, 51:52], in0=smal[:, 50:51],
                             scalar1=0.5, scalar2=None, op0=ALU.mult).then_inc(ve, 1); v += 1
        vector.tensor_tensor(out=smal[:, 52:53], in0=smal[:, 49:50], in1=smal[:, 50:51],
                             op=ALU.mult).then_inc(ve, 1); v += 1
        vector.reciprocal(out=smal[:, 54:55], in_=smal[:, 49:50]).then_inc(ve, 1); v += 1
        vector.wait_ge(ve, v)
        vector.tensor_scalar(out=smal[:, 53:54], in0=smal[:, 51:52],
                             scalar1=-1.0, scalar2=None, op0=ALU.mult).then_inc(ve, 1); v += 1
    ctx.cnt["ve"] += 6
    ctx.emit(vector=fv3)
    ctx.barrier()

    t0, v0, a0 = ctx.cnt["te"], ctx.cnt["ve"], ctx.cnt["ae"]
    NV = 11
    SEG = NV + 4

    def round_done(tt):
        return v0 + NV if tt == 0 else v0 + SEG * tt + 7

    def even_evac_done(t, m):
        if t <= NT - 2:
            return v0 + SEG * t + 2 * NV + 1 + m
        return v0 + NV * NT + 4 * (NT - 1) + m + 1

    def fvr(vector):
        v = v0

        def evacs(tt):
            nonlocal v
            for m in range(4):
                fc = 2 * m
                k = tt * FC + fc
                vector.wait_ge(te, t0 + k + 1)
                vector.tensor_copy(big1[:, fc * 2048 + tt * 128:fc * 2048 + (tt + 1) * 128],
                                   psum[4 + k % 4][:, 0:128]).then_inc(ve, 1)
            v += 4

        iv = big0[:, 3072:3584].bitcast(I32)
        for tt in range(NT):
            slot = tt % 2
            if tt >= 2:
                vector.wait_ge(te, t0 + 8 * (tt - 1))
            vector.tensor_scalar(out=stage[:, slot, :], in0=xres[:, tt, :],
                                 scalar1=stat[:, tt, 0:1], scalar2=stat[:, tt, 1:2],
                                 op0=ALU.subtract, op1=ALU.mult).then_inc(ve, 1); v += 1
            vector.wait_ge(ve, v)
            vector.tensor_scalar(out=stage[:, slot, :], in0=stage[:, slot, :],
                                 scalar1=smal[:, 54:55], scalar2=None, op0=ALU.mult).then_inc(ve, 1); v += 1
            for s in range(2):
                hsl = slice(s * 512, (s + 1) * 512)
                vector.wait_ge(ve, v)
                vector.tensor_copy(iv[:, 0:512], stage[:, slot, hsl]).then_inc(ve, 1); v += 1
                vector.wait_ge(ve, v)
                vector.tensor_copy(big0[:, s * 512:(s + 1) * 512], iv[:, 0:512]).then_inc(ve, 1); v += 1
            vector.wait_ge(ve, v)
            vector.tensor_tensor(out=big0[:, 1024:2048], in0=stage[:, slot, :],
                                 in1=big0[:, 0:1024], op=ALU.subtract).then_inc(ve, 1); v += 1
            vector.wait_ge(ve, v)
            vector.scalar_tensor_tensor(out=stage[:, slot, :], in0=big0[:, 1024:2048],
                                        scalar=0.5, in1=big0[:, 0:1024],
                                        op0=ALU.is_ge, op1=ALU.add).then_inc(ve, 1); v += 1
            vector.tensor_scalar(out=big0[:, 2048:3072], in0=big0[:, 1024:2048],
                                 scalar1=-0.5, scalar2=None, op0=ALU.is_le).then_inc(ve, 1); v += 1
            vector.wait_ge(ve, v)
            vector.tensor_tensor(out=stage[:, slot, :], in0=stage[:, slot, :],
                                 in1=big0[:, 2048:3072], op=ALU.subtract).then_inc(ve, 1); v += 1
            vector.wait_ge(ve, v)
            vector.tensor_scalar(out=stage[:, slot, :], in0=stage[:, slot, :],
                                 scalar1=127.0, scalar2=-128.0,
                                 op0=ALU.min, op1=ALU.max).then_inc(ve, 1); v += 1
            if tt >= 1:
                evacs(tt - 1)
        evacs(NT - 1)

    def ftT(tensor):
        tensor.drain()
        for tt in range(NT):
            slot = tt % 2
            tensor.wait_ge(ve, round_done(tt))
            for fc in range(FC):
                k = tt * FC + fc
                if k >= 4:
                    j = k - 4
                    tj, fcj = divmod(j, FC)
                    if fcj % 2 == 0:
                        tensor.wait_ge(ve, even_evac_done(tj, fcj // 2))
                    else:
                        tensor.wait_ge(ae, a0 + (j // FC) * 4 + (fcj + 1) // 2)
                tensor.transpose(out=psum[4 + k % 4][:, 0:128],
                                 in_=stage[:, slot, fc * 128:(fc + 1) * 128],
                                 identity=cons[:, 0:128]).then_inc(te, 1)

    def fae(scalar):
        for k in range(1, NT * FC, 2):
            tt, fc = divmod(k, FC)
            scalar.wait_ge(te, t0 + k + 1)
            scalar.copy(out=big1[:, fc * 2048 + tt * 128:fc * 2048 + (tt + 1) * 128],
                        in_=psum[4 + k % 4][:, 0:128]).then_inc(ae, 1)

    ctx.cnt["ve"] += SEG * NT
    ctx.cnt["ae"] += NT * FC // 2
    ctx.cnt["te"] += NT * FC
    ctx.emit(vector=fvr, tensor=ftT, scalar=fae)
    ctx.barrier()

    def fvb(scalar):
        scalar.dma_start(out=big0[:, 2048:3072], in_=dram["l0fb_b"][:, :]).then_inc(
            ctx.sems["dsc"], 16)
    ctx.cnt["dsc"] += 16
    ctx.emit(scalar=fvb)
    ctx.barrier()

    t0, v0, a0 = ctx.cnt["te"], ctx.cnt["ve"], ctx.cnt["ae"]
    wl = [ctx.sems["wl0"], ctx.sems["wl1"]]
    w0 = [ctx.cnt["wl0"], ctx.cnt["wl1"]]

    def quant_ve(ob):
        return v0 + 44 * ob + 12

    def evac_ve(k):
        ob, tt = divmod(k, NT)
        return v0 + 44 * ob + 12 + 2 * (tt + 1)

    def fsy5(sync):
        sync.wait_ge(te, t0)
        for ob in range(2):
            if ob > 0:
                sync.wait_ge(te, t0 + NT * FC * ob)
            for r in range(2):
                sync.dma_start(out=wbuf[:, r, :].rearrange("p (a o) -> p a o", a=4),
                               in_=dram["l0fw_t"][r * 512:(r + 1) * 512, ob * 512:(ob + 1) * 512]
                               .rearrange("(a p) o -> p a o", p=128)).then_inc(wl[r], 16)

    def fvec(vector):
        v = v0
        for ob in range(2):
            for r in range(2):
                vector.wait_ge(wl[r], w0[r] + 16 * (ob + 1))
                for s in range(2):
                    sl = slice(s * 1024, (s + 1) * 1024)
                    vector.wait_ge(ve, v)
                    vector.tensor_scalar(out=big0[:, 0:1024], in0=wbuf[:, r, sl],
                                         scalar1=smal[:, 51:52], scalar2=None,
                                         op0=ALU.is_gt).then_inc(ve, 1); v += 1
                    vector.tensor_scalar(out=big0[:, 1024:2048], in0=wbuf[:, r, sl],
                                         scalar1=smal[:, 53:54], scalar2=None,
                                         op0=ALU.is_lt).then_inc(ve, 1); v += 1
                    vector.wait_ge(ve, v)
                    vector.tensor_tensor(out=wbuf[:, r, sl], in0=big0[:, 0:1024],
                                         in1=big0[:, 1024:2048], op=ALU.subtract).then_inc(ve, 1); v += 1
            for tt in range(NT):
                k = ob * NT + tt
                vector.wait_ge(ae, a0 + k + 1)
                bank = tt % 2
                vector.tensor_tensor(out=psum[bank][:, :], in0=psum[bank][:, :],
                                     in1=big0[:, 2048 + ob * 512:2048 + (ob + 1) * 512],
                                     op=ALU.add).then_inc(ve, 1); v += 1
                vector.wait_ge(ve, v)
                vector.tensor_tensor(out=xres[:, tt, ob * 512:(ob + 1) * 512],
                                     in0=xres[:, tt, ob * 512:(ob + 1) * 512],
                                     in1=psum[bank][:, :], op=ALU.add).then_inc(ve, 1); v += 1

    def ftm(tensor):
        tensor.drain()
        for ob in range(2):
            tensor.wait_ge(ve, quant_ve(ob))
            for tt in range(NT):
                k = ob * NT + tt
                bank = tt % 2
                if k >= 2:
                    tensor.wait_ge(ve, evac_ve(k - 2))
                for fc in range(FC):
                    r, ic = divmod(fc, 4)
                    tensor.matmul(psum[bank][:, :],
                                  lhsT=big1[:, fc * 2048 + tt * 128: fc * 2048 + (tt + 1) * 128],
                                  rhs=wbuf[:, r, ic * 512:(ic + 1) * 512],
                                  start=(fc == 0), stop=(fc == FC - 1)).then_inc(te, 1)

    def fev(scalar):
        for ob in range(2):
            for tt in range(NT):
                k = ob * NT + tt
                bank = tt % 2
                scalar.wait_ge(te, t0 + FC * (k + 1))
                scalar.activation(out=psum[bank][:, :], in_=psum[bank][:, :], func=AF.Copy,
                                  scale=smal[:, 52:53]).then_inc(ae, 1)

    ctx.cnt["wl0"] += 32
    ctx.cnt["wl1"] += 32
    ctx.cnt["te"] += 2 * NT * FC
    ctx.cnt["ae"] += 2 * NT
    ctx.cnt["ve"] += 2 * (12 + 2 * NT)
    ctx.emit(sync=fsy5, vector=fvec, tensor=ftm, scalar=fev)
    ctx.barrier()



def proj_xs(ctx, sb, psum, dram_w, src_fn, nt, out_cols, ev_scalar, ev_vector,
            n_ae_per, n_ve_per, banks=(0, 1, 2, 3), in_chunks=8, w_row0=0, order="sv"):
    wbuf = sb["wbuf"]
    nob = out_cols // 512
    te, ve, ae = ctx.sems["te"], ctx.sems["ve"], ctx.sems["ae"]
    t0, v0, a0 = ctx.cnt["te"], ctx.cnt["ve"], ctx.cnt["ae"]
    wl = [ctx.sems["wl0"], ctx.sems["wl1"]]
    w0 = [ctx.cnt["wl0"], ctx.cnt["wl1"]]
    nbank = len(banks)
    last_is_vector = (n_ve_per > 0) if order == "sv" else False
    free_per = n_ve_per if last_is_vector else (n_ae_per or n_ve_per)
    free_sem = ve if last_is_vector else (ae if n_ae_per else ve)
    free_base = v0 if last_is_vector else (a0 if n_ae_per else v0)
    if order == "vs":
        free_per, free_sem, free_base = n_ve_per, ve, v0
    nin = in_chunks * 128

    def fsy(sync):
        sync.wait_ge(te, t0)
        for ob in range(nob):
            if ob > 0:
                sync.wait_ge(te, t0 + 2 + nt * in_chunks * ob)
            for r in range(2):
                sync.dma_start(
                    out=wbuf[:, r, :].rearrange("p (a o) -> p a o", a=4),
                    in_=dram_w[w_row0 + r * (nin // 2):w_row0 + (r + 1) * (nin // 2),
                               ob * 512:(ob + 1) * 512]
                    .rearrange("(a p) o -> p a o", p=128)).then_inc(wl[r], 16)

    def ft(tensor):
        tensor.drain()
        tensor.wait_ge(wl[0], w0[0] + 16)
        for _ in range(2):
            tensor.matmul(psum[banks[0]][:, :], lhsT=src_fn(0, 0, 0),
                          rhs=wbuf[:, 0, 0:512], start=True, stop=True).then_inc(te, 1)
        for ob in range(nob):
            tensor.wait_ge(wl[0], w0[0] + 16 * (ob + 1))
            tensor.wait_ge(wl[1], w0[1] + 16 * (ob + 1))
            for tb in range(nt):
                k = ob * nt + tb
                bank = banks[k % nbank]
                if k >= nbank:
                    tensor.wait_ge(free_sem, free_base + free_per * (k - nbank + 1))
                for fc in range(in_chunks):
                    r, ic = divmod(fc, in_chunks // 2)
                    tensor.matmul(psum[bank][:, :],
                                  lhsT=src_fn(fc, ob, tb),
                                  rhs=wbuf[:, r, ic * 512:(ic + 1) * 512],
                                  start=(fc == 0), stop=(fc == in_chunks - 1)).then_inc(te, 1)

    def fa(scalar):
        for k in range(nob * nt):
            ob, tb = divmod(k, nt)
            if order == "vs" and n_ve_per:
                scalar.wait_ge(ve, v0 + n_ve_per * (k + 1))
            else:
                scalar.wait_ge(te, t0 + 2 + in_chunks * (k + 1))
            ev_scalar(scalar, banks[k % nbank], ob, tb, a0 + n_ae_per * k)

    def fv(vector):
        for k in range(nob * nt):
            ob, tb = divmod(k, nt)
            if order == "sv" and n_ae_per:
                vector.wait_ge(ae, a0 + n_ae_per * (k + 1))
            else:
                vector.wait_ge(te, t0 + 2 + in_chunks * (k + 1))
            ev_vector(vector, banks[k % nbank], ob, tb, v0 + n_ve_per * k)

    ctx.cnt["wl0"] += 16 * nob
    ctx.cnt["wl1"] += 16 * nob
    ctx.cnt["te"] += nob * nt * in_chunks + 2
    ctx.cnt["ae"] += nob * nt * n_ae_per
    ctx.cnt["ve"] += nob * nt * n_ve_per
    fns = {"sync": fsy, "tensor": ft}
    if n_ae_per:
        fns["scalar"] = fa
    if n_ve_per:
        fns["vector"] = fv
    ctx.emit(**fns)
    ctx.barrier()


def proj_ws(ctx, sb, psum, dram_w, xfm_fn, ntb, out_chunks, ev_scalar, ev_vector,
            n_ae_per, n_ve_per, banks=(0, 1, 2, 3), in_chunks=8, tokw=512, order="sv",
            w_col0=0):
    wbuf = sb["wbuf"]
    te, ve, ae = ctx.sems["te"], ctx.sems["ve"], ctx.sems["ae"]
    t0, v0, a0 = ctx.cnt["te"], ctx.cnt["ve"], ctx.cnt["ae"]
    wl = [ctx.sems["wl0"], ctx.sems["wl1"]]
    w0 = [ctx.cnt["wl0"], ctx.cnt["wl1"]]
    nbank = len(banks)
    last_is_vector = (n_ve_per > 0) if order == "sv" else False
    free_per = n_ve_per if last_is_vector else (n_ae_per or n_ve_per)
    free_sem = ve if last_is_vector else (ae if n_ae_per else ve)
    free_base = v0 if last_is_vector else (a0 if n_ae_per else v0)
    if order == "vs":
        free_per, free_sem, free_base = n_ve_per, ve, v0

    def fsy(sync):
        sync.wait_ge(te, t0)
        for oc in range(out_chunks):
            s = oc % 2
            if oc >= 2:
                sync.wait_ge(te, t0 + ntb * in_chunks * (oc - 1))
            sync.dma_start(
                out=wbuf[:, s, 0:in_chunks * 128].rearrange("p (a o) -> p a o", o=128),
                in_=dram_w[0:in_chunks * 128, w_col0 + oc * 128:w_col0 + (oc + 1) * 128]
                .rearrange("(a p) o -> p a o", p=128)).then_inc(wl[s], 16)

    def ft(tensor):
        tensor.drain()
        for oc in range(out_chunks):
            s = oc % 2
            tensor.wait_ge(wl[s], w0[s] + 16 * (oc // 2 + 1))
            for tb in range(ntb):
                k = oc * ntb + tb
                bank = banks[k % nbank]
                if k >= nbank:
                    tensor.wait_ge(free_sem, free_base + free_per * (k - nbank + 1))
                for fc in range(in_chunks):
                    tensor.matmul(psum[bank][:, 0:tokw],
                                  lhsT=wbuf[:, s, fc * 128:(fc + 1) * 128],
                                  rhs=xfm_fn(fc, tb),
                                  start=(fc == 0), stop=(fc == in_chunks - 1)).then_inc(te, 1)

    def fa(scalar):
        for k in range(out_chunks * ntb):
            oc, tb = divmod(k, ntb)
            if order == "vs" and n_ve_per:
                scalar.wait_ge(ve, v0 + n_ve_per * (k + 1))
            else:
                scalar.wait_ge(te, t0 + in_chunks * (k + 1))
            ev_scalar(scalar, banks[k % nbank], oc, tb, a0 + n_ae_per * k)

    def fv(vector):
        for k in range(out_chunks * ntb):
            oc, tb = divmod(k, ntb)
            if order == "sv" and n_ae_per:
                vector.wait_ge(ae, a0 + n_ae_per * (k + 1))
            else:
                vector.wait_ge(te, t0 + in_chunks * (k + 1))
            ev_vector(vector, banks[k % nbank], oc, tb, v0 + n_ve_per * k)

    ctx.cnt["wl0"] += 16 * ((out_chunks + 1) // 2)
    ctx.cnt["wl1"] += 16 * (out_chunks // 2)
    ctx.cnt["te"] += out_chunks * ntb * in_chunks
    ctx.cnt["ae"] += out_chunks * ntb * n_ae_per
    ctx.cnt["ve"] += out_chunks * ntb * n_ve_per
    fns = {"sync": fsy, "tensor": ft}
    if n_ae_per:
        fns["scalar"] = fa
    if n_ve_per:
        fns["vector"] = fv
    ctx.emit(**fns)
    ctx.barrier()



def emit_rwkv(ctx, dram, sb, psum, use_idx):
    xres, big0, big1, stat, smal = (sb["xres"], sb["big0"], sb["big1"], sb["stat"], sb["smal"])
    cons, stage = sb["cons"], sb["stage"]
    dsy, dsc, dgp = ctx.sems["dsy"], ctx.sems["dsc"], ctx.sems["dgp"]
    te, ve, ae, ge, cc = (ctx.sems[k] for k in ("te", "ve", "ae", "ge", "cc"))
    agi = dram["ag_rw%d_i" % use_idx]
    ago = dram["ag_rw%d_o" % use_idx]

    ln_stats(ctx, sb)

    y0, v0 = ctx.cnt["dsy"], ctx.cnt["ve"]

    def fmu(sync):
        sync.dma_start(out=smal[:, 160:168], in_=dram["mur_fm"][:, :]).then_inc(dsy, 16)
        sync.dma_start(out=smal[:, 168:176], in_=dram["muk_fm"][:, :]).then_inc(dsy, 16)

    def fmv(vector):
        vector.wait_ge(dsy, y0 + 32)
        vector.tensor_scalar(out=smal[:, 176:184], in0=smal[:, 160:168],
                             scalar1=-1.0, scalar2=1.0, op0=ALU.mult, op1=ALU.add).then_inc(ve, 1)
        vector.tensor_scalar(out=smal[:, 184:192], in0=smal[:, 168:176],
                             scalar1=-1.0, scalar2=1.0, op0=ALU.mult, op1=ALU.add).then_inc(ve, 1)
    ctx.cnt["dsy"] += 32
    ctx.cnt["ve"] += 2
    ctx.emit(sync=fmu, vector=fmv)

    v0 = ctx.cnt["ve"]

    def fb1(vector):
        vector.tensor_scalar(out=big1[:, 0:1024], in0=xres[:, 15, :],
                             scalar1=stat[:, 15, 0:1], scalar2=stat[:, 15, 1:2],
                             op0=ALU.subtract, op1=ALU.mult).then_inc(ve, 1)
    ctx.cnt["ve"] += 1
    ctx.emit(vector=fb1)
    ctx.barrier()

    g0, p0, c0 = ctx.cnt["ge"], ctx.cnt["dgp"], ctx.cnt["cc"]
    y0 = ctx.cnt["dsy"]

    def fb2i(sync):
        sync.dma_start(out=sb["sidx"][:, 0:1], in_=dram["rwidx"][:, :]).then_inc(dsy, 16)

    def fb2(gpsimd):
        gpsimd.dma_start(out=agi[:, :], in_=big1[:, 0:1024]).then_inc(dgp, 16)
        gpsimd.wait_ge(dgp, p0 + 16)
        gpsimd.collective_compute(
            "AllGather", ALU.bypass, replica_groups=[list(range(NCORES))],
            ins=[agi.opt()], outs=[ago.opt()]).then_inc(cc, 1)
        gpsimd.wait_ge(cc, c0 + 1)
        gpsimd.wait_ge(dsy, y0 + 16)
        gpsimd.indirect_dma_start(
            out=big1[:, 1024:2048], out_offset=None,
            in_=ago[:, :], in_offset=bass.IndirectOffsetOnAxis(ap=sb["sidx"][:, 0:1], axis=0),
        ).then_inc(dgp, 16)
    ctx.cnt["dgp"] += 32
    ctx.cnt["cc"] += 1
    ctx.cnt["dsy"] += 16
    ctx.emit(sync=fb2i)
    ctx.emit(gpsimd=fb2)
    ctx.barrier()

    t0, v0, a0 = ctx.cnt["te"], ctx.cnt["ve"], ctx.cnt["ae"]

    def fb3t(tensor):
        for fc in range(8):
            if fc >= 4:
                tensor.wait_ge(ve, v0 + fc - 3)
            tensor.transpose(out=psum[4 + fc % 4][:, 0:128],
                             in_=big1[:, 1024 + fc * 128:1024 + (fc + 1) * 128],
                             identity=cons[:, 0:128]).then_inc(te, 1)

    def fb3v(vector):
        v = v0
        for fc in range(8):
            vector.wait_ge(te, t0 + fc + 1)
            vector.tensor_copy(smal[:, 144 + fc:145 + fc],
                               psum[4 + fc % 4][:, 127:128]).then_inc(ve, 1); v += 1
        vector.wait_ge(ve, v)
        vector.tensor_scalar(out=smal[:, 144:152], in0=smal[:, 144:152],
                             scalar1=smal[:, 0:1], scalar2=None, op0=ALU.mult).then_inc(ve, 1)
    ctx.cnt["te"] += 8
    ctx.cnt["ve"] += 9
    ctx.emit(tensor=fb3t, vector=fb3v)
    ctx.barrier()

    for half in range(2):
        tt0 = half * 8
        t0, v0, a0 = ctx.cnt["te"], ctx.cnt["ve"], ctx.cnt["ae"]

        def fxt(vector, tt0=tt0, t0=t0, v0=v0):
            v = v0
            for i in range(8):
                slot = i % 2
                if i >= 2:
                    vector.wait_ge(te, t0 + 8 * (i - 1))
                vector.tensor_scalar(out=stage[:, slot, :], in0=xres[:, tt0 + i, :],
                                     scalar1=stat[:, tt0 + i, 0:1],
                                     scalar2=stat[:, tt0 + i, 1:2],
                                     op0=ALU.subtract, op1=ALU.mult).then_inc(ve, 1); v += 1
                if i >= 1:
                    for m in range(4):
                        fc = 2 * m
                        k = (i - 1) * 8 + fc
                        vector.wait_ge(te, t0 + k + 1)
                        vector.tensor_copy(
                            big1[:, fc * 1024 + (i - 1) * 128:fc * 1024 + i * 128],
                            psum[4 + k % 4][:, 0:128]).then_inc(ve, 1); v += 1
            for m in range(4):
                fc = 2 * m
                k = 7 * 8 + fc
                vector.wait_ge(te, t0 + k + 1)
                vector.tensor_copy(big1[:, fc * 1024 + 7 * 128:fc * 1024 + 8 * 128],
                                   psum[4 + k % 4][:, 0:128]).then_inc(ve, 1); v += 1

        def xt_done(i):
            return v0 + 1 if i == 0 else v0 + 5 * i - 3

        def ev_evac_done(t, m):
            if t <= 6:
                return v0 + 5 * t + 3 + m
            return v0 + 37 + m

        def fxT(tensor, tt0=tt0, t0=t0, v0=v0, a0=a0):
            tensor.drain()
            for i in range(8):
                slot = i % 2
                tensor.wait_ge(ve, xt_done(i))
                for fc in range(8):
                    k = i * 8 + fc
                    if k >= 4:
                        j = k - 4
                        tj, fcj = divmod(j, 8)
                        if fcj % 2 == 0:
                            tensor.wait_ge(ve, ev_evac_done(tj, fcj // 2))
                        else:
                            tensor.wait_ge(ae, a0 + 4 * tj + (fcj + 1) // 2)
                    tensor.transpose(out=psum[4 + k % 4][:, 0:128],
                                     in_=stage[:, slot, fc * 128:(fc + 1) * 128],
                                     identity=cons[:, 0:128]).then_inc(te, 1)

        def fxA(scalar, tt0=tt0, t0=t0):
            for k in range(1, 64, 2):
                i, fc = divmod(k, 8)
                scalar.wait_ge(te, t0 + k + 1)
                scalar.copy(out=big1[:, fc * 1024 + i * 128:fc * 1024 + (i + 1) * 128],
                            in_=psum[4 + k % 4][:, 0:128]).then_inc(ae, 1)

        ctx.cnt["ve"] += 8 + 32
        ctx.cnt["ae"] += 32
        ctx.cnt["te"] += 64
        ctx.emit(vector=fxt, tensor=fxT, scalar=fxA)
        ctx.barrier()

        bnd = 144 if half == 0 else 152
        v0 = ctx.cnt["ve"]

        def fmx2(vector, bnd=bnd, v0=v0):
            v = v0
            for fc in range(8):
                for (buf, dstoff, mu, omu) in ((big0, 0, 160, 176), (big1, 8192, 168, 184)):
                    do = dstoff + fc * 1024
                    vector.tensor_scalar(
                        out=buf[:, do + 1:do + 1024],
                        in0=big1[:, fc * 1024:fc * 1024 + 1023],
                        scalar1=smal[:, omu + fc:omu + fc + 1], scalar2=None,
                        op0=ALU.mult).then_inc(ve, 1); v += 1
                    vector.tensor_scalar(
                        out=buf[:, do:do + 1],
                        in0=smal[:, bnd + fc:bnd + fc + 1],
                        scalar1=smal[:, omu + fc:omu + fc + 1], scalar2=None,
                        op0=ALU.mult).then_inc(ve, 1); v += 1
                    vector.wait_ge(ve, v)
                    vector.scalar_tensor_tensor(
                        out=buf[:, do:do + 1024],
                        in0=big1[:, fc * 1024:(fc + 1) * 1024],
                        scalar=smal[:, mu + fc:mu + fc + 1],
                        in1=buf[:, do:do + 1024],
                        op0=ALU.mult, op1=ALU.add).then_inc(ve, 1); v += 1
            vector.wait_ge(ve, v)
            for fc in range(8):
                vector.tensor_copy(smal[:, 152 + fc:153 + fc],
                                   big1[:, fc * 1024 + 1023:fc * 1024 + 1024]).then_inc(ve, 1); v += 1
        ctx.cnt["ve"] += 8 * 2 * 3 + 8
        ctx.emit(vector=fmx2)
        ctx.barrier()

        def src_r(fc, ob, tb):
            return big0[:, fc * 1024 + tb * 128: fc * 1024 + (tb + 1) * 128]

        def ev_r_s(scalar, bank, ob, tb, abase):
            scalar.activation(out=big1[:, tb * 1024 + ob * 512: tb * 1024 + (ob + 1) * 512],
                              in_=psum[bank][:, :], func=AF.Sigmoid).then_inc(ae, 1)
        proj_xs(ctx, sb, psum, dram["wr_t"], src_r, 8, 1024, ev_r_s, None, 1, 0)

        def xfm_k(fc, tb):
            return big1[:, 8192 + fc * 1024 + tb * 512: 8192 + fc * 1024 + (tb + 1) * 512]

        def ev_k_s(scalar, bank, oc, tb, abase):
            scalar.activation(out=big0[:, oc * 1024 + tb * 512: oc * 1024 + (tb + 1) * 512],
                              in_=psum[bank][:, 0:512], func=AF.Relu).then_inc(ae, 1)

        def ev_k_v(vector, bank, oc, tb, vbase):
            sl = slice(oc * 1024 + tb * 512, oc * 1024 + (tb + 1) * 512)
            vector.tensor_tensor(out=big0[:, sl], in0=big0[:, sl], in1=big0[:, sl],
                                 op=ALU.mult).then_inc(ve, 1)
        proj_ws(ctx, sb, psum, dram["wk2_t"], xfm_k, 2, 8, ev_k_s, ev_k_v, 1, 1)

        def src_kv(fc, ob, tb):
            return big0[:, fc * 1024 + tb * 128: fc * 1024 + (tb + 1) * 128]

        def ev_kv_v(vector, bank, ob, tb, vbase, tt0=tt0):
            vector.tensor_tensor(out=psum[bank][:, :], in0=psum[bank][:, :],
                                 in1=big1[:, tb * 1024 + ob * 512: tb * 1024 + (ob + 1) * 512],
                                 op=ALU.mult).then_inc(ve, 1)
            vector.wait_ge(ve, vbase + 1)
            vector.tensor_tensor(out=xres[:, tt0 + tb, ob * 512:(ob + 1) * 512],
                                 in0=xres[:, tt0 + tb, ob * 512:(ob + 1) * 512],
                                 in1=psum[bank][:, :], op=ALU.add).then_inc(ve, 1)
        proj_xs(ctx, sb, psum, dram["wv2_t"], src_kv, 8, 1024, None, ev_kv_v, 0, 2)



def emit_mamba(ctx, dram, sb, psum):
    xres, big0, big1, stat, smal, cons, stage, sidx = (
        sb["xres"], sb["big0"], sb["big1"], sb["stat"], sb["smal"], sb["cons"],
        sb["stage"], sb["sidx"])
    dtb, msc = sb["dtb"], sb["msc"]
    dsy, dsc, dgp = ctx.sems["dsy"], ctx.sems["dsc"], ctx.sems["dgp"]
    te, ve, ae, ge, cc = (ctx.sems[k] for k in ("te", "ve", "ae", "ge", "cc"))
    stm = [ctx.sems["st%d" % i] for i in range(4)]

    ln_stats(ctx, sb)

    a0, v0 = ctx.cnt["ae"], ctx.cnt["ve"]

    def frv(vector):
        v = v0
        for tt in range(NT):
            vector.tensor_copy(stat[:, tt, 6:7], stat[:, tt, 1:2]).then_inc(ve, 1); v += 1
            vector.tensor_tensor(out=stat[:, tt, 5:6], in0=stat[:, tt, 3:4],
                                 in1=stat[:, tt, 1:2], op=ALU.mult).then_inc(ve, 1); v += 1
            vector.wait_ge(ve, v)
            vector.tensor_tensor(out=stat[:, tt, 5:6], in0=stat[:, tt, 5:6],
                                 in1=stat[:, tt, 1:2], op=ALU.mult).then_inc(ve, 1); v += 1
            vector.wait_ge(ae, a0 + tt + 1)
            vector.reciprocal(out=stat[:, tt, 5:6], in_=stat[:, tt, 7:8]).then_inc(ve, 1); v += 1
            vector.wait_ge(ve, v)
            vector.tensor_tensor(out=stat[:, tt, 1:2], in0=stat[:, tt, 6:7],
                                 in1=stat[:, tt, 5:6], op=ALU.mult).then_inc(ve, 1); v += 1

    def fra(scalar):
        for tt in range(NT):
            scalar.wait_ge(ve, v0 + 5 * tt + 3)
            scalar.activation(out=stat[:, tt, 7:8], in_=stat[:, tt, 5:6],
                              func=AF.Sqrt, scale=-1.0, bias=1e-6).then_inc(ae, 1)
    ctx.cnt["ve"] += 5 * NT
    ctx.cnt["ae"] += NT
    ctx.emit(vector=frv, scalar=fra)
    ctx.barrier()

    y0 = ctx.cnt["dsy"]

    def fld(sync):
        for dst, src in ((smal[:, 224:232], dram["wln1_fm"][:, :]),
                         (msc[:, 32:64], dram["dtbias_b"][:, :]),
                         (msc[:, 0:32], dram["alog_b"][:, :]),
                         (msc[:, 384:416], dram["d_bcast"][:, 0:32]),
                         (msc[:, 416:488], dram["wconv_fm"][:, :]),
                         (msc[:, 488:506], dram["bconv_fm"][:, :])):
            sync.dma_start(out=dst, in_=src).then_inc(dsy, 16)
    ctx.cnt["dsy"] += 16 * 6
    a0, v0 = ctx.cnt["ae"], ctx.cnt["ve"]

    def fnA(scalar):
        scalar.activation(out=msc[:, 0:32], in_=msc[:, 0:32], func=AF.Exp).then_inc(ae, 1)

    def fnAv(vector):
        vector.memset(msc[:, 256:384], 1.0).then_inc(ve, 1)
        vector.wait_ge(ae, a0 + 1)
        vector.tensor_scalar(out=msc[:, 0:32], in0=msc[:, 0:32], scalar1=-1.0,
                             scalar2=None, op0=ALU.mult).then_inc(ve, 1)
    ctx.emit(sync=fld)
    ctx.barrier()
    ctx.cnt["ae"] += 1
    ctx.cnt["ve"] += 2
    ctx.emit(scalar=fnA, vector=fnAv)
    ctx.barrier()

    stream_xnorm_transpose(
        ctx, sb, psum,
        lambda fc, tt: big1[:, fc * 2048 + tt * 128: fc * 2048 + (tt + 1) * 128],
        list(range(NT)), scale_fn=lambda fc: smal[:, 224 + fc:225 + fc])

    def xsrc(fc, ob, tb):
        return big1[:, fc * 2048 + tb * 128: fc * 2048 + (tb + 1) * 128]

    def xfm_w(fc, tb):
        return big1[:, fc * 2048 + tb * 512: fc * 2048 + (tb + 1) * 512]

    st0 = [ctx.cnt["st%d" % i] for i in range(4)]

    def ev_z(scalar, bank, ob, tb, abase, st0=st0):
        k = ob * NT + tb
        slot = k % 4
        scalar.wait_ge(stm[slot], st0[slot] + 16 * (k // 4))
        stg = big0[:, 1024 + slot * 512:1024 + (slot + 1) * 512]
        scalar.copy(out=stg, in_=psum[bank][:, 0:512]).then_inc(ae, 1)
        scalar.wait_ge(ae, abase + 1)
        scalar.dma_start(out=dram["d_z"][tb * 128:(tb + 1) * 128, ob * 512:(ob + 1) * 512],
                         in_=stg).then_inc(stm[slot], 16)
    for i in range(4):
        ctx.cnt["st%d" % i] += 16 * 16
    proj_xs(ctx, sb, psum, dram["win_t"], xsrc, NT, 2048, ev_z, None, 1, 0)

    st0 = [ctx.cnt["st%d" % i] for i in range(4)]

    def ev_xbc(scalar, bank, oc, tb, abase, st0=st0):
        k = oc * 4 + tb
        slot = k % 4
        scalar.wait_ge(stm[slot], st0[slot] + 16 * (k // 4))
        stg = big0[:, 1024 + slot * 512:1024 + (slot + 1) * 512]
        scalar.copy(out=stg, in_=psum[bank][:, 0:512]).then_inc(ae, 1)
        scalar.wait_ge(ae, abase + 1)
        scalar.dma_start(out=dram["d_xbc"][oc, :, 3 + tb * 512:3 + (tb + 1) * 512],
                         in_=stg).then_inc(stm[slot], 16)
    for i in range(4):
        ctx.cnt["st%d" % i] += 16 * 18
    proj_ws(ctx, sb, psum, dram["win_t"], xfm_w, 4, 18, ev_xbc, None, 1, 0, w_col0=2048)

    y0 = ctx.cnt["dsy"]

    def fwdt(sync):
        sync.dma_start(out=big0[:, 4096:4352].rearrange("p (a o) -> p a o", a=8),
                       in_=dram["win_t"][0:1024, 4352:4384]
                       .rearrange("(a p) o -> p a o", p=128)).then_inc(dsy, 16)
    ctx.cnt["dsy"] += 16
    ctx.emit(sync=fwdt)
    ctx.barrier()

    t0, v0, a0 = ctx.cnt["te"], ctx.cnt["ve"], ctx.cnt["ae"]

    tmp1 = big0[:, 6400:6432]
    tmp2 = big0[:, 6432:6464]

    def fdt_t(tensor):
        tensor.drain()
        for tt in range(NT):
            bank = tt % 2
            if tt >= 2:
                tensor.wait_ge(ve, v0 + 3 * (tt - 2) + 1)
            for fc in range(8):
                tensor.matmul(psum[bank][:, 0:32],
                              lhsT=big1[:, fc * 2048 + tt * 128:fc * 2048 + (tt + 1) * 128],
                              rhs=big0[:, 4096 + fc * 32:4096 + (fc + 1) * 32],
                              start=(fc == 0), stop=(fc == 7)).then_inc(te, 1)

    def fdt_v(vector):
        v = v0
        for tt in range(NT):
            bank = tt % 2
            sl = slice(tt * 32, (tt + 1) * 32)
            vector.wait_ge(te, t0 + 8 * (tt + 1))
            vector.tensor_tensor(out=dtb[:, sl], in0=psum[bank][:, 0:32],
                                 in1=msc[:, 32:64], op=ALU.add).then_inc(ve, 1); v += 1
            vector.wait_ge(ve, v)
            vector.tensor_scalar(out=tmp2, in0=dtb[:, sl], scalar1=0.0,
                                 scalar2=None, op0=ALU.max).then_inc(ve, 1); v += 1
            vector.wait_ge(ve, v)
            vector.wait_ge(ae, a0 + 3 * (tt + 1))
            vector.tensor_tensor(out=dtb[:, sl], in0=tmp2, in1=tmp1,
                                 op=ALU.add).then_inc(ve, 1); v += 1

    def fdt_a(scalar):
        a = a0
        for tt in range(NT):
            sl = slice(tt * 32, (tt + 1) * 32)
            scalar.wait_ge(ve, v0 + 3 * tt + 1)
            scalar.activation(out=tmp1, in_=dtb[:, sl], func=AF.Abs).then_inc(ae, 1); a += 1
            scalar.wait_ge(ae, a)
            scalar.activation(out=tmp1, in_=tmp1, func=AF.Exp,
                              scale=-1.0).then_inc(ae, 1); a += 1
            scalar.wait_ge(ae, a)
            scalar.activation(out=tmp1, in_=tmp1, func=AF.Ln,
                              bias=1.0).then_inc(ae, 1); a += 1
    ctx.cnt["te"] += 8 * NT
    ctx.cnt["ve"] += 3 * NT
    ctx.cnt["ae"] += 3 * NT
    ctx.emit(tensor=fdt_t, vector=fdt_v, scalar=fdt_a)
    ctx.barrier()

    y0 = ctx.cnt["dsy"]

    def fbx(sync):
        for oc in range(18):
            sync.dma_start(out=dram["ag_cv_i"][oc * 128:(oc + 1) * 128, 0:4],
                           in_=dram["d_xbc"][oc, :, 2047:2051]).then_inc(dsy, 16)
        for oc in range(18):
            sync.dma_start(out=sidx[:, 32 + oc:33 + oc],
                           in_=dram["cvidx2"][oc * 128:(oc + 1) * 128, :]).then_inc(dsy, 16)
    ctx.cnt["dsy"] += 16 * 36
    ctx.emit(sync=fbx)
    ctx.barrier()

    p0, c0, v0 = ctx.cnt["dgp"], ctx.cnt["cc"], ctx.cnt["ve"]

    def fcv_ag(gpsimd):
        gpsimd.collective_compute(
            "AllGather", ALU.bypass, replica_groups=[list(range(NCORES))],
            ins=[dram["ag_cv_i"].opt()], outs=[dram["ag_cv_o"].opt()]).then_inc(cc, 1)
        gpsimd.wait_ge(cc, c0 + 1)
        n = 0
        for oc in range(18):
            gpsimd.indirect_dma_start(
                out=stage[:, 0, oc * 4:(oc + 1) * 4], out_offset=None,
                in_=dram["ag_cv_o"][:, :],
                in_offset=bass.IndirectOffsetOnAxis(ap=sidx[:, 32 + oc:33 + oc], axis=0),
            ).then_inc(dgp, 16)
            n += 1
            gpsimd.wait_ge(dgp, p0 + 16 * n)
    ctx.cnt["cc"] += 1
    ctx.cnt["dgp"] += 16 * 18
    ctx.emit(gpsimd=fcv_ag)
    ctx.barrier()

    def fcv_fl(vector):
        vector.tensor_scalar(out=stage[:, 0, 0:72], in0=stage[:, 0, 0:72],
                             scalar1=smal[:, 0:1], scalar2=None,
                             op0=ALU.mult).then_inc(ve, 1)
    ctx.cnt["ve"] += 1
    ctx.emit(vector=fcv_fl)
    ctx.barrier()

    y0 = ctx.cnt["dsy"]

    def fbw(sync):
        for oc in range(18):
            sync.dma_start(out=dram["d_xbc"][oc, :, 0:3],
                           in_=stage[:, 0, oc * 4 + 1:oc * 4 + 4]).then_inc(dsy, 16)
    ctx.cnt["dsy"] += 16 * 18
    ctx.emit(sync=fbw)
    ctx.barrier()

    v0, a0 = ctx.cnt["ve"], ctx.cnt["st2"]
    ab0 = ctx.cnt["ae"]
    sb0 = [ctx.cnt["st0"], ctx.cnt["st1"]]
    OUT = big0[:, 4224:6272]

    def fcv_in(sync):
        for oc in range(18):
            slot = oc % 2
            if oc >= 2:
                sync.wait_ge(ve, v0 + 5 * (oc - 2) + 4)
            sync.dma_start(out=big0[:, slot * 2112:slot * 2112 + 2051],
                           in_=dram["d_xbc"][oc, :, 0:2051]).then_inc(stm[slot], 16)

    def fcv_v(vector):
        v = v0
        for oc in range(18):
            slot = oc % 2
            IN = big0[:, slot * 2112:slot * 2112 + 2051]
            vector.wait_ge(stm[slot], sb0[slot] + 16 * (oc // 2 + 1))
            vector.wait_ge(st2, a0 + 16 * oc)
            vector.tensor_scalar(out=OUT[:, 0:2048], in0=IN[:, 0:2048],
                                 scalar1=msc[:, 416 + oc * 4:417 + oc * 4],
                                 scalar2=None, op0=ALU.mult).then_inc(ve, 1); v += 1
            for k in range(1, 4):
                vector.wait_ge(ve, v)
                vector.scalar_tensor_tensor(
                    out=OUT[:, 0:2048], in0=IN[:, k:2048 + k],
                    scalar=msc[:, 416 + oc * 4 + k:417 + oc * 4 + k],
                    in1=OUT[:, 0:2048], op0=ALU.mult, op1=ALU.add).then_inc(ve, 1); v += 1
            vector.wait_ge(ve, v)
            vector.tensor_scalar(out=OUT[:, 0:2048], in0=OUT[:, 0:2048],
                                 scalar1=msc[:, 488 + oc:489 + oc], scalar2=None,
                                 op0=ALU.add).then_inc(ve, 1); v += 1

    def fcv_a(scalar):
        for oc in range(18):
            scalar.wait_ge(ve, v0 + 5 * (oc + 1))
            scalar.activation(out=OUT[:, 0:2048], in_=OUT[:, 0:2048],
                              func=AF.Silu).then_inc(ae, 1)
            scalar.wait_ge(ae, ab0 + oc + 1)
            scalar.dma_start(out=dram["d_xbc"][oc, :, 3:2051],
                             in_=OUT[:, 0:2048]).then_inc(st2, 16)
    st2 = ctx.sems["st2"]
    ctx.cnt["st0"] += 16 * 9
    ctx.cnt["st1"] += 16 * 9
    ctx.cnt["ve"] += 5 * 18
    ctx.cnt["ae"] += 18
    ctx.cnt["st2"] += 16 * 18
    ctx.emit(sync=fcv_in, vector=fcv_v, scalar=fcv_a)
    ctx.barrier()

    v0 = ctx.cnt["ve"]

    def fz0(vector):
        vector.memset(big0[:, 0:1024], 0.0).then_inc(ve, 1)
        vector.memset(big0[:, 1024:2048], 0.0).then_inc(ve, 1)
        vector.memset(msc[:, 160:192], 0.0).then_inc(ve, 1)
    ctx.cnt["ve"] += 3
    ctx.emit(vector=fz0)
    ctx.barrier()

    LB, PB, EB, CE, GT = 4096, 4224, 4352, 4480, 4608
    CB = [4736, 4864]
    YB = 5120

    for c in range(NT):
        y0 = ctx.cnt["dsy"]

        def fd(sync, c=c):
            for oc in range(16):
                sync.dma_start(out=big1[:, oc * 128:(oc + 1) * 128],
                               in_=dram["d_xbc"][oc, :, 3 + c * 128:3 + (c + 1) * 128]
                               ).then_inc(dsy, 16)
            sync.dma_start(out=big1[:, 2048:2176],
                           in_=dram["d_xbc"][16, :, 3 + c * 128:3 + (c + 1) * 128]).then_inc(dsy, 16)
            sync.dma_start(out=big1[:, 2176:2304],
                           in_=dram["d_xbc"][17, :, 3 + c * 128:3 + (c + 1) * 128]).then_inc(dsy, 16)
        ctx.cnt["dsy"] += 16 * 18
        ctx.emit(sync=fd)
        ctx.barrier()

        t0, v0, a0 = ctx.cnt["te"], ctx.cnt["ve"], ctx.cnt["ae"]

        def fpv(vector, c=c, t0=t0, v0=v0):
            v = v0
            vector.tensor_tensor(out=msc[:, 224:256], in0=dtb[:, c * 32:(c + 1) * 32],
                                 in1=msc[:, 0:32], op=ALU.mult).then_inc(ve, 1); v += 1
            vector.wait_ge(te, t0 + 1)
            vector.tensor_copy(msc[:, 64:96], psum[0][:, 0:32]).then_inc(ve, 1); v += 1
            vector.wait_ge(te, t0 + 2)
            vector.tensor_copy(big0[:, GT:GT + 128], psum[1][:, 0:128]).then_inc(ve, 1); v += 1
            for k in range(17):
                vector.wait_ge(te, t0 + 2 + k + 1)
                bank = 4 + k % 2
                if k == 0:
                    vector.tensor_copy(big1[:, 2304:2432], psum[bank][:, 0:128]).then_inc(ve, 1); v += 1
                else:
                    fc = k - 1
                    vector.tensor_copy(big1[:, 4096 + fc * 128:4096 + (fc + 1) * 128],
                                       psum[bank][:, 0:128]).then_inc(ve, 1); v += 1
                    for hh in range(2):
                        h = 2 * fc + hh
                        vector.tensor_scalar(
                            out=big1[:, 6144 + h * 64:6144 + (h + 1) * 64],
                            in0=psum[bank][:, hh * 64:(hh + 1) * 64],
                            scalar1=dtb[:, c * 32 + h:c * 32 + h + 1], scalar2=None,
                            op0=ALU.mult).then_inc(ve, 1); v += 1

        def fpt(tensor, c=c, t0=t0, v0=v0):
            tensor.drain()
            tensor.wait_ge(ve, v0 + 1)
            tensor.matmul(psum[0][:, 0:32], lhsT=cons[:, 512:640], rhs=msc[:, 224:256],
                          start=True, stop=True).then_inc(te, 1)
            tensor.matmul(psum[1][:, 0:128], lhsT=big1[:, 2048:2176], rhs=big1[:, 2176:2304],
                          start=True, stop=True).then_inc(te, 1)
            tensor.drain()
            for k in range(17):
                if k >= 2:
                    if k - 2 == 0:
                        tensor.wait_ge(ve, v0 + 4)
                    else:
                        tensor.wait_ge(ve, v0 + 4 + 3 * (k - 2))
                src = big1[:, 2048:2176] if k == 0 else big1[:, (k - 1) * 128:k * 128]
                tensor.transpose(out=psum[4 + k % 2][:, 0:128], in_=src,
                                 identity=cons[:, 0:128]).then_inc(te, 1)
        ctx.cnt["te"] += 2 + 17
        ctx.cnt["ve"] += 3 + 1 + 16 * 3
        ctx.emit(tensor=fpt, vector=fpv)
        ctx.barrier()

        t0, v0 = ctx.cnt["te"], ctx.cnt["ve"]

        def fc1v(vector, t0=t0, v0=v0):
            for h in range(32):
                slot = CB[h % 2]
                if h >= 2:
                    vector.wait_ge(te, t0 + h - 1)
                vector.tensor_scalar(out=big0[:, slot:slot + 128], in0=msc[:, 256:384],
                                     scalar1=msc[:, 64 + h:65 + h], scalar2=None,
                                     op0=ALU.mult).then_inc(ve, 1)
                if h >= 2:
                    j = h - 2
                    vector.wait_ge(te, t0 + j + 1)
                    vector.tensor_copy(big1[:, 10240 + j * 128:10240 + (j + 1) * 128],
                                       psum[4 + j % 2][:, 0:128]).then_inc(ve, 1)
            for j in (30, 31):
                vector.wait_ge(te, t0 + j + 1)
                vector.tensor_copy(big1[:, 10240 + j * 128:10240 + (j + 1) * 128],
                                   psum[4 + j % 2][:, 0:128]).then_inc(ve, 1)

        def fc1t(tensor, t0=t0, v0=v0):
            tensor.drain()
            for h in range(32):
                tensor.wait_ge(ve, v0 + (h + 1 if h < 2 else 2 * h))
                tensor.transpose(out=psum[4 + h % 2][:, 0:128],
                                 in_=big0[:, CB[h % 2]:CB[h % 2] + 128],
                                 identity=cons[:, 0:128]).then_inc(te, 1)
        ctx.cnt["te"] += 32
        ctx.cnt["ve"] += 64
        ctx.emit(tensor=fc1t, vector=fc1v)
        ctx.barrier()

        t0, v0, a0 = ctx.cnt["te"], ctx.cnt["ve"], ctx.cnt["ae"]

        def fhv(vector, c=c, t0=t0, v0=v0, a0=a0):
            v = v0
            for h in range(32):
                R = big1[:, 10240 + h * 128:10240 + (h + 1) * 128]
                if h >= 1:
                    vector.wait_ge(te, t0 + 2 * h - 1)
                vector.scalar_tensor_tensor(out=big0[:, LB:LB + 128], in0=R,
                                            scalar=msc[:, 64 + h:65 + h],
                                            in1=cons[:, 640:768],
                                            op0=ALU.subtract, op1=ALU.add).then_inc(ve, 1); v += 1
                vector.wait_ge(ae, a0 + 2 * h + 1)
                vector.tensor_copy(msc[:, 96 + h:97 + h],
                                   big0[:, LB + 127:LB + 128]).then_inc(ve, 1); v += 1
                vector.tensor_tensor(out=big0[:, PB:PB + 128], in0=big0[:, LB:LB + 128],
                                     in1=big0[:, GT:GT + 128], op=ALU.mult).then_inc(ve, 1); v += 1
                vector.wait_ge(ae, a0 + 2 * h + 2)
                vector.tensor_copy(msc[:, 128 + h:129 + h],
                                   big0[:, EB + 127:EB + 128]).then_inc(ve, 1); v += 1
                if h >= 1:
                    vector.wait_ge(te, t0 + 2 * h)
                vector.tensor_tensor(out=big0[:, CE:CE + 128], in0=big1[:, 2176:2304],
                                     in1=big0[:, EB:EB + 128], op=ALU.mult).then_inc(ve, 1); v += 1

        def fha(scalar, t0=t0, v0=v0, a0=a0):
            for h in range(32):
                scalar.wait_ge(ve, v0 + 5 * h + 1)
                scalar.activation(out=big0[:, LB:LB + 128], in_=big0[:, LB:LB + 128],
                                  func=AF.Exp).then_inc(ae, 1)
                if h >= 1:
                    scalar.wait_ge(ve, v0 + 5 * h)
                scalar.activation(out=big0[:, EB:EB + 128],
                                  in_=big1[:, 10240 + h * 128:10240 + (h + 1) * 128],
                                  func=AF.Exp).then_inc(ae, 1)

        def fht(tensor, t0=t0, v0=v0, a0=a0):
            tensor.drain()
            for h in range(32):
                bank = h // 8
                vector_pt_done = v0 + 5 * h + 3
                tensor.wait_ge(ve, vector_pt_done)
                tensor.matmul(psum[bank][:, (h % 8) * 64:(h % 8) * 64 + 64],
                              lhsT=big0[:, PB:PB + 128],
                              rhs=big1[:, 6144 + h * 64:6144 + (h + 1) * 64],
                              start=True, stop=False).then_inc(te, 1)
                tensor.wait_ge(ve, v0 + 5 * h + 5)
                tensor.matmul(psum[bank][:, (h % 8) * 64:(h % 8) * 64 + 64],
                              lhsT=big0[:, CE:CE + 128],
                              rhs=big0[:, h * 64:(h + 1) * 64],
                              start=False, stop=True).then_inc(te, 1)
        ctx.cnt["ve"] += 5 * 32
        ctx.cnt["ae"] += 2 * 32
        ctx.cnt["te"] += 2 * 32
        ctx.emit(vector=fhv, scalar=fha, tensor=fht)
        ctx.barrier()

        t0, v0 = ctx.cnt["te"], ctx.cnt["ve"]

        def fdv(vector, c=c, t0=t0, v0=v0):
            v = v0
            for h in range(32):
                vector.tensor_scalar(out=big1[:, 8192 + h * 64:8192 + (h + 1) * 64],
                                     in0=big1[:, 6144 + h * 64:6144 + (h + 1) * 64],
                                     scalar1=msc[:, 96 + h:97 + h], scalar2=None,
                                     op0=ALU.mult).then_inc(ve, 1); v += 1
            for h in range(32):
                vector.scalar_tensor_tensor(
                    out=big0[:, YB + h * 64:YB + (h + 1) * 64],
                    in0=big1[:, 4096 + h * 64:4096 + (h + 1) * 64],
                    scalar=msc[:, 384 + h:385 + h],
                    in1=psum[h // 8][:, (h % 8) * 64:(h % 8) * 64 + 64],
                    op0=ALU.mult, op1=ALU.add).then_inc(ve, 1); v += 1
            for h in range(32):
                vector.wait_ge(te, t0 + 4 * ((h * 64) // 512 + 1))
                sl = slice(h * 64, (h + 1) * 64)
                vector.scalar_tensor_tensor(
                    out=big0[:, sl], in0=big0[:, sl],
                    scalar=msc[:, 128 + h:129 + h],
                    in1=psum[4 + h // 8][:, (h % 8) * 64:(h % 8) * 64 + 64],
                    op0=ALU.mult, op1=ALU.add).then_inc(ve, 1); v += 1

        def fdt2(tensor, t0=t0, v0=v0):
            for ob in range(4):
                tensor.wait_ge(ve, v0 + 8 * (ob + 1))
                for r in range(4):
                    tensor.matmul(psum[4 + ob][:, r * 128:(r + 1) * 128],
                                  lhsT=big1[:, 2304:2432],
                                  rhs=big1[:, 8192 + ob * 512 + r * 128:8192 + ob * 512 + (r + 1) * 128],
                                  start=True, stop=True).then_inc(te, 1)

        def fdd(scalar, c=c, v0=v0):
            scalar.wait_ge(ve, v0 + 64)
            scalar.dma_start(out=dram["d_ypart"][c * 128:(c + 1) * 128, :],
                             in_=big0[:, YB:YB + 2048]).then_inc(dsc, 16)
        ctx.cnt["ve"] += 96
        ctx.cnt["te"] += 16
        ctx.cnt["dsc"] += 16
        ctx.emit(vector=fdv, tensor=fdt2, scalar=fdd)
        ctx.barrier()

    y0 = ctx.cnt["dsy"]

    def fse(sync):
        sync.dma_start(out=dram["ag_mb_i"][:, :], in_=big0[:, 0:2048]).then_inc(dsy, 16)
        sync.dma_start(out=sidx[:, 50:51], in_=dram["mbidx"][:, :]).then_inc(dsy, 16)
    ctx.cnt["dsy"] += 32
    ctx.emit(sync=fse)
    ctx.barrier()

    p0, c0 = ctx.cnt["dgp"], ctx.cnt["cc"]

    def fmb_ag(gpsimd):
        gpsimd.collective_compute(
            "AllGather", ALU.bypass, replica_groups=[list(range(NCORES))],
            ins=[dram["ag_mb_i"].opt()], outs=[dram["ag_mb_o"].opt()]).then_inc(cc, 1)
        gpsimd.wait_ge(cc, c0 + 1)
        gpsimd.indirect_dma_start(
            out=big0[:, 0:2048], out_offset=None, in_=dram["ag_mb_o"][:, :],
            in_offset=bass.IndirectOffsetOnAxis(ap=sidx[:, 50:51], axis=0),
        ).then_inc(dgp, 16)
    ctx.cnt["cc"] += 1
    ctx.cnt["dgp"] += 16
    ctx.emit(gpsimd=fmb_ag)
    ctx.barrier()

    y0 = ctx.cnt["dsy"]

    def fwm(sync):
        sync.dma_start(out=smal[:, 224:240], in_=dram["wmnorm_fm"][:, :]).then_inc(dsy, 16)
    ctx.cnt["dsy"] += 16
    ctx.emit(sync=fwm)
    ctx.barrier()

    for c in range(NT):
        y0 = ctx.cnt["dsy"]

        def fgd(sync, c=c):
            sync.dma_start(out=big0[:, YB:YB + 2048],
                           in_=dram["d_ypart"][c * 128:(c + 1) * 128, :]).then_inc(dsy, 16)
            sync.dma_start(out=big1[:, 0:2048],
                           in_=dram["d_z"][c * 128:(c + 1) * 128, :]).then_inc(dsy, 16)
            sync.dma_start(out=big1[:, 2048:2176],
                           in_=dram["d_xbc"][17, :, 3 + c * 128:3 + (c + 1) * 128]).then_inc(dsy, 16)
        ctx.cnt["dsy"] += 48
        ctx.emit(sync=fgd)
        ctx.barrier()

        t0, v0, a0 = ctx.cnt["te"], ctx.cnt["ve"], ctx.cnt["ae"]

        def fgv(vector, c=c, t0=t0, v0=v0, a0=a0):
            v = v0
            vector.tensor_tensor(out=msc[:, 224:256], in0=dtb[:, c * 32:(c + 1) * 32],
                                 in1=msc[:, 0:32], op=ALU.mult).then_inc(ve, 1); v += 1
            vector.wait_ge(te, t0 + 1)
            vector.tensor_copy(msc[:, 64:96], psum[0][:, 0:32]).then_inc(ve, 1); v += 1
            vector.wait_ge(ve, v)
            vector.tensor_tensor(out=msc[:, 192:224], in0=msc[:, 160:192],
                                 in1=msc[:, 64:96], op=ALU.add).then_inc(ve, 1); v += 1
            vector.wait_ge(ae, a0 + 1)
            vector.tensor_scalar(out=msc[:, 192:224], in0=msc[:, 192:224],
                                 scalar1=smal[:, 0:1], scalar2=None,
                                 op0=ALU.mult).then_inc(ve, 1); v += 1
            vector.wait_ge(te, t0 + 2)
            vector.tensor_tensor(out=msc[:, 160:192], in0=msc[:, 160:192],
                                 in1=psum[1][:, 0:32], op=ALU.add).then_inc(ve, 1); v += 1
            for h in range(32):
                vector.wait_ge(te, t0 + 2 + ((h * 64) // 512 + 1))
                vector.scalar_tensor_tensor(
                    out=big0[:, YB + h * 64:YB + (h + 1) * 64],
                    in0=psum[4 + h // 8][:, (h % 8) * 64:(h % 8) * 64 + 64],
                    scalar=msc[:, 192 + h:193 + h],
                    in1=big0[:, YB + h * 64:YB + (h + 1) * 64],
                    op0=ALU.mult, op1=ALU.add).then_inc(ve, 1); v += 1
            vector.wait_ge(ae, a0 + 2)
            for ob in range(4):
                vector.wait_ge(ve, v)
                vector.tensor_tensor(out=big0[:, YB + ob * 512:YB + (ob + 1) * 512],
                                     in0=big0[:, YB + ob * 512:YB + (ob + 1) * 512],
                                     in1=big1[:, ob * 512:(ob + 1) * 512],
                                     op=ALU.mult).then_inc(ve, 1); v += 1
            vector.wait_ge(ae, a0 + 3)
            vector.tensor_reduce(out=msc[:, 506:507], in_=big1[:, 4096:6144],
                                 axis=AX.X, op=ALU.add).then_inc(ve, 1); v += 1
            vector.wait_ge(ve, v)
            vector.tensor_scalar(out=msc[:, 507:508], in0=msc[:, 506:507],
                                 scalar1=1.0 / 2048.0, scalar2=None,
                                 op0=ALU.mult).then_inc(ve, 1); v += 1
            vector.wait_ge(ae, a0 + 4)
            vector.reciprocal(out=msc[:, 509:510], in_=msc[:, 508:509]).then_inc(ve, 1); v += 1
            vector.wait_ge(ve, v)
            for ob in range(4):
                vector.tensor_scalar(out=big0[:, YB + ob * 512:YB + (ob + 1) * 512],
                                     in0=big0[:, YB + ob * 512:YB + (ob + 1) * 512],
                                     scalar1=msc[:, 509:510], scalar2=None,
                                     op0=ALU.mult).then_inc(ve, 1); v += 1
            for fc in range(16):
                vector.wait_ge(te, t0 + 6 + fc + 1)
                vector.tensor_scalar(out=big1[:, 6144 + fc * 128:6144 + (fc + 1) * 128],
                                     in0=psum[4 + fc % 2][:, 0:128],
                                     scalar1=smal[:, 224 + fc:225 + fc], scalar2=None,
                                     op0=ALU.mult).then_inc(ve, 1); v += 1

        def fgt(tensor, c=c, t0=t0, v0=v0):
            tensor.drain()
            tensor.wait_ge(ve, v0 + 1)
            tensor.matmul(psum[0][:, 0:32], lhsT=cons[:, 512:640], rhs=msc[:, 224:256],
                          start=True, stop=True).then_inc(te, 1)
            tensor.matmul(psum[1][:, 0:32], lhsT=msc[:, 256:384], rhs=msc[:, 224:256],
                          start=True, stop=True).then_inc(te, 1)
            tensor.wait_ge(ve, v0 + 4)
            for ob in range(4):
                tensor.matmul(psum[4 + ob][:, 0:512], lhsT=big1[:, 2048:2176],
                              rhs=big0[:, ob * 512:(ob + 1) * 512],
                              start=True, stop=True).then_inc(te, 1)
            tensor.drain()
            tensor.wait_ge(ve, v0 + 48)
            for fc in range(16):
                if fc >= 2:
                    tensor.wait_ge(ve, v0 + 48 + fc - 1)
                tensor.transpose(out=psum[4 + fc % 2][:, 0:128],
                                 in_=big0[:, YB + fc * 128:YB + (fc + 1) * 128],
                                 identity=cons[:, 0:128]).then_inc(te, 1)

        def fga(scalar, c=c, t0=t0, v0=v0, a0=a0):
            scalar.wait_ge(ve, v0 + 3)
            scalar.activation(out=msc[:, 192:224], in_=msc[:, 192:224],
                              func=AF.Exp).then_inc(ae, 1)
            scalar.wait_ge(ve, v0 + 37)
            scalar.activation(out=big1[:, 0:2048], in_=big1[:, 0:2048],
                              func=AF.Silu).then_inc(ae, 1)
            scalar.wait_ge(ve, v0 + 41)
            scalar.activation(out=big1[:, 4096:6144], in_=big0[:, YB:YB + 2048],
                              func=AF.Square).then_inc(ae, 1)
            scalar.wait_ge(ve, v0 + 43)
            scalar.activation(out=msc[:, 508:509], in_=msc[:, 507:508],
                              func=AF.Sqrt, scale=1.0, bias=1e-5).then_inc(ae, 1)
            for fc in range(16):
                scalar.wait_ge(ve, v0 + 48 + fc + 1)
                scalar.dma_start(out=dram["d_act"][fc * 128:(fc + 1) * 128, c * 128:(c + 1) * 128],
                                 in_=big1[:, 6144 + fc * 128:6144 + (fc + 1) * 128]
                                 ).then_inc(st2, 16)
        ctx.cnt["ve"] += 5 + 32 + 4 + 3 + 4 + 16
        ctx.cnt["te"] += 6 + 16
        ctx.cnt["ae"] += 4
        ctx.cnt["st2"] += 16 * 16
        ctx.emit(vector=fgv, tensor=fgt, scalar=fga)
        ctx.barrier()

    for half in range(2):
        y0 = ctx.cnt["dsy"]

        def fgl(sync, half=half):
            for fc in range(16):
                sync.dma_start(out=big1[:, fc * 1024:(fc + 1) * 1024],
                               in_=dram["d_act"][fc * 128:(fc + 1) * 128,
                                                 half * 1024:(half + 1) * 1024]).then_inc(dsy, 16)
        ctx.cnt["dsy"] += 16 * 16
        ctx.emit(sync=fgl)
        ctx.barrier()

        def srcA(fc, ob, tb):
            return big1[:, fc * 1024 + tb * 128: fc * 1024 + (tb + 1) * 128]

        def srcB(fc, ob, tb):
            return big1[:, (8 + fc) * 1024 + tb * 128: (8 + fc) * 1024 + (tb + 1) * 128]

        def ev_wA(vector, bank, ob, tb, vbase):
            vector.tensor_copy(big0[:, tb * 1024 + ob * 512:tb * 1024 + (ob + 1) * 512],
                               psum[bank][:, 0:512]).then_inc(ve, 1)
        proj_xs(ctx, sb, psum, dram["wout_t"], srcA, 8, 1024, None, ev_wA, 0, 1)

        st0 = [ctx.cnt["st%d" % i] for i in range(4)]

        def ev_wBv(vector, bank, ob, tb, vbase):
            sl = slice(tb * 1024 + ob * 512, tb * 1024 + (ob + 1) * 512)
            vector.tensor_tensor(out=big0[:, sl], in0=big0[:, sl],
                                 in1=psum[bank][:, 0:512], op=ALU.add).then_inc(ve, 1)

        def ev_wBs(scalar, bank, ob, tb, abase, half=half, st0=st0):
            k = ob * 8 + tb
            slot = k % 4
            scalar.wait_ge(stm[slot], st0[slot] + 16 * (k // 4))
            stg = stage[:, slot // 2, (slot % 2) * 512:(slot % 2) * 512 + 512]
            scalar.copy(out=stg, in_=big0[:, tb * 1024 + ob * 512:tb * 1024 + (ob + 1) * 512]
                        ).then_inc(ae, 1)
            scalar.wait_ge(ae, abase + 1)
            scalar.dma_start(
                out=dram["d_aout"][(half * 8 + tb) * 128:(half * 8 + tb + 1) * 128,
                                   ob * 512:(ob + 1) * 512],
                in_=stg).then_inc(stm[slot], 16)
        for i in range(4):
            ctx.cnt["st%d" % i] += 16 * 4
        proj_xs(ctx, sb, psum, dram["wout_t"], srcB, 8, 1024, ev_wBs, ev_wBv, 1, 1,
                w_row0=1024, order="vs")

    y0 = ctx.cnt["dsy"]

    def fl2(sync):
        sync.dma_start(out=smal[:, 224:232], in_=dram["wln2_fm"][:, :]).then_inc(dsy, 16)
    ctx.cnt["dsy"] += 16
    ctx.emit(sync=fl2)
    ctx.barrier()

    for tt in range(NT):
        y0 = ctx.cnt["dsy"]

        def fh_in(sync, tt=tt):
            sync.dma_start(out=big0[:, 0:1024],
                           in_=dram["d_aout"][tt * 128:(tt + 1) * 128, :]).then_inc(dsy, 16)
        ctx.cnt["dsy"] += 16
        ctx.emit(sync=fh_in)
        ctx.barrier()

        t0, v0, a0 = ctx.cnt["te"], ctx.cnt["ve"], ctx.cnt["ae"]

        def fhv2(vector, tt=tt, t0=t0, v0=v0, a0=a0):
            v = v0
            vector.tensor_scalar(out=big0[:, 1024:2048], in0=xres[:, tt, :],
                                 scalar1=stat[:, tt, 0:1], scalar2=stat[:, tt, 6:7],
                                 op0=ALU.subtract, op1=ALU.mult).then_inc(ve, 1); v += 1
            vector.wait_ge(ve, v)
            vector.tensor_tensor(out=big0[:, 2048:3072], in0=big0[:, 0:1024],
                                 in1=big0[:, 1024:2048], op=ALU.add).then_inc(ve, 1); v += 1
            vector.wait_ge(ae, a0 + 1)
            vector.tensor_reduce(out=msc[:, 506:507], in_=big0[:, 3072:4096],
                                 axis=AX.X, op=ALU.add).then_inc(ve, 1); v += 1
            vector.wait_ge(ve, v)
            vector.tensor_scalar(out=msc[:, 507:508], in0=msc[:, 506:507],
                                 scalar1=1.0 / 1024.0, scalar2=None,
                                 op0=ALU.mult).then_inc(ve, 1); v += 1
            vector.wait_ge(ae, a0 + 2)
            vector.reciprocal(out=msc[:, 509:510], in_=msc[:, 508:509]).then_inc(ve, 1); v += 1
            vector.wait_ge(ve, v)
            vector.tensor_scalar(out=big0[:, 3072:4096], in0=big0[:, 2048:3072],
                                 scalar1=msc[:, 509:510], scalar2=None,
                                 op0=ALU.mult).then_inc(ve, 1); v += 1
            for fc in range(8):
                vector.wait_ge(te, t0 + fc + 1)
                vector.tensor_scalar(out=big1[:, fc * 2048 + tt * 128:fc * 2048 + (tt + 1) * 128],
                                     in0=psum[4 + fc % 2][:, 0:128],
                                     scalar1=smal[:, 224 + fc:225 + fc], scalar2=None,
                                     op0=ALU.mult).then_inc(ve, 1); v += 1

        def fha2(scalar, tt=tt, v0=v0, a0=a0):
            scalar.wait_ge(ve, v0 + 2)
            scalar.activation(out=big0[:, 3072:4096], in_=big0[:, 2048:3072],
                              func=AF.Square).then_inc(ae, 1)
            scalar.wait_ge(ve, v0 + 4)
            scalar.activation(out=msc[:, 508:509], in_=msc[:, 507:508],
                              func=AF.Sqrt, scale=1.0, bias=1e-6).then_inc(ae, 1)
            scalar.wait_ge(ve, v0 + 2)
            scalar.dma_start(out=dram["d_aout"][tt * 128:(tt + 1) * 128, :],
                             in_=big0[:, 2048:3072]).then_inc(st2, 16)

        def fht2(tensor, tt=tt, v0=v0):
            tensor.drain()
            tensor.wait_ge(ve, v0 + 6)
            for fc in range(8):
                if fc >= 2:
                    tensor.wait_ge(ve, v0 + 6 + fc - 1)
                tensor.transpose(out=psum[4 + fc % 2][:, 0:128],
                                 in_=big0[:, 3072 + fc * 128:3072 + (fc + 1) * 128],
                                 identity=cons[:, 0:128]).then_inc(te, 1)
        ctx.cnt["ve"] += 6 + 8
        ctx.cnt["ae"] += 2
        ctx.cnt["st2"] += 16
        ctx.cnt["te"] += 8
        ctx.emit(vector=fhv2, scalar=fha2, tensor=fht2)
        ctx.barrier()

    wl = [ctx.sems["wl0"], ctx.sems["wl1"]]
    w0 = [ctx.cnt["wl0"], ctx.cnt["wl1"]]
    t0, v0, a0 = ctx.cnt["te"], ctx.cnt["ve"], ctx.cnt["ae"]
    oslots = [ctx.sems["st3"], ctx.sems["st2"]]
    ob0 = [ctx.cnt["st3"], ctx.cnt["st2"]]

    def fgu_sy(sync):
        sync.wait_ge(te, t0)
        for oc2 in range(32):
            if oc2 >= 1:
                sync.wait_ge(te, t0 + 64 * oc2)
            for r in range(2):
                sync.dma_start(
                    out=wbuf_ap(sb)[:, r, 0:1024].rearrange("p (a o) -> p a o", o=128),
                    in_=dram["wgu_t"][0:1024, (2 * oc2 + r) * 128:(2 * oc2 + r + 1) * 128]
                    .rearrange("(a p) o -> p a o", p=128)).then_inc(wl[r], 16)

    def fgu_t(tensor):
        tensor.drain()
        for oc2 in range(32):
            tensor.wait_ge(wl[0], w0[0] + 16 * (oc2 + 1))
            tensor.wait_ge(wl[1], w0[1] + 16 * (oc2 + 1))
            for tb in range(4):
                k = oc2 * 4 + tb
                if k >= 2:
                    tensor.wait_ge(ve, v0 + (k - 1))
                for fc in range(8):
                    tensor.matmul(psum[tb % 2][:, 0:512],
                                  lhsT=wbuf_ap(sb)[:, 0, fc * 128:(fc + 1) * 128],
                                  rhs=big1[:, fc * 2048 + tb * 512:fc * 2048 + (tb + 1) * 512],
                                  start=(fc == 0), stop=(fc == 7)).then_inc(te, 1)
                for fc in range(8):
                    tensor.matmul(psum[2 + tb % 2][:, 0:512],
                                  lhsT=wbuf_ap(sb)[:, 1, fc * 128:(fc + 1) * 128],
                                  rhs=big1[:, fc * 2048 + tb * 512:fc * 2048 + (tb + 1) * 512],
                                  start=(fc == 0), stop=(fc == 7)).then_inc(te, 1)

    def fgu_a(scalar):
        for k in range(128):
            oc2, tb = divmod(k, 4)
            scalar.wait_ge(te, t0 + 16 * (k + 1))
            if k >= 2:
                scalar.wait_ge(ve, v0 + k - 1)
            slot = k % 2
            scalar.activation(out=big0[:, slot * 512:(slot + 1) * 512],
                              in_=psum[2 + tb % 2][:, 0:512], func=AF.Silu).then_inc(ae, 1)
            if k >= 1:
                j = k - 1
                oj, tj = divmod(j, 4)
                scalar.wait_ge(ve, v0 + j + 1)
                scalar.dma_start(out=dram["d_act"][oj * 128:(oj + 1) * 128,
                                                   tj * 512:(tj + 1) * 512],
                                 in_=big0[:, 1024 + (j % 2) * 512:1024 + (j % 2 + 1) * 512]
                                 ).then_inc(oslots[j % 2], 16)
        j = 127
        oj, tj = divmod(j, 4)
        scalar.wait_ge(ve, v0 + 128)
        scalar.dma_start(out=dram["d_act"][oj * 128:(oj + 1) * 128,
                                           tj * 512:(tj + 1) * 512],
                         in_=big0[:, 1024 + (j % 2) * 512:1024 + (j % 2 + 1) * 512]
                         ).then_inc(oslots[j % 2], 16)

    def fgu_v(vector):
        v = v0
        for k in range(128):
            oc2, tb = divmod(k, 4)
            slot = k % 2
            vector.wait_ge(ae, a0 + k + 1)
            if k >= 2:
                vector.wait_ge(oslots[k % 2], ob0[k % 2] + 16 * (k // 2))
            vector.tensor_tensor(out=big0[:, 1024 + slot * 512:1024 + (slot + 1) * 512],
                                 in0=big0[:, slot * 512:(slot + 1) * 512],
                                 in1=psum[tb % 2][:, 0:512], op=ALU.mult).then_inc(ve, 1); v += 1

    ctx.cnt["wl0"] += 16 * 32
    ctx.cnt["wl1"] += 16 * 32
    ctx.cnt["te"] += 16 * 128
    ctx.cnt["ae"] += 128
    ctx.cnt["ve"] += 128
    ctx.cnt["st3"] += 16 * 64
    ctx.cnt["st2"] += 16 * 64
    ctx.emit(sync=fgu_sy, tensor=fgu_t, scalar=fgu_a, vector=fgu_v)
    ctx.barrier()

    for g in range(4):
        y0 = ctx.cnt["dsy"]

        def fdl(sync, g=g):
            for oc in range(32):
                sync.dma_start(out=big1[:, oc * 512:(oc + 1) * 512],
                               in_=dram["d_act"][oc * 128:(oc + 1) * 128,
                                                 g * 512:(g + 1) * 512]).then_inc(dsy, 16)
            for tb in range(4):
                sync.dma_start(out=big0[:, 4096 + tb * 1024:4096 + (tb + 1) * 1024],
                               in_=dram["d_aout"][(g * 4 + tb) * 128:(g * 4 + tb + 1) * 128, :]
                               ).then_inc(dsy, 16)
        ctx.cnt["dsy"] += 16 * 36
        ctx.emit(sync=fdl)
        ctx.barrier()

        for q in range(4):
            def srcq(fc, ob, tb, q=q):
                return big1[:, (q * 8 + fc) * 512 + tb * 128:(q * 8 + fc) * 512 + (tb + 1) * 128]

            if q == 0:
                def ev_d0(vector, bank, ob, tb, vbase):
                    vector.tensor_copy(big0[:, tb * 1024 + ob * 512:tb * 1024 + (ob + 1) * 512],
                                       psum[bank][:, 0:512]).then_inc(ve, 1)
                proj_xs(ctx, sb, psum, dram["wd_t"], srcq, 4, 1024, None, ev_d0, 0, 1)
            elif q < 3:
                def ev_dq(vector, bank, ob, tb, vbase):
                    sl = slice(tb * 1024 + ob * 512, tb * 1024 + (ob + 1) * 512)
                    vector.tensor_tensor(out=big0[:, sl], in0=big0[:, sl],
                                         in1=psum[bank][:, 0:512], op=ALU.add).then_inc(ve, 1)
                proj_xs(ctx, sb, psum, dram["wd_t"], srcq, 4, 1024, None, ev_dq, 0, 1,
                        w_row0=q * 1024)
            else:
                def ev_d3(vector, bank, ob, tb, vbase, g=g):
                    sl = slice(tb * 1024 + ob * 512, tb * 1024 + (ob + 1) * 512)
                    vector.tensor_tensor(out=big0[:, sl], in0=big0[:, sl],
                                         in1=psum[bank][:, 0:512], op=ALU.add).then_inc(ve, 1)
                    vector.wait_ge(ve, vbase + 1)
                    vector.tensor_tensor(out=big0[:, sl], in0=big0[:, sl],
                                         in1=big0[:, 4096 + tb * 1024 + ob * 512:
                                                  4096 + tb * 1024 + (ob + 1) * 512],
                                         op=ALU.add).then_inc(ve, 1)
                    vector.wait_ge(ve, vbase + 2)
                    vector.tensor_tensor(out=xres[:, g * 4 + tb, ob * 512:(ob + 1) * 512],
                                         in0=xres[:, g * 4 + tb, ob * 512:(ob + 1) * 512],
                                         in1=big0[:, sl], op=ALU.add).then_inc(ve, 1)
                proj_xs(ctx, sb, psum, dram["wd_t"], srcq, 4, 1024, None, ev_d3, 0, 3,
                        w_row0=3 * 1024)


def wbuf_ap(sb):
    return sb["wbuf"]



def emit_retention(ctx, dram, sb, psum):
    xres, big0, big1, stat, smal, cons, stage = (
        sb["xres"], sb["big0"], sb["big1"], sb["stat"], sb["smal"], sb["cons"], sb["stage"])
    dsy, dsc, dgp = ctx.sems["dsy"], ctx.sems["dsc"], ctx.sems["dgp"]
    te, ve, ae, ge, cc = (ctx.sems[k] for k in ("te", "ve", "ae", "ge", "cc"))
    stm = [ctx.sems["st%d" % i] for i in range(4)]
    GQ = float(GAMMA) ** 128

    ln_stats(ctx, sb)
    stream_xnorm_transpose(ctx, sb, psum,
                           lambda fc, tt: big1[:, fc * 2048 + tt * 128: fc * 2048 + (tt + 1) * 128],
                           list(range(NT)))

    def xfm_w(fc, tb):
        return big1[:, fc * 2048 + tb * 512: fc * 2048 + (tb + 1) * 512]

    def xsrc(fc, ob, tb):
        return big1[:, fc * 2048 + tb * 128: fc * 2048 + (tb + 1) * 128]

    for which in ("q", "k"):
        wt = dram["retq" if which == "q" else "retk"]
        dst = dram["d_qfm" if which == "q" else "d_kfm"]
        st0 = [ctx.cnt["st%d" % i] for i in range(4)]

        def ev_qk7(scalar, bank, oc, tb, abase, dst=dst, st0=st0):
            k = oc * 4 + tb
            slot = k % 4
            scalar.wait_ge(stm[slot], st0[slot] + 16 * (k // 4))
            stg = big0[:, 1024 + slot * 512:1024 + (slot + 1) * 512]
            scalar.copy(out=stg, in_=psum[bank][:, 0:512]).then_inc(ae, 1)
            scalar.wait_ge(ae, abase + 1)
            scalar.dma_start(out=dst[oc, :, tb * 512:(tb + 1) * 512],
                             in_=stg).then_inc(stm[slot], 16)
        for i in range(4):
            ctx.cnt["st%d" % i] += 16 * 8
        proj_ws(ctx, sb, psum, wt, xfm_w, 4, 8, ev_qk7, None, 1, 0)

    st0 = [ctx.cnt["st%d" % i] for i in range(4)]

    def ev_v7(scalar, bank, ob, tb, abase, st0=st0):
        k = ob * NT + tb
        slot = k % 4
        scalar.wait_ge(stm[slot], st0[slot] + 16 * (k // 4))
        stg = big0[:, 1024 + slot * 512:1024 + (slot + 1) * 512]
        scalar.copy(out=stg, in_=psum[bank][:, 0:512]).then_inc(ae, 1)
        scalar.wait_ge(ae, abase + 1)
        scalar.dma_start(out=dram["d_vtm"][tb * 128:(tb + 1) * 128, ob * 512:(ob + 1) * 512],
                         in_=stg).then_inc(stm[slot], 16)
    for i in range(4):
        ctx.cnt["st%d" % i] += 16 * 8
    proj_xs(ctx, sb, psum, dram["retv"], xsrc, NT, 1024, ev_v7, None, 1, 0)

    v0 = ctx.cnt["ve"]

    def fz(vector):
        for i in range(8):
            vector.memset(big0[:, i * 1024:(i + 1) * 1024], 0.0).then_inc(ve, 1)
    ctx.cnt["ve"] += 8
    ctx.emit(vector=fz)
    ctx.barrier()

    for c in range(NT):
        y0 = ctx.cnt["dsy"]

        def fdma(sync, c=c):
            for fc in range(8):
                sync.dma_start(out=big1[:, fc * 128:(fc + 1) * 128],
                               in_=dram["d_qfm"][fc, :, c * 128:(c + 1) * 128]).then_inc(dsy, 16)
                sync.dma_start(out=big1[:, 1024 + fc * 128:1024 + (fc + 1) * 128],
                               in_=dram["d_kfm"][fc, :, c * 128:(c + 1) * 128]).then_inc(dsy, 16)
            sync.dma_start(out=big1[:, 2048:3072],
                           in_=dram["d_vtm"][c * 128:(c + 1) * 128, :]).then_inc(dsy, 16)
        ctx.cnt["dsy"] += 16 * 17
        ctx.emit(sync=fdma)
        ctx.barrier()

        t0, v0, a0 = ctx.cnt["te"], ctx.cnt["ve"], ctx.cnt["ae"]

        def fb1t(tensor, t0=t0, v0=v0, a0=a0):
            tensor.drain()
            for fc in range(8):
                tensor.matmul(psum[0][:, 0:128], lhsT=big1[:, 1024 + fc * 128:1024 + (fc + 1) * 128],
                              rhs=big1[:, fc * 128:(fc + 1) * 128],
                              start=(fc == 0), stop=(fc == 7)).then_inc(te, 1)
            tensor.drain()
            for fc in range(8):
                if fc >= 2:
                    j = fc - 2
                    if j % 2 == 0:
                        tensor.wait_ge(ve, v0 + j // 2 + 1)
                    else:
                        tensor.wait_ge(ae, a0 + (j + 1) // 2)
                tensor.transpose(out=psum[4 + fc % 2][:, 0:128],
                                 in_=big1[:, 1024 + fc * 128:1024 + (fc + 1) * 128],
                                 identity=cons[:, 0:128]).then_inc(te, 1)

        def fb1v(vector, t0=t0):
            for fc in range(0, 8, 2):
                vector.wait_ge(te, t0 + 8 + fc + 1)
                vector.tensor_scalar(out=big1[:, 3072 + fc * 128:3072 + (fc + 1) * 128],
                                     in0=psum[4 + fc % 2][:, 0:128],
                                     scalar1=smal[:, 5:6], scalar2=None,
                                     op0=ALU.mult).then_inc(ve, 1)

        def fb1a(scalar, t0=t0):
            for fc in range(1, 8, 2):
                scalar.wait_ge(te, t0 + 8 + fc + 1)
                scalar.activation(out=big1[:, 3072 + fc * 128:3072 + (fc + 1) * 128],
                                  in_=psum[4 + fc % 2][:, 0:128], func=AF.Copy,
                                  scale=smal[:, 5:6]).then_inc(ae, 1)
        ctx.cnt["te"] += 16
        ctx.cnt["ve"] += 4
        ctx.cnt["ae"] += 4
        ctx.emit(tensor=fb1t, vector=fb1v, scalar=fb1a)
        ctx.barrier()

        t0, v0 = ctx.cnt["te"], ctx.cnt["ve"]

        def fb2v(vector):
            vector.tensor_tensor(out=big1[:, 4096:4224], in0=psum[0][:, 0:128],
                                 in1=cons[:, 384:512], op=ALU.mult).then_inc(ve, 1)

        def fb2t(tensor, t0=t0, v0=v0):
            tensor.wait_ge(ve, v0 + 1)
            for ob in range(2):
                tensor.matmul(psum[1 + ob][:, 0:512], lhsT=big1[:, 4096:4224],
                              rhs=big1[:, 2048 + ob * 512:2048 + (ob + 1) * 512],
                              start=True, stop=True).then_inc(te, 1)
            for ob in range(2):
                bank = 3 if ob == 0 else 6
                for fc in range(8):
                    tensor.matmul(psum[bank][:, 0:512], lhsT=big1[:, fc * 128:(fc + 1) * 128],
                                  rhs=big0[:, fc * 1024 + ob * 512:fc * 1024 + (ob + 1) * 512],
                                  start=(fc == 0), stop=(fc == 7)).then_inc(te, 1)
        ctx.cnt["ve"] += 1
        ctx.cnt["te"] += 18
        ctx.emit(vector=fb2v, tensor=fb2t)
        ctx.barrier()

        v0 = ctx.cnt["ve"]

        def fb3v(vector, c=c, v0=v0):
            v = v0
            for ob in range(2):
                bank = 3 if ob == 0 else 6
                vector.tensor_copy(stage[:, 1, ob * 512:(ob + 1) * 512],
                                   psum[1 + ob][:, 0:512]).then_inc(ve, 1); v += 1
                vector.wait_ge(ve, v)
                vector.scalar_tensor_tensor(
                    out=stage[:, 0, ob * 512:(ob + 1) * 512], in0=psum[bank][:, 0:512],
                    scalar=smal[:, 4:5], in1=stage[:, 1, ob * 512:(ob + 1) * 512],
                    op0=ALU.mult, op1=ALU.add).then_inc(ve, 1); v += 1
                vector.wait_ge(ve, v)
                vector.tensor_tensor(out=xres[:, c, ob * 512:(ob + 1) * 512],
                                     in0=xres[:, c, ob * 512:(ob + 1) * 512],
                                     in1=stage[:, 0, ob * 512:(ob + 1) * 512],
                                     op=ALU.add).then_inc(ve, 1); v += 1
        ctx.cnt["ve"] += 6
        ctx.emit(vector=fb3v)
        ctx.barrier()

        t0, v0 = ctx.cnt["te"], ctx.cnt["ve"]
        sbanks = (0, 4, 5, 7)

        def fb4t(tensor, t0=t0, v0=v0):
            for k in range(16):
                dkc, ob = divmod(k, 2)
                if k >= 4:
                    tensor.wait_ge(ve, v0 + k - 3)
                tensor.matmul(psum[sbanks[k % 4]][:, 0:512],
                              lhsT=big1[:, 3072 + dkc * 128:3072 + (dkc + 1) * 128],
                              rhs=big1[:, 2048 + ob * 512:2048 + (ob + 1) * 512],
                              start=True, stop=True).then_inc(te, 1)

        def fb4v(vector, t0=t0):
            for k in range(16):
                dkc, ob = divmod(k, 2)
                vector.wait_ge(te, t0 + k + 1)
                sl = slice(dkc * 1024 + ob * 512, dkc * 1024 + (ob + 1) * 512)
                vector.scalar_tensor_tensor(out=big0[:, sl], in0=big0[:, sl],
                                            scalar=GQ, in1=psum[sbanks[k % 4]][:, 0:512],
                                            op0=ALU.mult, op1=ALU.add).then_inc(ve, 1)
        ctx.cnt["te"] += 16
        ctx.cnt["ve"] += 16
        ctx.emit(tensor=fb4t, vector=fb4v)
        ctx.barrier()

    y0 = ctx.cnt["dsy"]

    def fexp(sync):
        for fc in range(8):
            sync.dma_start(out=dram["ag_ret_i"][fc * 128:(fc + 1) * 128, :],
                           in_=big0[:, fc * 1024:(fc + 1) * 1024]).then_inc(dsy, 16)
        for fc in range(8):
            sync.dma_start(out=sb["sidx"][:, 24 + fc:25 + fc],
                           in_=dram["retidx"][fc * 128:(fc + 1) * 128, :]).then_inc(dsy, 16)
        sync.dma_start(out=smal[:, 140:156], in_=dram["gtcol"][:, :]).then_inc(dsy, 16)
    ctx.cnt["dsy"] += 16 * 17
    ctx.emit(sync=fexp)
    ctx.barrier()

    p0, c0 = ctx.cnt["dgp"], ctx.cnt["cc"]

    def fag(gpsimd, p0=p0, c0=c0):
        gpsimd.collective_compute(
            "AllGather", ALU.bypass, replica_groups=[list(range(NCORES))],
            ins=[dram["ag_ret_i"].opt()], outs=[dram["ag_ret_o"].opt()]).then_inc(cc, 1)
        gpsimd.wait_ge(cc, c0 + 1)
        n = 0
        for fc in range(8):
            gpsimd.indirect_dma_start(
                out=big0[:, fc * 1024:(fc + 1) * 1024], out_offset=None,
                in_=dram["ag_ret_o"][:, :],
                in_offset=bass.IndirectOffsetOnAxis(ap=sb["sidx"][:, 24 + fc:25 + fc], axis=0),
            ).then_inc(dgp, 16)
            n += 1
            gpsimd.wait_ge(dgp, p0 + 16 * n)
    ctx.cnt["cc"] += 1
    ctx.cnt["dgp"] += 16 * 8
    ctx.emit(gpsimd=fag)
    ctx.barrier()

    for tt in range(NT):
        y0 = ctx.cnt["dsy"]

        def fq(sync, tt=tt):
            for fc in range(8):
                sync.dma_start(out=big1[:, fc * 128:(fc + 1) * 128],
                               in_=dram["d_qfm"][fc, :, tt * 128:(tt + 1) * 128]).then_inc(dsy, 16)
        ctx.cnt["dsy"] += 16 * 8
        ctx.emit(sync=fq)
        ctx.barrier()

        t0, v0 = ctx.cnt["te"], ctx.cnt["ve"]

        def fct(tensor, t0=t0):
            for ob in range(2):
                for fc in range(8):
                    tensor.matmul(psum[ob][:, 0:512], lhsT=big1[:, fc * 128:(fc + 1) * 128],
                                  rhs=big0[:, fc * 1024 + ob * 512:fc * 1024 + (ob + 1) * 512],
                                  start=(fc == 0), stop=(fc == 7)).then_inc(te, 1)

        def fcv(vector, tt=tt, t0=t0, v0=v0):
            v = v0
            for ob in range(2):
                vector.wait_ge(te, t0 + 8 * (ob + 1))
                vector.tensor_scalar(out=stage[:, 0, ob * 512:(ob + 1) * 512],
                                     in0=psum[ob][:, 0:512],
                                     scalar1=smal[:, 140 + tt:141 + tt], scalar2=None,
                                     op0=ALU.mult).then_inc(ve, 1); v += 1
                vector.wait_ge(ve, v)
                vector.tensor_tensor(out=xres[:, tt, ob * 512:(ob + 1) * 512],
                                     in0=xres[:, tt, ob * 512:(ob + 1) * 512],
                                     in1=stage[:, 0, ob * 512:(ob + 1) * 512],
                                     op=ALU.add).then_inc(ve, 1); v += 1
        ctx.cnt["te"] += 16
        ctx.cnt["ve"] += 4
        ctx.emit(tensor=fct, vector=fcv)
        ctx.barrier()



def emit_mha(ctx, dram, sb, psum, rope=False):
    xres, big0, big1, stat, smal, cons, sidx, stage = (
        sb["xres"], sb["big0"], sb["big1"], sb["stat"], sb["smal"], sb["cons"],
        sb["sidx"], sb["stage"])
    dsy, dsc, dgp = ctx.sems["dsy"], ctx.sems["dsc"], ctx.sems["dgp"]
    te, ve, ae, ge, cc = (ctx.sems[k] for k in ("te", "ve", "ae", "ge", "cc"))
    stm = [ctx.sems["st%d" % i] for i in range(4)]
    agi = dram["ag_kv5_i"]
    ago = dram["ag_kv5_o"]
    NKT = 33 if rope else 32
    SW = NKT * 128

    ln_stats(ctx, sb)

    stream_xnorm_transpose(ctx, sb, psum,
                           lambda fc, tt: big1[:, fc * 2048 + tt * 128: fc * 2048 + (tt + 1) * 128],
                           list(range(NT)))

    def xsrc(fc, ob, tb):
        return big1[:, fc * 2048 + tb * 128: fc * 2048 + (tb + 1) * 128]

    def xfm_w(fc, tb):
        return big1[:, fc * 2048 + tb * 512: fc * 2048 + (tb + 1) * 512]

    kv_view = agi.rearrange("(f j) c -> f j c", j=2)

    for which in ("q", "k"):
        if rope:
            wt = dram["wq_t" if which == "q" else "wk_t"]
            bias_col = None
        else:
            wt = dram["ww_t"][:, 0:1024] if which == "q" else dram["ww_t"][:, 1024:2048]
        st0 = [ctx.cnt["st%d" % i] for i in range(4)]
        biasoff = 0 if which == "q" else 8

        def ev_qk(scalar, bank, oc, tb2, abase, which=which, st0=st0, biasoff=biasoff):
            k = oc * 4 + tb2
            slot = k % 4
            scalar.wait_ge(stm[slot], st0[slot] + 16 * (k // 4))
            if rope:
                bcol = sb["qkbias"][:, (0 if which == "q" else 1) * 8 + oc:
                                    (0 if which == "q" else 1) * 8 + oc + 1]
            else:
                bcol = sb["qkbias"][:, biasoff + oc:biasoff + oc + 1]
            stg = big0[:, 1024 + slot * 512:1024 + (slot + 1) * 512]
            scalar.activation(out=stg, in_=psum[bank][:, :], func=AF.Identity,
                              bias=bcol).then_inc(ae, 1)
            scalar.wait_ge(ae, abase + 1)
            if which == "q":
                dst = dram["d_qfm"][oc, :, tb2 * 512:(tb2 + 1) * 512]
            else:
                j, cb = divmod(tb2 * 512, 1024)
                dst = kv_view[oc * 128:(oc + 1) * 128, j, cb:cb + 512]
            scalar.dma_start(out=dst, in_=stg).then_inc(stm[slot], 16)

        if which == "q":
            y0b = ctx.cnt["dsy"]

            def fbl(sync):
                if rope:
                    sync.dma_start(out=sb["qkbias"][:, 0:8], in_=dram["bq_fm"][:, :]).then_inc(dsy, 16)
                    sync.dma_start(out=sb["qkbias"][:, 8:16], in_=dram["bk_fm"][:, :]).then_inc(dsy, 16)
                else:
                    sync.dma_start(out=sb["qkbias"][:, 0:16], in_=dram["wb_fm"][:, 0:16]).then_inc(dsy, 16)
            ctx.cnt["dsy"] += 32 if rope else 16
            ctx.emit(sync=fbl)
            ctx.barrier()

        for i in range(4):
            ctx.cnt["st%d" % i] += 16 * 8
        proj_ws(ctx, sb, psum, wt, xfm_w, 4, 8, ev_qk, None, 1, 0)

    st0 = [ctx.cnt["st%d" % i] for i in range(4)]
    y0 = ctx.cnt["dsc"]

    def fvb(scalar):
        src_ap = dram["bv_b"][:, :] if rope else dram["wb_b"][:, 2048:3072]
        scalar.dma_start(out=big0[:, 0:1024], in_=src_ap).then_inc(dsc, 16)
    ctx.cnt["dsc"] += 16
    ctx.emit(scalar=fvb)
    ctx.barrier()

    def ev_v_v(vector, bank, ob, tb, vbase):
        k = ob * NT + tb
        slot = k % 4
        vector.wait_ge(stm[slot], st0[slot] + 16 * (k // 4))
        stg = big0[:, 1024 + slot * 512:1024 + (slot + 1) * 512]
        vector.tensor_tensor(out=stg, in0=psum[bank][:, :],
                             in1=big0[:, ob * 512:(ob + 1) * 512], op=ALU.add).then_inc(ve, 1)

    def ev_v_s(scalar, bank, ob, tb, abase):
        k = ob * NT + tb
        slot = k % 4
        stg = big0[:, 1024 + slot * 512:1024 + (slot + 1) * 512]
        dst = agi[2048 + tb * 128:2048 + (tb + 1) * 128, ob * 512:(ob + 1) * 512]
        scalar.dma_start(out=dst, in_=stg).then_inc(stm[slot], 16)
        scalar.nop().then_inc(ae, 1)

    vwt = dram["wv_t"] if rope else dram["ww_t"][:, 2048:3072]
    for i in range(4):
        ctx.cnt["st%d" % i] += 16 * 8
    proj_xs(ctx, sb, psum, vwt, xsrc, NT, 1024, ev_v_s, ev_v_v, 1, 1, order="vs")

    if rope:
        y0 = ctx.cnt["dsy"]

        def fcs(sync):
            sync.dma_start(out=big1[:, 0:2048], in_=dram["cosfm"][:, :]).then_inc(dsy, 16)
            sync.dma_start(out=big1[:, 2048:4096], in_=dram["sinfm"][:, :]).then_inc(dsy, 16)
            sync.dma_start(out=big1[:, 4096:4224], in_=dram["ropePT"][:, :]).then_inc(dsy, 16)
        ctx.cnt["dsy"] += 48
        ctx.emit(sync=fcs)
        ctx.barrier()

        for which in ("q", "k"):
            t0, v0, a0 = ctx.cnt["te"], ctx.cnt["ve"], ctx.cnt["ae"]
            st0 = [ctx.cnt["st0"], ctx.cnt["st1"]]

            def blk_ap(k):
                return big1[:, 4352 + (k % 2) * 512:4352 + (k % 2 + 1) * 512]

            def dram_ap(k, which=which):
                oc, tb = divmod(k, 4)
                if which == "q":
                    return dram["d_qfm"][oc, :, tb * 512:(tb + 1) * 512]
                j, cb = divmod(tb * 512, 1024)
                return kv_view[oc * 128:(oc + 1) * 128, j, cb:cb + 512]

            def frs(sync, st0=st0):
                for k in range(32):
                    s = k % 2
                    if k >= 2:
                        sync.wait_ge(stm[s], st0[s] + 32 * (k // 2))
                    sync.dma_start(out=blk_ap(k), in_=dram_ap(k)).then_inc(stm[s], 16)

            def frt(tensor, t0=t0, v0=v0, st0=st0):
                for k in range(32):
                    s = k % 2
                    tensor.wait_ge(stm[s], st0[s] + 32 * (k // 2) + 16)
                    if k >= 4:
                        tensor.wait_ge(ve, v0 + 3 * (k - 4) + 2)
                    tensor.matmul(psum[k % 4][:, 0:512], lhsT=big1[:, 4096:4224],
                                  rhs=blk_ap(k), start=True, stop=True).then_inc(te, 1)

            def frv(vector, t0=t0, v0=v0):
                v = v0
                for k in range(32):
                    s = k % 2
                    tb = k % 4
                    cosb = big1[:, tb * 512:(tb + 1) * 512]
                    sinb = big1[:, 2048 + tb * 512:2048 + (tb + 1) * 512]
                    vector.wait_ge(te, t0 + k + 1)
                    vector.tensor_tensor(out=blk_ap(k), in0=blk_ap(k), in1=cosb,
                                         op=ALU.mult).then_inc(ve, 1); v += 1
                    vector.tensor_tensor(out=stage[:, s, 0:512], in0=psum[k % 4][:, 0:512],
                                         in1=sinb, op=ALU.mult).then_inc(ve, 1); v += 1
                    vector.wait_ge(ve, v)
                    vector.tensor_tensor(out=blk_ap(k), in0=blk_ap(k),
                                         in1=stage[:, s, 0:512], op=ALU.add).then_inc(ve, 1); v += 1

            def fra(scalar, v0=v0, st0=st0):
                for k in range(32):
                    s = k % 2
                    scalar.wait_ge(ve, v0 + 3 * k + 3)
                    scalar.dma_start(out=dram_ap(k), in_=blk_ap(k)).then_inc(stm[s], 16)
            ctx.cnt["te"] += 32
            ctx.cnt["ve"] += 96
            ctx.cnt["st0"] += 32 * 16
            ctx.cnt["st1"] += 32 * 16
            ctx.emit(sync=frs, tensor=frt, vector=frv, scalar=fra)
            ctx.barrier()

    p0, c0 = ctx.cnt["dgp"], ctx.cnt["cc"]

    def fag(gpsimd):
        gpsimd.collective_compute(
            "AllGather", ALU.bypass, replica_groups=[list(range(NCORES))],
            ins=[agi.opt()], outs=[ago.opt()]).then_inc(cc, 1)
    ctx.cnt["cc"] += 1
    ctx.emit(gpsimd=fag)
    ctx.barrier()

    y0 = ctx.cnt["dsy"]
    vtab = "v5idx"
    ktab = "k5idx"

    def fvidx(sync):
        for k32 in range(32):
            sync.dma_start(out=sidx[:, 16 + k32:17 + k32],
                           in_=dram[vtab][k32 * 128:(k32 + 1) * 128, :]).then_inc(dsy, 16)
    ctx.cnt["dsy"] += 16 * 32
    ctx.emit(sync=fvidx)
    ctx.barrier()

    for hg in range(4):
        y0 = ctx.cnt["dsy"]
        p0 = ctx.cnt["dgp"]

        def fkidx(sync, hg=hg):
            for i in range(8):
                sync.dma_start(out=sidx[:, 8 + i:9 + i],
                               in_=dram[ktab][(hg * 8 + i) * 128:(hg * 8 + i + 1) * 128, :]
                               ).then_inc(dsy, 16)
        ctx.cnt["dsy"] += 16 * 8
        ctx.emit(sync=fkidx)
        ctx.barrier()

        def fgat(gpsimd, hg=hg, p0=p0):
            n = 0
            for c2 in range(2):
                for half in range(2):
                    for j in range(2):
                        i = c2 * 4 + half * 2 + j
                        gpsimd.indirect_dma_start(
                            out=big1[:, c2 * 4096 + half * 2048 + j * 1024:
                                     c2 * 4096 + half * 2048 + (j + 1) * 1024],
                            out_offset=None, in_=ago[:, :],
                            in_offset=bass.IndirectOffsetOnAxis(ap=sidx[:, 8 + i:9 + i], axis=0),
                        ).then_inc(dgp, 16)
                        n += 1
                        gpsimd.wait_ge(dgp, p0 + 16 * n)
            for k32 in range(32):
                gpsimd.indirect_dma_start(
                    out=big1[:, 8192 + k32 * 256:8192 + (k32 + 1) * 256],
                    out_offset=None, in_=ago[:, :],
                    in_offset=bass.IndirectOffsetOnAxis(ap=sidx[:, 16 + k32:17 + k32], axis=0),
                    element_offset=256 * hg,
                ).then_inc(dgp, 16)
                n += 1
                gpsimd.wait_ge(dgp, p0 + 16 * n)
        ctx.cnt["dgp"] += 16 * 40
        ctx.emit(gpsimd=fgat)
        ctx.barrier()

        emit_attn_core(ctx, dram, sb, psum, hg, rope)

    stream_load_transpose(ctx, sb, psum,
                          lambda tt: dram["d_aout"][tt * 128:(tt + 1) * 128, :],
                          lambda fc, tt: big1[:, fc * 2048 + tt * 128: fc * 2048 + (tt + 1) * 128],
                          NT)
    y0 = ctx.cnt["dsc"]

    def fob(scalar):
        scalar.dma_start(out=big0[:, 0:1024],
                         in_=dram["bo_b" if rope else "outb_b"][:, :]).then_inc(dsc, 16)
    ctx.cnt["dsc"] += 16
    ctx.emit(scalar=fob)
    ctx.barrier()

    def ev_o_v(vector, bank, ob, tb, vbase):
        vector.tensor_tensor(out=psum[bank][:, :], in0=psum[bank][:, :],
                             in1=big0[:, ob * 512:(ob + 1) * 512], op=ALU.add).then_inc(ve, 1)
        vector.wait_ge(ve, vbase + 1)
        vector.tensor_tensor(out=xres[:, tb, ob * 512:(ob + 1) * 512],
                             in0=xres[:, tb, ob * 512:(ob + 1) * 512],
                             in1=psum[bank][:, :], op=ALU.add).then_inc(ve, 1)
    owt = dram["wo_t"] if rope else dram["outw_t"]
    proj_xs(ctx, sb, psum, owt, xsrc, NT, 1024, None, ev_o_v, 0, 2)


def emit_attn_core(ctx, dram, sb, psum, hg, rope):
    xres, big0, big1, stat, smal, cons, sidx = (
        sb["xres"], sb["big0"], sb["big1"], sb["stat"], sb["smal"], sb["cons"], sb["sidx"])
    dsy, dsc = ctx.sems["dsy"], ctx.sems["dsc"]
    te, ve, ae = ctx.sems["te"], ctx.sems["ve"], ctx.sems["ae"]
    stm = [ctx.sems["st%d" % i] for i in range(4)]
    NKT = 33 if rope else 32
    SW = NKT * 128
    NB = (SW + 511) // 512
    S_off, Q_off, AO_off = 0, 4352, 4864
    PT_off = S_off

    def fz(vector):
        vector.memset(big0[:, Q_off:Q_off + 512], 0.0).then_inc(ve, 1)
    ctx.cnt["ve"] += 1
    ctx.emit(vector=fz)
    ctx.barrier()

    agi = dram["ag_kv5_i"] if rope else None
    kv_view = agi.rearrange("(f j) c -> f j c", j=2) if rope else None
    SC = 17 if rope else 8
    VB = 14 if rope else 11
    K33_off, V33_off = 5120, 5376

    for s in range(NT):
        y0 = ctx.cnt["dsc"]

        def fql(scalar, s=s, hg=hg, tq0=ctx.cnt["te"]):
            scalar.wait_ge(te, tq0)
            for ho in range(4):
                c2, rsel = divmod(ho, 2)
                scalar.dma_start(
                    out=big0[64 * rsel:64 * rsel + 64,
                             Q_off + ho * 128:Q_off + (ho + 1) * 128],
                    in_=dram["d_qfm"][2 * hg + c2, 64 * rsel:64 * rsel + 64,
                                      s * 128:(s + 1) * 128]).then_inc(dsc, 16)
            if rope:
                for c2 in range(2):
                    scalar.dma_start(
                        out=big0[:, K33_off + c2 * 128:K33_off + (c2 + 1) * 128],
                        in_=kv_view[(2 * hg + c2) * 128:(2 * hg + c2 + 1) * 128,
                                    s // 8, (s % 8) * 128:(s % 8) * 128 + 128]
                    ).then_inc(dsc, 16)
                scalar.dma_start(
                    out=big0[:, V33_off:V33_off + 256],
                    in_=agi[2048 + s * 128:2048 + (s + 1) * 128,
                            hg * 256:(hg + 1) * 256]).then_inc(dsc, 16)
                scalar.dma_start(out=sb["mrow_sb"][0:1, 0:4224],
                                 in_=dram["mrow"][s:s + 1, :]).then_inc(dsc, 16)
        ctx.cnt["dsc"] += 128 if rope else 64
        ctx.emit(scalar=fql)
        ctx.barrier()

        for ho in range(4):
            c2, rsel = divmod(ho, 2)
            t0, v0, a0 = ctx.cnt["te"], ctx.cnt["ve"], ctx.cnt["ae"]
            st0 = [ctx.cnt["st%d" % i] for i in range(4)]
            qap = big0[:, Q_off + ho * 128:Q_off + (ho + 1) * 128]

            def pv_rhs(j, ho=ho):
                if j == 32:
                    return big0[:, V33_off + 64 * ho:V33_off + 64 * (ho + 1)]
                return big1[:, 8192 + j * 256 + 64 * ho:8192 + j * 256 + 64 * (ho + 1)]

            def ft(tensor, s=s, ho=ho, c2=c2, rsel=rsel, t0=t0, v0=v0, a0=a0, qap=qap):
                tensor.drain()
                tensor.wait_ge(ve, v0)
                tensor.wait_ge(ae, a0)
                for kb in range(8):
                    bank = kb % 4
                    if kb >= 4:
                        tensor.wait_ge(ve, v0 + kb - 3)
                    tensor.matmul(psum[bank][:, :], lhsT=qap,
                                  rhs=big1[:, c2 * 4096 + kb * 512:c2 * 4096 + (kb + 1) * 512],
                                  start=True, stop=not rope).then_inc(te, 1)
                    if rope:
                        tensor.matmul(psum[bank][:, :], lhsT=sb["cbf"][0:1, 0:128],
                                      rhs=sb["mrow_sb"][0:1, kb * 512:(kb + 1) * 512],
                                      start=False, stop=True).then_inc(te, 1)
                if rope:
                    tensor.matmul(psum[4][:, 384:512], lhsT=qap,
                                  rhs=big0[:, K33_off + c2 * 128:K33_off + (c2 + 1) * 128],
                                  start=True, stop=True).then_inc(te, 1)
                tensor.drain()
                tensor.wait_ge(ae, a0 + 1)
                for kt in range(NKT):
                    bank_t = 4 + kt % 3
                    if kt >= 3:
                        j = kt - 3
                        if j % 2 == 0:
                            tensor.wait_ge(ve, v0 + VB + j // 2 + 1)
                        else:
                            tensor.wait_ge(ae, a0 + 1 + (j + 1) // 2)
                    tensor.transpose(out=psum[bank_t][:, 0:128],
                                     in_=big0[:, S_off + kt * 128:S_off + (kt + 1) * 128],
                                     identity=cons[:, 0:128]).then_inc(te, 1)
                    if kt >= 1:
                        j = kt - 1
                        if j % 2 == 0:
                            tensor.wait_ge(ve, v0 + VB + j // 2 + 1)
                        else:
                            tensor.wait_ge(ae, a0 + 1 + (j + 1) // 2)
                        tensor.matmul(psum[7][:, 0:64],
                                      lhsT=big0[:, PT_off + j * 128:PT_off + (j + 1) * 128],
                                      rhs=pv_rhs(j),
                                      start=(j == 0), stop=False).then_inc(te, 1)
                j = NKT - 1
                if j % 2 == 0:
                    tensor.wait_ge(ve, v0 + VB + j // 2 + 1)
                else:
                    tensor.wait_ge(ae, a0 + 1 + (j + 1) // 2)
                tensor.matmul(psum[7][:, 0:64],
                              lhsT=big0[:, PT_off + j * 128:PT_off + (j + 1) * 128],
                              rhs=pv_rhs(j),
                              start=False, stop=True).then_inc(te, 1)

            def fv(vector, s=s, ho=ho, t0=t0, v0=v0, a0=a0, st0=st0):
                v = v0
                for kb in range(8):
                    bank = kb % 4
                    vector.wait_ge(te, t0 + (2 * (kb + 1) if rope else kb + 1))
                    vector.tensor_scalar(out=big0[:, S_off + kb * 512:S_off + (kb + 1) * 512],
                                         in0=psum[bank][:, :], scalar1=0.125,
                                         scalar2=-3.0e38, op0=ALU.mult, op1=ALU.max,
                                         accum_out=smal[:, 208 + kb:209 + kb]).then_inc(ve, 1); v += 1
                if rope:
                    vector.wait_ge(te, t0 + 17)
                    vector.scalar_tensor_tensor(
                        out=big0[:, S_off + 4096:S_off + 4224], in0=psum[4][:, 384:512],
                        scalar=0.125, in1=cons[:, 256:384],
                        op0=ALU.mult, op1=ALU.add).then_inc(ve, 1); v += 1
                    vector.wait_ge(ve, v)
                    vector.tensor_reduce(out=smal[:, 220:221],
                                         in_=big0[:, S_off + 4096:S_off + 4224],
                                         axis=AX.X, op=ALU.max).then_inc(ve, 1); v += 1
                vector.wait_ge(ve, v)
                vector.tensor_reduce(out=smal[:, 216:217], in_=smal[:, 208:216],
                                     axis=AX.X, op=ALU.max).then_inc(ve, 1); v += 1
                if rope:
                    vector.wait_ge(ve, v)
                    vector.tensor_tensor(out=smal[:, 216:217], in0=smal[:, 216:217],
                                         in1=smal[:, 220:221], op=ALU.max).then_inc(ve, 1); v += 1
                vector.wait_ge(ve, v)
                vector.tensor_scalar(out=smal[:, 217:218], in0=smal[:, 216:217],
                                     scalar1=-1.0, scalar2=None, op0=ALU.mult).then_inc(ve, 1); v += 1
                vector.wait_ge(ae, a0 + 1)
                vector.reciprocal(out=smal[:, 219:220], in_=smal[:, 218:219]).then_inc(ve, 1); v += 1
                for kt in range(0, NKT, 2):
                    vector.wait_ge(te, t0 + SC + 2 * kt + 1)
                    vector.tensor_copy(big0[:, PT_off + kt * 128:PT_off + (kt + 1) * 128],
                                       psum[4 + kt % 3][:, 0:128]).then_inc(ve, 1); v += 1
                vector.wait_ge(te, t0 + SC + 2 * NKT)
                slot = (s * 4 + ho) % 2
                vector.wait_ge(stm[slot], st0[slot])
                vector.tensor_scalar(out=big0[:, AO_off + slot * 64:AO_off + (slot + 1) * 64],
                                     in0=psum[7][:, 0:64], scalar1=smal[:, 219:220],
                                     scalar2=None, op0=ALU.mult).then_inc(ve, 1); v += 1

            def fa(scalar, s=s, ho=ho, t0=t0, v0=v0, a0=a0, st0=st0):
                scalar.wait_ge(ve, v0 + (13 if rope else 10))
                scalar.activation(out=big0[:, S_off:S_off + SW], in_=big0[:, S_off:S_off + SW],
                                  func=AF.Exp, bias=smal[:, 217:218],
                                  accum_out=smal[:, 218:219]).then_inc(ae, 1)
                for kt in range(1, NKT, 2):
                    scalar.wait_ge(te, t0 + SC + 2 * kt + 1)
                    scalar.copy(out=big0[:, PT_off + kt * 128:PT_off + (kt + 1) * 128],
                                in_=psum[4 + kt % 3][:, 0:128]).then_inc(ae, 1)
                slot = (s * 4 + ho) % 2
                scalar.wait_ge(ve, v0 + VB + (NKT + 1) // 2 + 1)
                scalar.dma_start(out=dram["d_aout"][s * 128:(s + 1) * 128,
                                                    hg * 256 + ho * 64:hg * 256 + (ho + 1) * 64],
                                 in_=big0[:, AO_off + slot * 64:AO_off + (slot + 1) * 64]
                                 ).then_inc(stm[slot], 16)

            nve = VB + (NKT + 1) // 2 + 1
            nae = 1 + NKT // 2
            ctx.cnt["te"] += SC + 2 * NKT
            ctx.cnt["ve"] += nve
            ctx.cnt["ae"] += nae
            ctx.cnt["st%d" % ((s * 4 + ho) % 2)] += 16
            ctx.emit(tensor=ft, scalar=fa, vector=fv)
        ctx.barrier()


def emit_whole_program(ctx, dram, sb, psum, y_out):
    ctx.dram = dram
    load_consts(ctx, dram, sb)
    load_x0(ctx, dram, sb)

    for s in BUILD_SUBLAYERS:
        if s == 1:
            emit_mamba(ctx, dram, sb, psum)
        elif s == 2:
            emit_quant_linear(ctx, dram, sb, psum, use_idx=1)
        elif s == 4:
            emit_quant_linear(ctx, dram, sb, psum, use_idx=2)
        elif s == 3:
            emit_mha(ctx, dram, sb, psum, rope=True)
        elif s == 5:
            emit_mha(ctx, dram, sb, psum, rope=False)
            if DEBUG_TAP == "multi":
                dsy_ = ctx.sems["dsy"]

                def ftap2(sync):
                    for tt in range(NT):
                        sync.dma_start(out=dram["dbgy"][tt * 128:(tt + 1) * 128, :],
                                       in_=dram["d_aout"][tt * 128:(tt + 1) * 128, :]).then_inc(dsy_, 16)
                    for tt in range(NT):
                        sync.dma_start(out=dram["dbgy"][NTOK + tt * 128:NTOK + (tt + 1) * 128, :],
                                       in_=sb["big1"][:, tt * 1024:(tt + 1) * 1024]).then_inc(dsy_, 16)
                ctx.cnt["dsy"] += 32 * 16
                ctx.emit(sync=ftap2)
                ctx.barrier()
            if DEBUG_TAP in ("aout", "qfm", "agkv0", "agkv1"):
                dsy_ = ctx.sems["dsy"]

                def ftap(sync):
                    if DEBUG_TAP == "aout":
                        for tt in range(NT):
                            sync.dma_start(out=y_out[tt * 128:(tt + 1) * 128, :],
                                           in_=dram["d_aout"][tt * 128:(tt + 1) * 128, :]).then_inc(dsy_, 16)
                            ctx.cnt["dsy"] += 16
                    elif DEBUG_TAP == "qfm":
                        for a in range(8):
                            sync.dma_start(out=y_out[a * 256:a * 256 + 128, 0:1024],
                                           in_=dram["d_qfm"][a, :, 0:1024]).then_inc(dsy_, 16)
                            sync.dma_start(out=y_out[a * 256 + 128:a * 256 + 256, 0:1024],
                                           in_=dram["d_qfm"][a, :, 1024:2048]).then_inc(dsy_, 16)
                            ctx.cnt["dsy"] += 32
                    else:
                        r0 = 0 if DEBUG_TAP == "agkv0" else 2048
                        for tt in range(NT):
                            sync.dma_start(out=y_out[tt * 128:(tt + 1) * 128, :],
                                           in_=dram["ag_kv5_i"][r0 + tt * 128:r0 + (tt + 1) * 128, :]).then_inc(dsy_, 16)
                            ctx.cnt["dsy"] += 16
                ctx.emit(sync=ftap)
                ctx.barrier()
        elif s == 6:
            emit_rwkv(ctx, dram, sb, psum, use_idx=1)
        elif s == 7:
            emit_retention(ctx, dram, sb, psum)
        elif s == 8:
            emit_rwkv(ctx, dram, sb, psum, use_idx=2)

    xres = sb["xres"]
    dsy = ctx.sems["dsy"]
    y0 = ctx.cnt["dsy"]
    if DEBUG_TAP not in (None, "multi", "big0", "big1", "smal", "stat"):
        def fnone(sync):
            sync.wait_ge(dsy, y0)
        ctx.emit(sync=fnone)
        ctx.barrier()
        return

    def fout(sync):
        if DEBUG_TAP in (None, "multi"):
            for tt in range(NT):
                sync.dma_start(out=y_out[tt * 128:(tt + 1) * 128, :],
                               in_=xres[:, tt, :]).then_inc(dsy, 16)
        elif DEBUG_TAP in ("big0", "big1"):
            buf = sb[DEBUG_TAP]
            for tt in range(NT):
                sync.dma_start(out=y_out[tt * 128:(tt + 1) * 128, :],
                               in_=buf[:, tt * 1024:(tt + 1) * 1024]).then_inc(dsy, 16)
        elif DEBUG_TAP == "smal":
            sync.dma_start(out=y_out[0:128, 0:256], in_=sb["smal"][:, :]).then_inc(dsy, 16)
            for tt in range(1, NT):
                sync.dma_start(out=y_out[tt * 128:(tt + 1) * 128, :],
                               in_=xres[:, tt, :]).then_inc(dsy, 16)
        elif DEBUG_TAP == "stat":
            sync.dma_start(out=y_out[0:128, 0:NT * 8],
                           in_=sb["stat"][:, :, :].rearrange("p a b -> p (a b)")).then_inc(dsy, 16)
            for tt in range(1, NT):
                sync.dma_start(out=y_out[tt * 128:(tt + 1) * 128, :],
                               in_=xres[:, tt, :]).then_inc(dsy, 16)
        sync.wait_ge(dsy, y0 + 16 * NT)
    ctx.cnt["dsy"] += 16 * (NT if DEBUG_TAP is None or DEBUG_TAP in ("big0", "big1") else NT)
    ctx.emit(sync=fout)
    ctx.barrier()



_cache = {}


def _stage_core_inputs(inputs, c):
    import ml_dtypes
    b, h = divmod(c, 2)

    def t(a):
        return np.ascontiguousarray(np.asarray(a, np.float32).T)

    def bc(v):
        v = np.asarray(v, np.float32)
        return np.ascontiguousarray(np.broadcast_to(v[None, :], (128, v.shape[0])))

    m = {}
    x = np.asarray(inputs["L0_attn_a_hidden_states"], np.float32)
    m["x0"] = np.ascontiguousarray(x[b, h * NTOK:(h + 1) * NTOK, :])
    m["win_t"] = t(inputs["L0_attn_a_w_in"])
    m["wout_t"] = t(inputs["L0_attn_a_w_out"])
    m["wg_t"] = t(inputs["L0_attn_a_w_gate"])
    m["wu_t"] = t(inputs["L0_attn_a_w_up"])
    m["wd_t"] = t(inputs["L0_attn_a_w_down"])
    m["wq_t"] = t(inputs["L0_attn_b_wq"]); m["wk_t"] = t(inputs["L0_attn_b_wk"])
    m["wv_t"] = t(inputs["L0_attn_b_wv"]); m["wo_t"] = t(inputs["L0_attn_b_wo"])
    m["l0fw_t"] = t(inputs["L0_ffn_weight"])
    m["ww_t"] = t(inputs["L1_attn_a_W_w"]); m["outw_t"] = t(inputs["L1_attn_a_out_w"])
    m["retq"] = np.ascontiguousarray(np.asarray(inputs["L1_attn_b_weight_q"], np.float32))
    m["retk"] = np.ascontiguousarray(np.asarray(inputs["L1_attn_b_weight_k"], np.float32))
    m["retv"] = np.ascontiguousarray(np.asarray(inputs["L1_attn_b_weight_v"], np.float32))
    m["wr_t"] = t(inputs["L1_ffn_W_r"]); m["wk2_t"] = t(inputs["L1_ffn_W_k"])
    m["wv2_t"] = t(inputs["L1_ffn_W_v"])
    m["wln1_b"] = bc(inputs["L0_attn_a_w_ln1"])
    m["wmnorm_b"] = bc(inputs["L0_attn_a_w_mnorm"])
    m["wln2_b"] = bc(inputs["L0_attn_a_w_ln2"])
    m["bconv_fm"] = np.ascontiguousarray(
        np.asarray(inputs["L0_attn_a_b_conv"], np.float32).reshape(18, 128).T)
    m["wconv_fm"] = np.ascontiguousarray(
        np.asarray(inputs["L0_attn_a_w_conv"], np.float32).reshape(18, 128, 4)
        .transpose(1, 0, 2).reshape(128, 72))
    m["dtbias_fm"] = np.asarray(inputs["L0_attn_a_dt_bias"], np.float32).reshape(32, 1)
    m["alog_fm"] = np.asarray(inputs["L0_attn_a_A_log"], np.float32).reshape(32, 1)
    m["d_bcast"] = bc(inputs["L0_attn_a_D"])
    m["bq_b"] = bc(inputs["L0_attn_b_bq"]); m["bk_b"] = bc(inputs["L0_attn_b_bk"])
    m["bv_b"] = bc(inputs["L0_attn_b_bv"]); m["bo_b"] = bc(inputs["L0_attn_b_bo"])
    m["l0fb_b"] = bc(inputs["L0_ffn_bias"])
    m["wb_b"] = bc(inputs["L1_attn_a_W_b"]); m["outb_b"] = bc(inputs["L1_attn_a_out_b"])
    m["mur_b"] = bc(inputs["L1_ffn_mu_r"]); m["muk_b"] = bc(inputs["L1_ffn_mu_k"])
    m["mur_fm"] = np.ascontiguousarray(np.asarray(inputs["L1_ffn_mu_r"], np.float32).reshape(8, 128).T)
    m["wb_fm"] = np.ascontiguousarray(np.asarray(inputs["L1_attn_a_W_b"], np.float32).reshape(24, 128).T)
    m["bq_fm"] = np.ascontiguousarray(np.asarray(inputs["L0_attn_b_bq"], np.float32).reshape(8, 128).T)
    m["bk_fm"] = np.ascontiguousarray(np.asarray(inputs["L0_attn_b_bk"], np.float32).reshape(8, 128).T)
    m["muk_fm"] = np.ascontiguousarray(np.asarray(inputs["L1_ffn_mu_k"], np.float32).reshape(8, 128).T)
    m["ident"] = np.eye(128, dtype=np.float32)
    m["ones_row"] = np.ones((1, 128), np.float32)
    m["ones_col"] = np.ones((128, 1), np.float32)
    i = np.arange(128)
    m["tril01"] = (i[:, None] >= i[None, :]).astype(np.float32)
    m["triu_big"] = np.where(i[None, :] > i[:, None], np.float32(BIGNEG), 0.0).astype(np.float32)
    m["dtile"] = np.where(i[None, :] >= i[:, None],
                          np.float32(GAMMA) ** (i[None, :] - i[:, None]).astype(np.float32),
                          0.0).astype(np.float32)
    m["gp_col"] = (np.float32(GAMMA) ** np.arange(1, 129, dtype=np.float32)).reshape(128, 1)
    m["gk_col"] = (np.float32(GAMMA) ** np.arange(127, -1, -1, dtype=np.float32)).reshape(128, 1)
    pos = np.arange(h * NTOK, (h + 1) * NTOK, dtype=np.float32)
    inv = (10000.0 ** (np.arange(0, 64, 2, dtype=np.float32) / 64.0)) ** -1.0
    fr = pos[:, None] * inv[None, :]
    emb = np.concatenate([fr, fr], -1)
    m["rope_cos"] = np.cos(emb).astype(np.float32)
    m["rope_sin"] = np.sin(emb).astype(np.float32)
    P = np.zeros((128, 128), np.float32)
    for i in range(64):
        P[2 * i, 2 * i + 1] = -1.0
        P[2 * i + 1, 2 * i] = 1.0
    m["ropePT"] = np.ascontiguousarray(P.T)
    j = np.arange(128) % 64
    invj = inv[j % 32]
    m["cosfm"] = np.cos(pos[None, :] * invj[:, None]).astype(np.float32)
    m["sinfm"] = np.sin(pos[None, :] * invj[:, None]).astype(np.float32)
    g0 = 16 * h
    mr = np.zeros((16, 4224), np.float32)
    for s in range(16):
        mr[s, (g0 + s) * 128:4096] = BIGNEG
    m["mrow"] = mr.astype(ml_dtypes.bfloat16)
    m["ones_bf"] = np.ones((1, 128), ml_dtypes.bfloat16)
    m["constvals"] = np.ascontiguousarray(
        np.broadcast_to(np.array(CONST_VALUES, np.float32), (128, len(CONST_VALUES))))
    m["flags"] = np.ascontiguousarray(
        np.broadcast_to(np.array([float(h), 0.0, 0.0, 0.0], np.float32), (128, 4)))
    p = c ^ 1
    m["kvidx"] = (b * 4096 + np.arange(32 * 128, dtype=np.int32)).reshape(-1, 1)
    m["mbidx"] = (p * 128 + np.arange(128, dtype=np.int32)).reshape(-1, 1)
    m["cvidx"] = (p * 4 + np.arange(4, dtype=np.int32)).reshape(-1, 1)
    m["retidx"] = (p * 1024 + np.arange(1024, dtype=np.int32)).reshape(-1, 1)
    m["rwidx"] = (p * 128 + np.arange(128, dtype=np.int32)).reshape(-1, 1)
    m["gtcol"] = (float(h) * np.float32(GAMMA)
                  ** (np.arange(2048, dtype=np.float32).reshape(16, 128).T + 1.0)
                  ).astype(np.float32)
    ii = np.arange(128)
    m["triu01"] = (ii[:, None] <= ii[None, :]).astype(np.float32)
    m["tril_big"] = np.where(ii[None, :] < ii[:, None], np.float32(BIGNEG), 0.0).astype(np.float32)
    m["dtbias_b"] = bc(inputs["L0_attn_a_dt_bias"])
    m["alog_b"] = bc(inputs["L0_attn_a_A_log"])
    m["wln1_fm"] = np.ascontiguousarray(
        np.asarray(inputs["L0_attn_a_w_ln1"], np.float32).reshape(8, 128).T)
    m["wln2_fm"] = np.ascontiguousarray(
        np.asarray(inputs["L0_attn_a_w_ln2"], np.float32).reshape(8, 128).T)
    m["wmnorm_fm"] = np.ascontiguousarray(
        np.asarray(inputs["L0_attn_a_w_mnorm"], np.float32).reshape(16, 128).T)
    m["cvidx2"] = (p * 2304 + np.arange(2304, dtype=np.int32)).reshape(-1, 1)
    wgu = np.empty((1024, 8192), np.float32)
    wu_t = t(inputs["L0_attn_a_w_up"])
    wg_t2 = t(inputs["L0_attn_a_w_gate"])
    for i2 in range(32):
        wgu[:, (2 * i2) * 128:(2 * i2 + 1) * 128] = wu_t[:, i2 * 128:(i2 + 1) * 128]
        wgu[:, (2 * i2 + 1) * 128:(2 * i2 + 2) * 128] = wg_t2[:, i2 * 128:(i2 + 1) * 128]
    m["wgu_t"] = wgu
    kt = np.zeros((4, 2, 2, 2, 128), np.int32)
    pp = np.arange(128, dtype=np.int32)
    for hg in range(4):
        for c2 in range(2):
            for half in range(2):
                for j in range(2):
                    kt[hg, c2, half, j] = (2 * b + half) * 4096 + 2 * (256 * hg + c2 * 128 + pp) + j
    m["k5idx"] = kt.reshape(-1, 1)[:32 * 128]
    m["k5idx"] = kt.reshape(-1, 1)
    vt = np.zeros((32, 128), np.int32)
    for k32 in range(32):
        t = k32 * 128 + pp
        vt[k32] = (2 * b + t // 2048) * 4096 + 2048 + (t % 2048)
    m["v5idx"] = vt.reshape(-1, 1)
    kt3 = np.zeros((4, 2, 2, 2, 128), np.int32)
    for hg in range(4):
        for c2 in range(2):
            for half in range(2):
                for j in range(2):
                    kt3[hg, c2, half, j] = ((2 * b + half) * 4352
                                            + 2 * (256 * hg + c2 * 128 + pp) + j)
    m["k3idx"] = kt3.reshape(-1, 1)
    vt3 = np.zeros((32, 128), np.int32)
    for k32 in range(32):
        t = k32 * 128 + pp
        vt3[k32] = (2 * b + t // 2048) * 4352 + 2048 + (t % 2048)
    m["v3idx"] = vt3.reshape(-1, 1)
    return m


def kernel(**inputs):
    key = ("prog", BUILD_SUBLAYERS, DEBUG_TAP, DEBUG_STOP)
    if key not in _cache:
        _cache[key] = build_program()
    nc = _cache[key]
    in_maps = [_stage_core_inputs(inputs, c) for c in range(NCORES)]
    res = run_bass_kernel_spmd(nc, in_maps, core_ids=list(range(NCORES)))
    out = np.empty((4, 4096, 1024), np.float32)
    for c in range(NCORES):
        b, h = divmod(c, 2)
        out[b, h * NTOK:(h + 1) * NTOK, :] = res.results[c]["y"]
    return out


# version 05
# speedup vs baseline: 1.1438x; 1.1438x over previous
import numpy as np
import concourse.bass as bass
import concourse.mybir as mybir
import bass_rust as _br
from concourse.bass_utils import run_bass_kernel_spmd
RED = _br.ReduceOp

F32 = mybir.dt.float32
BF16 = mybir.dt.bfloat16
I32 = mybir.dt.int32
AF = mybir.ActivationFunctionType
ALU = mybir.AluOpType
AX = mybir.AxisListType

CONST_VALUES = (1e-5, 1e-6, -1.0, 0.125, 1.0, 0.5, 2.0, 0.0)

NCORES = 8
NTOK = 2048
NT = 16
D = 1024
FC = 8
GAMMA = 0.96875
BIGNEG = -float(2 ** 30)

BUILD_SUBLAYERS = (1, 2, 3, 4, 5, 6, 7, 8)
DEBUG_TAP = None
DEBUG_STOP = None

ENGINES = ("sync", "scalar", "vector", "tensor", "gpsimd")


class Ctx:

    def __init__(self, nc, block):
        self.nc = nc
        self.block = block
        self.sems = {}
        self.cnt = {}
        self.dma_sems = []
        self.phase_i = 0

    def add_sem(self, name, handle, dma=False):
        self.sems[name] = handle
        self.cnt[name] = 0
        if dma:
            self.dma_sems.append(name)

    def emit(self, **fns):
        for en in ENGINES:
            if en in fns:
                getattr(self.block, en)(fns[en])

    def barrier(self):
        self.phase_i += 1
        waits = [(self.sems[s], self.cnt[s])
                 for s in ("te", "ve", "ae", "ge", "dsy", "dsc", "dve", "dgp",
                           "cc", "wl0", "wl1", "st0", "st1", "st2", "st3")
                 if self.cnt.get(s)]

        def body(eng):
            for h, v in waits:
                eng.wait_ge(h, v)

        for en in ENGINES:
            getattr(self.block, en)(body)



def build_program():
    nc = bass.Bass("TRN2", target_bir_lowering=False, debug=False,
                   dynamic_dma_scratch_size=16384)

    dram = {}

    def din(name, shape, dtype=F32):
        dram[name] = nc.dram_tensor(name, shape, dtype, kind="ExternalInput").ap()
        return dram[name]

    def dint(name, shape, dtype=F32):
        dram[name] = nc.dram_tensor(name, shape, dtype).ap()
        return dram[name]

    din("x0", [NTOK, D])
    din("win_t", [D, 4384])
    din("wout_t", [2048, D])
    din("wg_t", [D, 4096])
    din("wu_t", [D, 4096])
    din("wd_t", [4096, D])
    din("wq_t", [D, D]); din("wk_t", [D, D]); din("wv_t", [D, D]); din("wo_t", [D, D])
    din("l0fw_t", [D, D])
    din("ww_t", [D, 3072]); din("outw_t", [D, D])
    din("retq", [D, D]); din("retk", [D, D]); din("retv", [D, D])
    din("wr_t", [D, D]); din("wk2_t", [D, D]); din("wv2_t", [D, D])
    din("wln1_b", [128, D])
    din("wmnorm_b", [128, 2048])
    din("wln2_b", [128, D])
    din("bconv_fm", [128, 18])
    din("wconv_fm", [128, 18 * 4])
    din("dtbias_fm", [32, 1])
    din("alog_fm", [32, 1])
    din("d_bcast", [128, 32])
    din("bq_b", [128, D]); din("bk_b", [128, D]); din("bv_b", [128, D]); din("bo_b", [128, D])
    din("l0fb_b", [128, D])
    din("wb_b", [128, 3072]); din("outb_b", [128, D])
    din("mur_b", [128, D]); din("muk_b", [128, D])
    din("mur_fm", [128, 8]); din("muk_fm", [128, 8])
    din("wb_fm", [128, 24])
    din("bq_fm", [128, 8]); din("bk_fm", [128, 8])
    din("ropePT", [128, 128])
    din("cosfm", [128, NTOK]); din("sinfm", [128, NTOK])
    din("k5idx", [4 * 8 * 128, 1], I32)
    din("v5idx", [32 * 128, 1], I32)
    din("k3idx", [4 * 8 * 128, 1], I32)
    din("v3idx", [32 * 128, 1], I32)
    din("ident", [128, 128])
    din("ones_row", [1, 128])
    din("ones_col", [128, 1])
    din("tril01", [128, 128])
    din("triu_big", [128, 128])
    din("dtile", [128, 128])
    din("gp_col", [128, 1])
    din("gk_col", [128, 1])
    din("rope_cos", [NTOK, 64])
    din("rope_sin", [NTOK, 64])
    din("mrow", [16, 4224], BF16)
    din("ones_bf", [1, 128], BF16)
    din("flags", [128, 4])
    din("constvals", [128, len(CONST_VALUES)])
    din("kvidx", [32 * 128, 1], I32)
    din("mbidx", [128, 1], I32)
    din("cvidx", [4, 1], I32)
    din("retidx", [1024, 1], I32)
    din("rwidx", [128, 1], I32)
    din("gtcol", [128, 16])
    din("triu01", [128, 128])
    din("tril_big", [128, 128])
    din("dtbias_b", [128, 32])
    din("alog_b", [128, 32])
    din("wln1_fm", [128, 8])
    din("wln2_fm", [128, 8])
    din("wmnorm_fm", [128, 16])
    din("cvidx2", [2304, 1], I32)
    din("wgu_t", [1024, 8192])
    din("sadd", [128, 512])

    y_out = nc.dram_tensor("y", [NTOK, D], F32, kind="ExternalOutput").ap()
    if DEBUG_TAP == "multi":
        dram["dbgy"] = nc.dram_tensor("dbgy", [2 * NTOK, D], F32, kind="ExternalOutput").ap()

    dint("d_z", [NTOK, 2048])
    dint("d_xbc", [18, 128, 2051])
    dint("d_ypart", [NTOK, 2048])
    dint("d_qfm", [FC, 128, NTOK])
    dint("d_kfm", [FC, 128, NTOK])
    dint("d_vtm", [NTOK, D])
    dint("d_aout", [NTOK, D])
    dint("ag_max1_i", [1, 1]); dint("ag_max1_o", [8, 1])
    dint("ag_max2_i", [1, 1]); dint("ag_max2_o", [8, 1])
    dint("ag_kv5_i", [4096, 1024]); dint("ag_kv5_o", [8 * 4096, 1024])
    dint("ag_cv_i", [2304, 4]); dint("ag_cv_o", [8 * 2304, 4])
    dint("d_act", [4096, 2048])
    dint("ag_mb_i", [128, 2048]); dint("ag_mb_o", [8 * 128, 2048])
    dint("ag_ret_i", [1024, D]); dint("ag_ret_o", [8 * 1024, D])
    dint("ag_rw1_i", [128, D]); dint("ag_rw1_o", [8 * 128, D])
    dint("ag_rw2_i", [128, D]); dint("ag_rw2_o", [8 * 128, D])

    from contextlib import ExitStack
    with ExitStack() as stack:
        ent = stack.enter_context
        xres = ent(nc.sbuf_tensor("xres", [128, NT, D], F32))
        big0 = ent(nc.sbuf_tensor("big0", [128, 8192], F32))
        big1 = ent(nc.sbuf_tensor("big1", [128, 16384], F32))
        wbuf = ent(nc.sbuf_tensor("wbuf", [128, 2, 2048], F32))
        stage = ent(nc.sbuf_tensor("stage", [128, 2, 1024], F32))
        stat = ent(nc.sbuf_tensor("stat", [128, NT, 8], F32))
        smal = ent(nc.sbuf_tensor("smal", [128, 256], F32))
        cons = ent(nc.sbuf_tensor("cons", [128, 768], F32))
        cbf = ent(nc.sbuf_tensor("cbf", [128, 128], BF16))
        mbf = ent(nc.sbuf_tensor("mbf", [1, 2, 512], BF16))
        mrow_sb = ent(nc.sbuf_tensor("mrow_sb", [1, 4352], BF16))
        sidx = ent(nc.sbuf_tensor("sidx", [128, 64], I32))
        dtb = ent(nc.sbuf_tensor("dtb", [128, 512], F32))
        msc = ent(nc.sbuf_tensor("msc", [128, 512], F32))
        qkbias = ent(nc.sbuf_tensor("qkbias", [128, 16], F32))
        ps = [ent(nc.psum_tensor("ps%d" % i, [128, 512], F32)) for i in range(8)]
        ps0, ps1, ps2, ps3, ps4, ps5, ps6, ps7 = ps
        s_bar = ent(nc.semaphore("bar"))
        s_dsy = ent(nc.semaphore("dsy"))
        s_dsc = ent(nc.semaphore("dsc"))
        s_dve = ent(nc.semaphore("dve"))
        s_dgp = ent(nc.semaphore("dgp"))
        s_te = ent(nc.semaphore("te"))
        s_ve = ent(nc.semaphore("ve"))
        s_ae = ent(nc.semaphore("ae"))
        s_ge = ent(nc.semaphore("ge"))
        s_cc = ent(nc.semaphore("cc"))
        s_wl0 = ent(nc.semaphore("wl0"))
        s_wl1 = ent(nc.semaphore("wl1"))
        s_st = [ent(nc.semaphore("st%d" % i)) for i in range(4)]
        block = ent(nc.Block())
        ctx = Ctx(nc, block)
        ctx.add_sem("bar", s_bar)
        ctx.add_sem("dsy", s_dsy, dma=True)
        ctx.add_sem("dsc", s_dsc, dma=True)
        ctx.add_sem("dve", s_dve, dma=True)
        ctx.add_sem("dgp", s_dgp, dma=True)
        ctx.add_sem("te", s_te)
        ctx.add_sem("ve", s_ve)
        ctx.add_sem("ae", s_ae)
        ctx.add_sem("ge", s_ge)
        ctx.add_sem("cc", s_cc)
        ctx.add_sem("wl0", s_wl0, dma=True)
        ctx.add_sem("wl1", s_wl1, dma=True)
        for i in range(4):
            ctx.add_sem("st%d" % i, s_st[i], dma=True)

        for i, v in enumerate(CONST_VALUES):
            nc.const_aps.register(F32, v, smal[:, 192 + i:193 + i])

        psum = [ps0, ps1, ps2, ps3, ps4, ps5, ps6, ps7]
        sb = dict(xres=xres, big0=big0, big1=big1, wbuf=wbuf, stage=stage, stat=stat,
                  smal=smal, cons=cons, cbf=cbf, mbf=mbf, mrow_sb=mrow_sb,
                  sidx=sidx, qkbias=qkbias, dtb=dtb, msc=msc)

        emit_whole_program(ctx, dram, sb, psum, y_out)

    return nc



def load_consts(ctx, dram, sb):
    cons, cbf, smal = sb["cons"], sb["cbf"], sb["smal"]
    dsy = ctx.sems["dsy"]
    n = 0

    def f(sync):
        nonlocal n
        for dst, src in (
            (cons[:, 0:128], dram["ident"][:, :]),
            (cons[:, 128:256], dram["tril01"][:, :]),
            (cons[:, 256:384], dram["triu_big"][:, :]),
            (cons[:, 384:512], dram["dtile"][:, :]),
            (cons[:, 512:640], dram["triu01"][:, :]),
            (cons[:, 640:768], dram["tril_big"][:, :]),
            (cbf[0:1, 0:128], dram["ones_bf"][:, :]),
            (smal[:, 0:4], dram["flags"][:, :]),
            (smal[:, 4:5], dram["gp_col"][:, :]),
            (smal[:, 5:6], dram["gk_col"][:, :]),
            (smal[0:1, 64:192], dram["ones_row"][:, :]),
            (smal[:, 6:7], dram["ones_col"][:, :]),
            (smal[:, 192:192 + len(CONST_VALUES)], dram["constvals"][:, :]),
        ):
            sync.dma_start(out=dst, in_=src).then_inc(dsy, 16)
            n += 1

    ctx.emit(sync=f)
    ctx.cnt["dsy"] += 16 * n
    ctx.barrier()


def load_x0(ctx, dram, sb):
    xres = sb["xres"]
    dsy = ctx.sems["dsy"]

    def f(sync):
        for tt in range(NT):
            sync.dma_start(out=xres[:, tt, :],
                           in_=dram["x0"][tt * 128:(tt + 1) * 128, :]).then_inc(dsy, 16)
    ctx.emit(sync=f)
    ctx.cnt["dsy"] += 16 * NT
    ctx.barrier()


def ln_stats(ctx, sb):
    xres, stat, big1 = sb["xres"], sb["stat"], sb["big1"]
    ae, ve = ctx.sems["ae"], ctx.sems["ve"]

    a0, v0 = ctx.cnt["ae"], ctx.cnt["ve"]

    def fa(scalar):
        for tt in range(NT):
            scalar.activation(out=big1[:, tt * 1024:(tt + 1) * 1024],
                              in_=xres[:, tt, :], func=AF.Square).then_inc(ae, 1)

    def fv(vector):
        v = v0
        for tt in range(NT):
            vector.tensor_reduce(out=stat[:, tt, 4:5], in_=xres[:, tt, :],
                                 axis=AX.X, op=ALU.add).then_inc(ve, 1); v += 1
            vector.wait_ge(ae, a0 + tt + 1)
            vector.tensor_reduce(out=stat[:, tt, 5:6], in_=big1[:, tt * 1024:(tt + 1) * 1024],
                                 axis=AX.X, op=ALU.add).then_inc(ve, 1); v += 1
        vector.wait_ge(ve, v)
        for tt in range(NT):
            vector.tensor_scalar(out=stat[:, tt, 0:1], in0=stat[:, tt, 4:5],
                                 scalar1=1.0 / D, scalar2=None, op0=ALU.mult).then_inc(ve, 1); v += 1
            vector.tensor_scalar(out=stat[:, tt, 6:7], in0=stat[:, tt, 5:6],
                                 scalar1=1.0 / D, scalar2=None, op0=ALU.mult).then_inc(ve, 1); v += 1
            vector.wait_ge(ve, v)
            vector.scalar_tensor_tensor(out=stat[:, tt, 3:4], in0=stat[:, tt, 0:1],
                                        scalar=stat[:, tt, 0:1], in1=stat[:, tt, 6:7],
                                        op0=ALU.mult, op1=ALU.subtract).then_inc(ve, 1); v += 1

    ctx.cnt["ae"] += NT
    ctx.cnt["ve"] += 5 * NT
    ctx.emit(scalar=fa, vector=fv)
    ctx.barrier()

    a0, v0 = ctx.cnt["ae"], ctx.cnt["ve"]

    def fa2(scalar):
        for tt in range(NT):
            scalar.activation(out=stat[:, tt, 7:8], in_=stat[:, tt, 3:4],
                              func=AF.Sqrt, scale=-1.0, bias=1e-5).then_inc(ae, 1)

    def fv2(vector):
        v = v0
        for tt in range(NT):
            vector.wait_ge(ae, a0 + tt + 1)
            vector.reciprocal(out=stat[:, tt, 1:2], in_=stat[:, tt, 7:8]).then_inc(ve, 1); v += 1
            vector.wait_ge(ve, v)
            vector.scalar_tensor_tensor(out=stat[:, tt, 2:3], in0=stat[:, tt, 0:1],
                                        scalar=-1.0, in1=stat[:, tt, 1:2],
                                        op0=ALU.mult, op1=ALU.mult).then_inc(ve, 1); v += 1

    ctx.cnt["ae"] += NT
    ctx.cnt["ve"] += 2 * NT
    ctx.emit(scalar=fa2, vector=fv2)
    ctx.barrier()


def transpose_to_fm(ctx, sb, psum, src_fn, dst_fn, n_fc=FC, nt=NT):
    cons = sb["cons"]
    te, ve, ae = ctx.sems["te"], ctx.sems["ve"], ctx.sems["ae"]
    t0, v0, a0 = ctx.cnt["te"], ctx.cnt["ve"], ctx.cnt["ae"]
    nops = nt * n_fc
    banks = [4, 5, 6, 7]

    def evac_done_wait(tensor, j):
        if j % 2 == 0:
            tensor.wait_ge(ve, v0 + j // 2 + 1)
        else:
            tensor.wait_ge(ae, a0 + (j + 1) // 2)

    def ft(tensor):
        tensor.drain()
        for k in range(nops):
            if k >= 4:
                evac_done_wait(tensor, k - 4)
            tt, fc = divmod(k, n_fc)
            tensor.transpose(out=psum[banks[k % 4]][:, 0:128],
                             in_=src_fn(tt)[:, fc * 128:(fc + 1) * 128],
                             identity=cons[:, 0:128]).then_inc(te, 1)

    def fv(vector):
        for k in range(0, nops, 2):
            tt, fc = divmod(k, n_fc)
            vector.wait_ge(te, t0 + k + 1)
            vector.tensor_copy(dst_fn(fc, tt), psum[banks[k % 4]][:, 0:128]).then_inc(ve, 1)

    def fs(scalar):
        for k in range(1, nops, 2):
            tt, fc = divmod(k, n_fc)
            scalar.wait_ge(te, t0 + k + 1)
            scalar.copy(out=dst_fn(fc, tt), in_=psum[banks[k % 4]][:, 0:128]).then_inc(ae, 1)

    ctx.cnt["te"] += nops
    ctx.cnt["ve"] += (nops + 1) // 2
    ctx.cnt["ae"] += nops // 2
    ctx.emit(tensor=ft, vector=fv, scalar=fs)
    ctx.barrier()


def stream_xnorm_transpose(ctx, sb, psum, dst_fn, tiles, scale_fn=None):
    xres, stat, stage, cons = sb["xres"], sb["stat"], sb["stage"], sb["cons"]
    te, ve, ae = ctx.sems["te"], ctx.sems["ve"], ctx.sems["ae"]
    t0, v0, a0 = ctx.cnt["te"], ctx.cnt["ve"], ctx.cnt["ae"]
    nt = len(tiles)

    def xt_done(i):
        return v0 + 1 if i == 0 else v0 + 5 * i - 3

    def ev_evac_done(t, m):
        if t <= nt - 2:
            return v0 + 5 * t + 3 + m
        return v0 + 5 * (nt - 1) + 2 + m

    def fxt(vector):
        def evacs(i):
            for m in range(4):
                fc = 2 * m
                k = i * 8 + fc
                vector.wait_ge(te, t0 + k + 1)
                if scale_fn is None:
                    vector.tensor_copy(dst_fn(fc, i), psum[4 + k % 4][:, 0:128]).then_inc(ve, 1)
                else:
                    vector.tensor_scalar(out=dst_fn(fc, i), in0=psum[4 + k % 4][:, 0:128],
                                         scalar1=scale_fn(fc), scalar2=None,
                                         op0=ALU.mult).then_inc(ve, 1)
        for i, tt in enumerate(tiles):
            slot = i % 2
            if i >= 2:
                vector.wait_ge(te, t0 + 8 * (i - 1))
            vector.tensor_scalar(out=stage[:, slot, :], in0=xres[:, tt, :],
                                 scalar1=stat[:, tt, 0:1], scalar2=stat[:, tt, 1:2],
                                 op0=ALU.subtract, op1=ALU.mult).then_inc(ve, 1)
            if i >= 1:
                evacs(i - 1)
        evacs(nt - 1)

    def fxT(tensor):
        tensor.drain()
        for i in range(nt):
            slot = i % 2
            tensor.wait_ge(ve, xt_done(i))
            for fc in range(8):
                k = i * 8 + fc
                if k >= 4:
                    j = k - 4
                    tj, fcj = divmod(j, 8)
                    if fcj % 2 == 0:
                        tensor.wait_ge(ve, ev_evac_done(tj, fcj // 2))
                    else:
                        tensor.wait_ge(ae, a0 + 4 * tj + (fcj + 1) // 2)
                tensor.transpose(out=psum[4 + k % 4][:, 0:128],
                                 in_=stage[:, slot, fc * 128:(fc + 1) * 128],
                                 identity=cons[:, 0:128]).then_inc(te, 1)

    def fxA(scalar):
        for k in range(1, nt * 8, 2):
            i, fc = divmod(k, 8)
            scalar.wait_ge(te, t0 + k + 1)
            if scale_fn is None:
                scalar.copy(out=dst_fn(fc, i), in_=psum[4 + k % 4][:, 0:128]).then_inc(ae, 1)
            else:
                scalar.activation(out=dst_fn(fc, i), in_=psum[4 + k % 4][:, 0:128],
                                  func=AF.Copy, scale=scale_fn(fc)).then_inc(ae, 1)

    ctx.cnt["ve"] += nt + 4 * nt
    ctx.cnt["ae"] += 4 * nt
    ctx.cnt["te"] += 8 * nt
    ctx.emit(vector=fxt, tensor=fxT, scalar=fxA)
    ctx.barrier()


def stream_load_transpose(ctx, sb, psum, src_dram_fn, dst_fn, nt, n_fc=FC):
    stage, cons = sb["stage"], sb["cons"]
    te, ve, ae = ctx.sems["te"], ctx.sems["ve"], ctx.sems["ae"]
    stm = [ctx.sems["st0"], ctx.sems["st1"]]
    t0, v0, a0 = ctx.cnt["te"], ctx.cnt["ve"], ctx.cnt["ae"]
    s0 = [ctx.cnt["st0"], ctx.cnt["st1"]]

    def odd_evac_done(t, m):
        return a0 + (n_fc // 2) * t + m + 1

    def fa(scalar):
        scalar.wait_ge(te, t0)
        for tt in range(nt):
            slot = tt % 2
            if tt >= 1:
                for m in range(n_fc // 2):
                    fc = 2 * m + 1
                    k = (tt - 1) * n_fc + fc
                    scalar.wait_ge(te, t0 + k + 1)
                    scalar.copy(out=dst_fn(fc, tt - 1),
                                in_=psum[4 + k % 4][:, 0:128]).then_inc(ae, 1)
            if tt >= 2:
                scalar.wait_ge(te, t0 + n_fc * (tt - 1))
            scalar.dma_start(out=stage[:, slot, 0:n_fc * 128],
                             in_=src_dram_fn(tt)).then_inc(stm[slot], 16)
        for m in range(n_fc // 2):
            fc = 2 * m + 1
            k = (nt - 1) * n_fc + fc
            scalar.wait_ge(te, t0 + k + 1)
            scalar.copy(out=dst_fn(fc, nt - 1),
                        in_=psum[4 + k % 4][:, 0:128]).then_inc(ae, 1)

    def ft(tensor):
        tensor.drain()
        for tt in range(nt):
            slot = tt % 2
            tensor.wait_ge(stm[slot], s0[slot] + 16 * (tt // 2 + 1))
            for fc in range(n_fc):
                k = tt * n_fc + fc
                if k >= 4:
                    j = k - 4
                    tj, fcj = divmod(j, n_fc)
                    if fcj % 2 == 0:
                        tensor.wait_ge(ve, v0 + (n_fc // 2) * tj + fcj // 2 + 1)
                    else:
                        tensor.wait_ge(ae, odd_evac_done(tj, (fcj - 1) // 2))
                tensor.transpose(out=psum[4 + k % 4][:, 0:128],
                                 in_=stage[:, slot, fc * 128:(fc + 1) * 128],
                                 identity=cons[:, 0:128]).then_inc(te, 1)

    def fv(vector):
        for k in range(0, nt * n_fc, 2):
            tt, fc = divmod(k, n_fc)
            vector.wait_ge(te, t0 + k + 1)
            vector.tensor_copy(dst_fn(fc, tt), psum[4 + k % 4][:, 0:128]).then_inc(ve, 1)

    for slot in range(2):
        ctx.cnt["st%d" % slot] += 16 * ((nt + 1 - slot) // 2)
    ctx.cnt["te"] += nt * n_fc
    ctx.cnt["ve"] += nt * n_fc // 2
    ctx.cnt["ae"] += nt * n_fc // 2
    ctx.emit(scalar=fa, tensor=ft, vector=fv)
    ctx.barrier()


def allgather(ctx, gpsimd, name):
    cc = ctx.sems["cc"]
    gpsimd.collective_compute(
        "AllGather", ALU.bypass, replica_groups=[list(range(NCORES))],
        ins=[ctx.dram[name + "_i"].opt()], outs=[ctx.dram[name + "_o"].opt()]).then_inc(cc, 1)
    ctx.cnt["cc"] += 1



def emit_quant_linear(ctx, dram, sb, psum, use_idx):
    xres, big0, big1, wbuf, stat, smal, stage = (
        sb["xres"], sb["big0"], sb["big1"], sb["wbuf"], sb["stat"], sb["smal"], sb["stage"])
    cons = sb["cons"]
    dsy, dve, dgp = ctx.sems["dsy"], ctx.sems["dve"], ctx.sems["dgp"]
    te, ve, ae, ge, cc = (ctx.sems[k] for k in ("te", "ve", "ae", "ge", "cc"))

    ln_stats(ctx, sb)

    v0 = ctx.cnt["ve"]

    def fv(vector):
        v = v0
        for tt in range(NT):
            slot = tt % 2
            if tt >= 2:
                vector.wait_ge(ve, v - 3)
            vector.tensor_scalar(out=stage[:, slot, :], in0=xres[:, tt, :],
                                 scalar1=stat[:, tt, 0:1], scalar2=stat[:, tt, 1:2],
                                 op0=ALU.subtract, op1=ALU.mult).then_inc(ve, 1); v += 1
            vector.wait_ge(ve, v)
            vector.tensor_reduce(out=smal[:, 16 + tt:17 + tt], in_=stage[:, slot, :],
                                 axis=AX.X, op=ALU.max, apply_absolute_value=True).then_inc(ve, 1); v += 1
        vector.wait_ge(ve, v)
        vector.tensor_reduce(out=smal[:, 32:33], in_=smal[:, 16:32],
                             axis=AX.X, op=ALU.max).then_inc(ve, 1)
    ctx.cnt["ve"] += 2 * NT + 1
    ctx.emit(vector=fv)
    ctx.barrier()

    if use_idx == 1:
        wl = [ctx.sems["wl0"], ctx.sems["wl1"]]
        for half in range(2):
            w0 = [ctx.cnt["wl0"], ctx.cnt["wl1"]]

            def fsy(sync, half=half, tw0=ctx.cnt["te"]):
                sync.wait_ge(te, tw0)
                for r in range(2):
                    sync.dma_start(out=wbuf[:, r, :].rearrange("p (a o) -> p a o", a=2),
                                   in_=dram["l0fw_t"][(2 * half + r) * 256:(2 * half + r + 1) * 256, :]
                                   .rearrange("(a p) o -> p a o", p=128)).then_inc(wl[r], 16)

            a0 = ctx.cnt["ae"]

            def fa(scalar, w0=w0, a0=a0):
                for r in range(2):
                    scalar.wait_ge(wl[r], w0[r] + 16)
                    scalar.activation(out=big1[:, r * 2048:(r + 1) * 2048],
                                      in_=wbuf[:, r, :], func=AF.Abs).then_inc(ae, 1)

            def fvw(vector, half=half, a0=a0):
                for r in range(2):
                    vector.wait_ge(ae, a0 + r + 1)
                    vector.tensor_reduce(out=smal[:, 40 + 2 * half + r:41 + 2 * half + r],
                                         in_=big1[:, r * 2048:(r + 1) * 2048],
                                         axis=AX.X, op=ALU.add).then_inc(ve, 1)
            ctx.cnt["wl0"] += 16
            ctx.cnt["wl1"] += 16
            ctx.cnt["ae"] += 2
            ctx.cnt["ve"] += 2
            ctx.emit(sync=fsy, scalar=fa, vector=fvw)
            ctx.barrier()

        def fvs(vector):
            vector.tensor_reduce(out=smal[:, 44:45], in_=smal[:, 40:44],
                                 axis=AX.X, op=ALU.add).then_inc(ve, 1)
        ctx.cnt["ve"] += 1
        ctx.emit(vector=fvs)
        ctx.barrier()

        t0w, v0w = ctx.cnt["te"], ctx.cnt["ve"]

        def fg(tensor):
            tensor.matmul(psum[0][0:1, 0:1], lhsT=smal[:, 6:7], rhs=smal[:, 44:45],
                          start=True, stop=True).then_inc(te, 1)

        def fgv(vector):
            vector.wait_ge(te, t0w + 1)
            vector.tensor_copy(smal[0:1, 46:47], psum[0][0:1, 0:1]).then_inc(ve, 1)

        def fg2b(tensor):
            tensor.wait_ge(ve, v0w + 1)
            tensor.matmul(psum[0][:, 0:1], lhsT=smal[0:1, 64:192], rhs=smal[0:1, 46:47],
                          start=True, stop=True).then_inc(te, 1)

        def fgv2(vector):
            vector.wait_ge(te, t0w + 2)
            vector.tensor_copy(smal[:, 45:46], psum[0][:, 0:1]).then_inc(ve, 1)
        ctx.cnt["te"] += 2
        ctx.cnt["ve"] += 2
        ctx.emit(tensor=fg, vector=fgv)
        ctx.emit(tensor=fg2b, vector=fgv2)
        ctx.barrier()

    agi = dram["ag_max%d_i" % use_idx]
    ago = dram["ag_max%d_o" % use_idx]
    g0, p0, c0 = ctx.cnt["ge"], ctx.cnt["dgp"], ctx.cnt["cc"]

    def fg2(gpsimd):
        gpsimd.tensor_reduce(out=smal[0:1, 33:34], in_=smal[:, 32:33],
                             axis=AX.C, op=ALU.max).then_inc(ge, 1)
        gpsimd.wait_ge(ge, g0 + 1)
        gpsimd.dma_start(out=agi[0:1, 0:1], in_=smal[0:1, 33:34]).then_inc(dgp, 16)
        gpsimd.wait_ge(dgp, p0 + 16)
        gpsimd.collective_compute(
            "AllGather", ALU.bypass, replica_groups=[list(range(NCORES))],
            ins=[agi.opt()], outs=[ago.opt()]).then_inc(cc, 1)
        gpsimd.wait_ge(cc, c0 + 1)
        gpsimd.dma_start(out=smal[0:8, 36:37], in_=ago[:, 0:1]).then_inc(dgp, 16)
        gpsimd.wait_ge(dgp, p0 + 32)
        gpsimd.tensor_reduce(out=smal[0:1, 37:38], in_=smal[0:8, 36:37],
                             axis=AX.C, op=ALU.max).then_inc(ge, 1)
    ctx.cnt["ge"] += 2
    ctx.cnt["dgp"] += 32
    ctx.cnt["cc"] += 1
    ctx.emit(gpsimd=fg2)
    ctx.barrier()

    t0b, v0b = ctx.cnt["te"], ctx.cnt["ve"]

    def fbc(tensor):
        tensor.matmul(psum[0][:, 0:1], lhsT=smal[0:1, 64:192], rhs=smal[0:1, 37:38],
                      start=True, stop=True).then_inc(te, 1)

    def fbcv(vector):
        vector.wait_ge(te, t0b + 1)
        vector.tensor_copy(smal[:, 34:35], psum[0][:, 0:1]).then_inc(ve, 1)
    ctx.cnt["te"] += 1
    ctx.cnt["ve"] += 1
    ctx.emit(tensor=fbc, vector=fbcv)
    ctx.barrier()

    v0 = ctx.cnt["ve"]

    def fv3(vector):
        v = v0
        vector.tensor_scalar(out=smal[:, 49:50], in0=smal[:, 34:35],
                             scalar1=1.0 / 127.0, scalar2=None, op0=ALU.mult).then_inc(ve, 1); v += 1
        vector.tensor_scalar(out=smal[:, 50:51], in0=smal[:, 45:46],
                             scalar1=1.0 / (1024.0 * 1024.0), scalar2=None, op0=ALU.mult).then_inc(ve, 1); v += 1
        vector.wait_ge(ve, v)
        vector.tensor_scalar(out=smal[:, 51:52], in0=smal[:, 50:51],
                             scalar1=0.5, scalar2=None, op0=ALU.mult).then_inc(ve, 1); v += 1
        vector.tensor_tensor(out=smal[:, 52:53], in0=smal[:, 49:50], in1=smal[:, 50:51],
                             op=ALU.mult).then_inc(ve, 1); v += 1
        vector.reciprocal(out=smal[:, 54:55], in_=smal[:, 49:50]).then_inc(ve, 1); v += 1
        vector.wait_ge(ve, v)
        vector.tensor_scalar(out=smal[:, 53:54], in0=smal[:, 51:52],
                             scalar1=-1.0, scalar2=None, op0=ALU.mult).then_inc(ve, 1); v += 1
    ctx.cnt["ve"] += 6
    ctx.emit(vector=fv3)
    ctx.barrier()

    t0, v0, a0 = ctx.cnt["te"], ctx.cnt["ve"], ctx.cnt["ae"]
    NV = 11
    SEG = NV + 4

    def round_done(tt):
        return v0 + NV if tt == 0 else v0 + SEG * tt + 7

    def even_evac_done(t, m):
        if t <= NT - 2:
            return v0 + SEG * t + 2 * NV + 1 + m
        return v0 + NV * NT + 4 * (NT - 1) + m + 1

    def fvr(vector):
        v = v0

        def evacs(tt):
            nonlocal v
            for m in range(4):
                fc = 2 * m
                k = tt * FC + fc
                vector.wait_ge(te, t0 + k + 1)
                vector.tensor_copy(big1[:, fc * 2048 + tt * 128:fc * 2048 + (tt + 1) * 128],
                                   psum[4 + k % 4][:, 0:128]).then_inc(ve, 1)
            v += 4

        iv = big0[:, 3072:3584].bitcast(I32)
        for tt in range(NT):
            slot = tt % 2
            if tt >= 2:
                vector.wait_ge(te, t0 + 8 * (tt - 1))
            vector.tensor_scalar(out=stage[:, slot, :], in0=xres[:, tt, :],
                                 scalar1=stat[:, tt, 0:1], scalar2=stat[:, tt, 1:2],
                                 op0=ALU.subtract, op1=ALU.mult).then_inc(ve, 1); v += 1
            vector.wait_ge(ve, v)
            vector.tensor_scalar(out=stage[:, slot, :], in0=stage[:, slot, :],
                                 scalar1=smal[:, 54:55], scalar2=None, op0=ALU.mult).then_inc(ve, 1); v += 1
            for s in range(2):
                hsl = slice(s * 512, (s + 1) * 512)
                vector.wait_ge(ve, v)
                vector.tensor_copy(iv[:, 0:512], stage[:, slot, hsl]).then_inc(ve, 1); v += 1
                vector.wait_ge(ve, v)
                vector.tensor_copy(big0[:, s * 512:(s + 1) * 512], iv[:, 0:512]).then_inc(ve, 1); v += 1
            vector.wait_ge(ve, v)
            vector.tensor_tensor(out=big0[:, 1024:2048], in0=stage[:, slot, :],
                                 in1=big0[:, 0:1024], op=ALU.subtract).then_inc(ve, 1); v += 1
            vector.wait_ge(ve, v)
            vector.scalar_tensor_tensor(out=stage[:, slot, :], in0=big0[:, 1024:2048],
                                        scalar=0.5, in1=big0[:, 0:1024],
                                        op0=ALU.is_ge, op1=ALU.add).then_inc(ve, 1); v += 1
            vector.tensor_scalar(out=big0[:, 2048:3072], in0=big0[:, 1024:2048],
                                 scalar1=-0.5, scalar2=None, op0=ALU.is_le).then_inc(ve, 1); v += 1
            vector.wait_ge(ve, v)
            vector.tensor_tensor(out=stage[:, slot, :], in0=stage[:, slot, :],
                                 in1=big0[:, 2048:3072], op=ALU.subtract).then_inc(ve, 1); v += 1
            vector.wait_ge(ve, v)
            vector.tensor_scalar(out=stage[:, slot, :], in0=stage[:, slot, :],
                                 scalar1=127.0, scalar2=-128.0,
                                 op0=ALU.min, op1=ALU.max).then_inc(ve, 1); v += 1
            if tt >= 1:
                evacs(tt - 1)
        evacs(NT - 1)

    def ftT(tensor):
        tensor.drain()
        for tt in range(NT):
            slot = tt % 2
            tensor.wait_ge(ve, round_done(tt))
            for fc in range(FC):
                k = tt * FC + fc
                if k >= 4:
                    j = k - 4
                    tj, fcj = divmod(j, FC)
                    if fcj % 2 == 0:
                        tensor.wait_ge(ve, even_evac_done(tj, fcj // 2))
                    else:
                        tensor.wait_ge(ae, a0 + (j // FC) * 4 + (fcj + 1) // 2)
                tensor.transpose(out=psum[4 + k % 4][:, 0:128],
                                 in_=stage[:, slot, fc * 128:(fc + 1) * 128],
                                 identity=cons[:, 0:128]).then_inc(te, 1)

    def fae(scalar):
        for k in range(1, NT * FC, 2):
            tt, fc = divmod(k, FC)
            scalar.wait_ge(te, t0 + k + 1)
            scalar.copy(out=big1[:, fc * 2048 + tt * 128:fc * 2048 + (tt + 1) * 128],
                        in_=psum[4 + k % 4][:, 0:128]).then_inc(ae, 1)

    ctx.cnt["ve"] += SEG * NT
    ctx.cnt["ae"] += NT * FC // 2
    ctx.cnt["te"] += NT * FC
    ctx.emit(vector=fvr, tensor=ftT, scalar=fae)
    ctx.barrier()

    def fvb(scalar):
        scalar.dma_start(out=big0[:, 2048:3072], in_=dram["l0fb_b"][:, :]).then_inc(
            ctx.sems["dsc"], 16)
    ctx.cnt["dsc"] += 16
    ctx.emit(scalar=fvb)
    ctx.barrier()

    t0, v0, a0 = ctx.cnt["te"], ctx.cnt["ve"], ctx.cnt["ae"]
    wl = [ctx.sems["wl0"], ctx.sems["wl1"]]
    w0 = [ctx.cnt["wl0"], ctx.cnt["wl1"]]

    def quant_ve(ob):
        return v0 + 44 * ob + 12

    def evac_ve(k):
        ob, tt = divmod(k, NT)
        return v0 + 44 * ob + 12 + 2 * (tt + 1)

    def fsy5(sync):
        sync.wait_ge(te, t0)
        for ob in range(2):
            if ob > 0:
                sync.wait_ge(te, t0 + NT * FC * ob)
            for r in range(2):
                sync.dma_start(out=wbuf[:, r, :].rearrange("p (a o) -> p a o", a=4),
                               in_=dram["l0fw_t"][r * 512:(r + 1) * 512, ob * 512:(ob + 1) * 512]
                               .rearrange("(a p) o -> p a o", p=128)).then_inc(wl[r], 16)

    def fvec(vector):
        v = v0
        for ob in range(2):
            for r in range(2):
                vector.wait_ge(wl[r], w0[r] + 16 * (ob + 1))
                for s in range(2):
                    sl = slice(s * 1024, (s + 1) * 1024)
                    vector.wait_ge(ve, v)
                    vector.tensor_scalar(out=big0[:, 0:1024], in0=wbuf[:, r, sl],
                                         scalar1=smal[:, 51:52], scalar2=None,
                                         op0=ALU.is_gt).then_inc(ve, 1); v += 1
                    vector.tensor_scalar(out=big0[:, 1024:2048], in0=wbuf[:, r, sl],
                                         scalar1=smal[:, 53:54], scalar2=None,
                                         op0=ALU.is_lt).then_inc(ve, 1); v += 1
                    vector.wait_ge(ve, v)
                    vector.tensor_tensor(out=wbuf[:, r, sl], in0=big0[:, 0:1024],
                                         in1=big0[:, 1024:2048], op=ALU.subtract).then_inc(ve, 1); v += 1
            for tt in range(NT):
                k = ob * NT + tt
                vector.wait_ge(ae, a0 + k + 1)
                bank = tt % 2
                vector.tensor_tensor(out=psum[bank][:, :], in0=psum[bank][:, :],
                                     in1=big0[:, 2048 + ob * 512:2048 + (ob + 1) * 512],
                                     op=ALU.add).then_inc(ve, 1); v += 1
                vector.wait_ge(ve, v)
                vector.tensor_tensor(out=xres[:, tt, ob * 512:(ob + 1) * 512],
                                     in0=xres[:, tt, ob * 512:(ob + 1) * 512],
                                     in1=psum[bank][:, :], op=ALU.add).then_inc(ve, 1); v += 1

    def ftm(tensor):
        tensor.drain()
        for ob in range(2):
            tensor.wait_ge(ve, quant_ve(ob))
            for tt in range(NT):
                k = ob * NT + tt
                bank = tt % 2
                if k >= 2:
                    tensor.wait_ge(ve, evac_ve(k - 2))
                for fc in range(FC):
                    r, ic = divmod(fc, 4)
                    tensor.matmul(psum[bank][:, :],
                                  lhsT=big1[:, fc * 2048 + tt * 128: fc * 2048 + (tt + 1) * 128],
                                  rhs=wbuf[:, r, ic * 512:(ic + 1) * 512],
                                  start=(fc == 0), stop=(fc == FC - 1)).then_inc(te, 1)

    def fev(scalar):
        for ob in range(2):
            for tt in range(NT):
                k = ob * NT + tt
                bank = tt % 2
                scalar.wait_ge(te, t0 + FC * (k + 1))
                scalar.activation(out=psum[bank][:, :], in_=psum[bank][:, :], func=AF.Copy,
                                  scale=smal[:, 52:53]).then_inc(ae, 1)

    ctx.cnt["wl0"] += 32
    ctx.cnt["wl1"] += 32
    ctx.cnt["te"] += 2 * NT * FC
    ctx.cnt["ae"] += 2 * NT
    ctx.cnt["ve"] += 2 * (12 + 2 * NT)
    ctx.emit(sync=fsy5, vector=fvec, tensor=ftm, scalar=fev)
    ctx.barrier()



def proj_xs(ctx, sb, psum, dram_w, src_fn, nt, out_cols, ev_scalar, ev_vector,
            n_ae_per, n_ve_per, banks=(0, 1, 2, 3), in_chunks=8, w_row0=0, order="sv"):
    wbuf = sb["wbuf"]
    nob = out_cols // 512
    te, ve, ae = ctx.sems["te"], ctx.sems["ve"], ctx.sems["ae"]
    t0, v0, a0 = ctx.cnt["te"], ctx.cnt["ve"], ctx.cnt["ae"]
    wl = [ctx.sems["wl0"], ctx.sems["wl1"]]
    w0 = [ctx.cnt["wl0"], ctx.cnt["wl1"]]
    nbank = len(banks)
    last_is_vector = (n_ve_per > 0) if order == "sv" else False
    free_per = n_ve_per if last_is_vector else (n_ae_per or n_ve_per)
    free_sem = ve if last_is_vector else (ae if n_ae_per else ve)
    free_base = v0 if last_is_vector else (a0 if n_ae_per else v0)
    if order == "vs":
        free_per, free_sem, free_base = n_ve_per, ve, v0
    nin = in_chunks * 128

    def fsy(sync):
        sync.wait_ge(te, t0)
        for ob in range(nob):
            if ob > 0:
                sync.wait_ge(te, t0 + 2 + nt * in_chunks * ob)
            for r in range(2):
                sync.dma_start(
                    out=wbuf[:, r, :].rearrange("p (a o) -> p a o", a=4),
                    in_=dram_w[w_row0 + r * (nin // 2):w_row0 + (r + 1) * (nin // 2),
                               ob * 512:(ob + 1) * 512]
                    .rearrange("(a p) o -> p a o", p=128)).then_inc(wl[r], 16)

    def ft(tensor):
        tensor.drain()
        tensor.wait_ge(wl[0], w0[0] + 16)
        for _ in range(2):
            tensor.matmul(psum[banks[0]][:, :], lhsT=src_fn(0, 0, 0),
                          rhs=wbuf[:, 0, 0:512], start=True, stop=True).then_inc(te, 1)
        for ob in range(nob):
            tensor.wait_ge(wl[0], w0[0] + 16 * (ob + 1))
            tensor.wait_ge(wl[1], w0[1] + 16 * (ob + 1))
            for tb in range(nt):
                k = ob * nt + tb
                bank = banks[k % nbank]
                if k >= nbank:
                    tensor.wait_ge(free_sem, free_base + free_per * (k - nbank + 1))
                for fc in range(in_chunks):
                    r, ic = divmod(fc, in_chunks // 2)
                    tensor.matmul(psum[bank][:, :],
                                  lhsT=src_fn(fc, ob, tb),
                                  rhs=wbuf[:, r, ic * 512:(ic + 1) * 512],
                                  start=(fc == 0), stop=(fc == in_chunks - 1)).then_inc(te, 1)

    def fa(scalar):
        for k in range(nob * nt):
            ob, tb = divmod(k, nt)
            if order == "vs" and n_ve_per:
                scalar.wait_ge(ve, v0 + n_ve_per * (k + 1))
            else:
                scalar.wait_ge(te, t0 + 2 + in_chunks * (k + 1))
            ev_scalar(scalar, banks[k % nbank], ob, tb, a0 + n_ae_per * k)

    def fv(vector):
        for k in range(nob * nt):
            ob, tb = divmod(k, nt)
            if order == "sv" and n_ae_per:
                vector.wait_ge(ae, a0 + n_ae_per * (k + 1))
            else:
                vector.wait_ge(te, t0 + 2 + in_chunks * (k + 1))
            ev_vector(vector, banks[k % nbank], ob, tb, v0 + n_ve_per * k)

    ctx.cnt["wl0"] += 16 * nob
    ctx.cnt["wl1"] += 16 * nob
    ctx.cnt["te"] += nob * nt * in_chunks + 2
    ctx.cnt["ae"] += nob * nt * n_ae_per
    ctx.cnt["ve"] += nob * nt * n_ve_per
    fns = {"sync": fsy, "tensor": ft}
    if n_ae_per:
        fns["scalar"] = fa
    if n_ve_per:
        fns["vector"] = fv
    ctx.emit(**fns)
    ctx.barrier()


def proj_ws(ctx, sb, psum, dram_w, xfm_fn, ntb, out_chunks, ev_scalar, ev_vector,
            n_ae_per, n_ve_per, banks=(0, 1, 2, 3), in_chunks=8, tokw=512, order="sv",
            w_col0=0):
    wbuf = sb["wbuf"]
    te, ve, ae = ctx.sems["te"], ctx.sems["ve"], ctx.sems["ae"]
    t0, v0, a0 = ctx.cnt["te"], ctx.cnt["ve"], ctx.cnt["ae"]
    wl = [ctx.sems["wl0"], ctx.sems["wl1"]]
    w0 = [ctx.cnt["wl0"], ctx.cnt["wl1"]]
    nbank = len(banks)
    last_is_vector = (n_ve_per > 0) if order == "sv" else False
    free_per = n_ve_per if last_is_vector else (n_ae_per or n_ve_per)
    free_sem = ve if last_is_vector else (ae if n_ae_per else ve)
    free_base = v0 if last_is_vector else (a0 if n_ae_per else v0)
    if order == "vs":
        free_per, free_sem, free_base = n_ve_per, ve, v0

    def fsy(sync):
        sync.wait_ge(te, t0)
        for oc in range(out_chunks):
            s = oc % 2
            if oc >= 2:
                sync.wait_ge(te, t0 + ntb * in_chunks * (oc - 1))
            sync.dma_start(
                out=wbuf[:, s, 0:in_chunks * 128].rearrange("p (a o) -> p a o", o=128),
                in_=dram_w[0:in_chunks * 128, w_col0 + oc * 128:w_col0 + (oc + 1) * 128]
                .rearrange("(a p) o -> p a o", p=128)).then_inc(wl[s], 16)

    def ft(tensor):
        tensor.drain()
        for oc in range(out_chunks):
            s = oc % 2
            tensor.wait_ge(wl[s], w0[s] + 16 * (oc // 2 + 1))
            for tb in range(ntb):
                k = oc * ntb + tb
                bank = banks[k % nbank]
                if k >= nbank:
                    tensor.wait_ge(free_sem, free_base + free_per * (k - nbank + 1))
                for fc in range(in_chunks):
                    tensor.matmul(psum[bank][:, 0:tokw],
                                  lhsT=wbuf[:, s, fc * 128:(fc + 1) * 128],
                                  rhs=xfm_fn(fc, tb),
                                  start=(fc == 0), stop=(fc == in_chunks - 1)).then_inc(te, 1)

    def fa(scalar):
        for k in range(out_chunks * ntb):
            oc, tb = divmod(k, ntb)
            if order == "vs" and n_ve_per:
                scalar.wait_ge(ve, v0 + n_ve_per * (k + 1))
            else:
                scalar.wait_ge(te, t0 + in_chunks * (k + 1))
            ev_scalar(scalar, banks[k % nbank], oc, tb, a0 + n_ae_per * k)

    def fv(vector):
        for k in range(out_chunks * ntb):
            oc, tb = divmod(k, ntb)
            if order == "sv" and n_ae_per:
                vector.wait_ge(ae, a0 + n_ae_per * (k + 1))
            else:
                vector.wait_ge(te, t0 + in_chunks * (k + 1))
            ev_vector(vector, banks[k % nbank], oc, tb, v0 + n_ve_per * k)

    ctx.cnt["wl0"] += 16 * ((out_chunks + 1) // 2)
    ctx.cnt["wl1"] += 16 * (out_chunks // 2)
    ctx.cnt["te"] += out_chunks * ntb * in_chunks
    ctx.cnt["ae"] += out_chunks * ntb * n_ae_per
    ctx.cnt["ve"] += out_chunks * ntb * n_ve_per
    fns = {"sync": fsy, "tensor": ft}
    if n_ae_per:
        fns["scalar"] = fa
    if n_ve_per:
        fns["vector"] = fv
    ctx.emit(**fns)
    ctx.barrier()



def emit_rwkv(ctx, dram, sb, psum, use_idx):
    xres, big0, big1, stat, smal = (sb["xres"], sb["big0"], sb["big1"], sb["stat"], sb["smal"])
    cons, stage = sb["cons"], sb["stage"]
    dsy, dsc, dgp = ctx.sems["dsy"], ctx.sems["dsc"], ctx.sems["dgp"]
    te, ve, ae, ge, cc = (ctx.sems[k] for k in ("te", "ve", "ae", "ge", "cc"))
    agi = dram["ag_rw%d_i" % use_idx]
    ago = dram["ag_rw%d_o" % use_idx]

    ln_stats(ctx, sb)

    y0, v0 = ctx.cnt["dsy"], ctx.cnt["ve"]

    def fmu(sync):
        sync.dma_start(out=smal[:, 160:168], in_=dram["mur_fm"][:, :]).then_inc(dsy, 16)
        sync.dma_start(out=smal[:, 168:176], in_=dram["muk_fm"][:, :]).then_inc(dsy, 16)

    def fmv(vector):
        vector.wait_ge(dsy, y0 + 32)
        vector.tensor_scalar(out=smal[:, 176:184], in0=smal[:, 160:168],
                             scalar1=-1.0, scalar2=1.0, op0=ALU.mult, op1=ALU.add).then_inc(ve, 1)
        vector.tensor_scalar(out=smal[:, 184:192], in0=smal[:, 168:176],
                             scalar1=-1.0, scalar2=1.0, op0=ALU.mult, op1=ALU.add).then_inc(ve, 1)
    ctx.cnt["dsy"] += 32
    ctx.cnt["ve"] += 2
    ctx.emit(sync=fmu, vector=fmv)

    v0 = ctx.cnt["ve"]

    def fb1(vector):
        vector.tensor_scalar(out=big1[:, 0:1024], in0=xres[:, 15, :],
                             scalar1=stat[:, 15, 0:1], scalar2=stat[:, 15, 1:2],
                             op0=ALU.subtract, op1=ALU.mult).then_inc(ve, 1)
    ctx.cnt["ve"] += 1
    ctx.emit(vector=fb1)
    ctx.barrier()

    g0, p0, c0 = ctx.cnt["ge"], ctx.cnt["dgp"], ctx.cnt["cc"]
    y0 = ctx.cnt["dsy"]

    def fb2i(sync):
        sync.dma_start(out=sb["sidx"][:, 0:1], in_=dram["rwidx"][:, :]).then_inc(dsy, 16)

    def fb2(gpsimd):
        gpsimd.dma_start(out=agi[:, :], in_=big1[:, 0:1024]).then_inc(dgp, 16)
        gpsimd.wait_ge(dgp, p0 + 16)
        gpsimd.collective_compute(
            "AllGather", ALU.bypass, replica_groups=[list(range(NCORES))],
            ins=[agi.opt()], outs=[ago.opt()]).then_inc(cc, 1)
        gpsimd.wait_ge(cc, c0 + 1)
        gpsimd.wait_ge(dsy, y0 + 16)
        gpsimd.indirect_dma_start(
            out=big1[:, 1024:2048], out_offset=None,
            in_=ago[:, :], in_offset=bass.IndirectOffsetOnAxis(ap=sb["sidx"][:, 0:1], axis=0),
        ).then_inc(dgp, 16)
    ctx.cnt["dgp"] += 32
    ctx.cnt["cc"] += 1
    ctx.cnt["dsy"] += 16
    ctx.emit(sync=fb2i)
    ctx.emit(gpsimd=fb2)
    ctx.barrier()

    t0, v0, a0 = ctx.cnt["te"], ctx.cnt["ve"], ctx.cnt["ae"]

    def fb3t(tensor):
        for fc in range(8):
            if fc >= 4:
                tensor.wait_ge(ve, v0 + fc - 3)
            tensor.transpose(out=psum[4 + fc % 4][:, 0:128],
                             in_=big1[:, 1024 + fc * 128:1024 + (fc + 1) * 128],
                             identity=cons[:, 0:128]).then_inc(te, 1)

    def fb3v(vector):
        v = v0
        for fc in range(8):
            vector.wait_ge(te, t0 + fc + 1)
            vector.tensor_copy(smal[:, 144 + fc:145 + fc],
                               psum[4 + fc % 4][:, 127:128]).then_inc(ve, 1); v += 1
        vector.wait_ge(ve, v)
        vector.tensor_scalar(out=smal[:, 144:152], in0=smal[:, 144:152],
                             scalar1=smal[:, 0:1], scalar2=None, op0=ALU.mult).then_inc(ve, 1)
    ctx.cnt["te"] += 8
    ctx.cnt["ve"] += 9
    ctx.emit(tensor=fb3t, vector=fb3v)
    ctx.barrier()

    for half in range(2):
        tt0 = half * 8
        t0, v0, a0 = ctx.cnt["te"], ctx.cnt["ve"], ctx.cnt["ae"]

        def fxt(vector, tt0=tt0, t0=t0, v0=v0):
            v = v0
            for i in range(8):
                slot = i % 2
                if i >= 2:
                    vector.wait_ge(te, t0 + 8 * (i - 1))
                vector.tensor_scalar(out=stage[:, slot, :], in0=xres[:, tt0 + i, :],
                                     scalar1=stat[:, tt0 + i, 0:1],
                                     scalar2=stat[:, tt0 + i, 1:2],
                                     op0=ALU.subtract, op1=ALU.mult).then_inc(ve, 1); v += 1
                if i >= 1:
                    for m in range(4):
                        fc = 2 * m
                        k = (i - 1) * 8 + fc
                        vector.wait_ge(te, t0 + k + 1)
                        vector.tensor_copy(
                            big1[:, fc * 1024 + (i - 1) * 128:fc * 1024 + i * 128],
                            psum[4 + k % 4][:, 0:128]).then_inc(ve, 1); v += 1
            for m in range(4):
                fc = 2 * m
                k = 7 * 8 + fc
                vector.wait_ge(te, t0 + k + 1)
                vector.tensor_copy(big1[:, fc * 1024 + 7 * 128:fc * 1024 + 8 * 128],
                                   psum[4 + k % 4][:, 0:128]).then_inc(ve, 1); v += 1

        def xt_done(i):
            return v0 + 1 if i == 0 else v0 + 5 * i - 3

        def ev_evac_done(t, m):
            if t <= 6:
                return v0 + 5 * t + 3 + m
            return v0 + 37 + m

        def fxT(tensor, tt0=tt0, t0=t0, v0=v0, a0=a0):
            tensor.drain()
            for i in range(8):
                slot = i % 2
                tensor.wait_ge(ve, xt_done(i))
                for fc in range(8):
                    k = i * 8 + fc
                    if k >= 4:
                        j = k - 4
                        tj, fcj = divmod(j, 8)
                        if fcj % 2 == 0:
                            tensor.wait_ge(ve, ev_evac_done(tj, fcj // 2))
                        else:
                            tensor.wait_ge(ae, a0 + 4 * tj + (fcj + 1) // 2)
                    tensor.transpose(out=psum[4 + k % 4][:, 0:128],
                                     in_=stage[:, slot, fc * 128:(fc + 1) * 128],
                                     identity=cons[:, 0:128]).then_inc(te, 1)

        def fxA(scalar, tt0=tt0, t0=t0):
            for k in range(1, 64, 2):
                i, fc = divmod(k, 8)
                scalar.wait_ge(te, t0 + k + 1)
                scalar.copy(out=big1[:, fc * 1024 + i * 128:fc * 1024 + (i + 1) * 128],
                            in_=psum[4 + k % 4][:, 0:128]).then_inc(ae, 1)

        ctx.cnt["ve"] += 8 + 32
        ctx.cnt["ae"] += 32
        ctx.cnt["te"] += 64
        ctx.emit(vector=fxt, tensor=fxT, scalar=fxA)
        ctx.barrier()

        bnd = 144 if half == 0 else 152
        v0 = ctx.cnt["ve"]

        def fmx2(vector, bnd=bnd, v0=v0):
            v = v0
            for fc in range(8):
                for (buf, dstoff, mu, omu) in ((big0, 0, 160, 176), (big1, 8192, 168, 184)):
                    do = dstoff + fc * 1024
                    vector.tensor_scalar(
                        out=buf[:, do + 1:do + 1024],
                        in0=big1[:, fc * 1024:fc * 1024 + 1023],
                        scalar1=smal[:, omu + fc:omu + fc + 1], scalar2=None,
                        op0=ALU.mult).then_inc(ve, 1); v += 1
                    vector.tensor_scalar(
                        out=buf[:, do:do + 1],
                        in0=smal[:, bnd + fc:bnd + fc + 1],
                        scalar1=smal[:, omu + fc:omu + fc + 1], scalar2=None,
                        op0=ALU.mult).then_inc(ve, 1); v += 1
                    vector.wait_ge(ve, v)
                    vector.scalar_tensor_tensor(
                        out=buf[:, do:do + 1024],
                        in0=big1[:, fc * 1024:(fc + 1) * 1024],
                        scalar=smal[:, mu + fc:mu + fc + 1],
                        in1=buf[:, do:do + 1024],
                        op0=ALU.mult, op1=ALU.add).then_inc(ve, 1); v += 1
            vector.wait_ge(ve, v)
            for fc in range(8):
                vector.tensor_copy(smal[:, 152 + fc:153 + fc],
                                   big1[:, fc * 1024 + 1023:fc * 1024 + 1024]).then_inc(ve, 1); v += 1
        ctx.cnt["ve"] += 8 * 2 * 3 + 8
        ctx.emit(vector=fmx2)
        ctx.barrier()

        def src_r(fc, ob, tb):
            return big0[:, fc * 1024 + tb * 128: fc * 1024 + (tb + 1) * 128]

        def ev_r_s(scalar, bank, ob, tb, abase):
            scalar.activation(out=big1[:, tb * 1024 + ob * 512: tb * 1024 + (ob + 1) * 512],
                              in_=psum[bank][:, :], func=AF.Sigmoid).then_inc(ae, 1)
        proj_xs(ctx, sb, psum, dram["wr_t"], src_r, 8, 1024, ev_r_s, None, 1, 0)

        def xfm_k(fc, tb):
            return big1[:, 8192 + fc * 1024 + tb * 512: 8192 + fc * 1024 + (tb + 1) * 512]

        def ev_k_s(scalar, bank, oc, tb, abase):
            scalar.activation(out=big0[:, oc * 1024 + tb * 512: oc * 1024 + (tb + 1) * 512],
                              in_=psum[bank][:, 0:512], func=AF.Relu).then_inc(ae, 1)

        def ev_k_v(vector, bank, oc, tb, vbase):
            sl = slice(oc * 1024 + tb * 512, oc * 1024 + (tb + 1) * 512)
            vector.tensor_tensor(out=big0[:, sl], in0=big0[:, sl], in1=big0[:, sl],
                                 op=ALU.mult).then_inc(ve, 1)
        proj_ws(ctx, sb, psum, dram["wk2_t"], xfm_k, 2, 8, ev_k_s, ev_k_v, 1, 1)

        def src_kv(fc, ob, tb):
            return big0[:, fc * 1024 + tb * 128: fc * 1024 + (tb + 1) * 128]

        def ev_kv_v(vector, bank, ob, tb, vbase, tt0=tt0):
            vector.tensor_tensor(out=psum[bank][:, :], in0=psum[bank][:, :],
                                 in1=big1[:, tb * 1024 + ob * 512: tb * 1024 + (ob + 1) * 512],
                                 op=ALU.mult).then_inc(ve, 1)
            vector.wait_ge(ve, vbase + 1)
            vector.tensor_tensor(out=xres[:, tt0 + tb, ob * 512:(ob + 1) * 512],
                                 in0=xres[:, tt0 + tb, ob * 512:(ob + 1) * 512],
                                 in1=psum[bank][:, :], op=ALU.add).then_inc(ve, 1)
        proj_xs(ctx, sb, psum, dram["wv2_t"], src_kv, 8, 1024, None, ev_kv_v, 0, 2)



def emit_mamba(ctx, dram, sb, psum):
    xres, big0, big1, stat, smal, cons, stage, sidx = (
        sb["xres"], sb["big0"], sb["big1"], sb["stat"], sb["smal"], sb["cons"],
        sb["stage"], sb["sidx"])
    dtb, msc = sb["dtb"], sb["msc"]
    dsy, dsc, dgp = ctx.sems["dsy"], ctx.sems["dsc"], ctx.sems["dgp"]
    te, ve, ae, ge, cc = (ctx.sems[k] for k in ("te", "ve", "ae", "ge", "cc"))
    stm = [ctx.sems["st%d" % i] for i in range(4)]

    ln_stats(ctx, sb)

    a0, v0 = ctx.cnt["ae"], ctx.cnt["ve"]

    def frv(vector):
        v = v0
        for tt in range(NT):
            vector.tensor_copy(stat[:, tt, 6:7], stat[:, tt, 1:2]).then_inc(ve, 1); v += 1
            vector.tensor_tensor(out=stat[:, tt, 5:6], in0=stat[:, tt, 3:4],
                                 in1=stat[:, tt, 1:2], op=ALU.mult).then_inc(ve, 1); v += 1
            vector.wait_ge(ve, v)
            vector.tensor_tensor(out=stat[:, tt, 5:6], in0=stat[:, tt, 5:6],
                                 in1=stat[:, tt, 1:2], op=ALU.mult).then_inc(ve, 1); v += 1
            vector.wait_ge(ae, a0 + tt + 1)
            vector.reciprocal(out=stat[:, tt, 5:6], in_=stat[:, tt, 7:8]).then_inc(ve, 1); v += 1
            vector.wait_ge(ve, v)
            vector.tensor_tensor(out=stat[:, tt, 1:2], in0=stat[:, tt, 6:7],
                                 in1=stat[:, tt, 5:6], op=ALU.mult).then_inc(ve, 1); v += 1

    def fra(scalar):
        for tt in range(NT):
            scalar.wait_ge(ve, v0 + 5 * tt + 3)
            scalar.activation(out=stat[:, tt, 7:8], in_=stat[:, tt, 5:6],
                              func=AF.Sqrt, scale=-1.0, bias=1e-6).then_inc(ae, 1)
    ctx.cnt["ve"] += 5 * NT
    ctx.cnt["ae"] += NT
    ctx.emit(vector=frv, scalar=fra)
    ctx.barrier()

    y0 = ctx.cnt["dsy"]

    def fld(sync):
        for dst, src in ((smal[:, 224:232], dram["wln1_fm"][:, :]),
                         (msc[:, 32:64], dram["dtbias_b"][:, :]),
                         (msc[:, 0:32], dram["alog_b"][:, :]),
                         (msc[:, 384:416], dram["d_bcast"][:, 0:32]),
                         (msc[:, 416:488], dram["wconv_fm"][:, :]),
                         (msc[:, 488:506], dram["bconv_fm"][:, :])):
            sync.dma_start(out=dst, in_=src).then_inc(dsy, 16)
    ctx.cnt["dsy"] += 16 * 6
    a0, v0 = ctx.cnt["ae"], ctx.cnt["ve"]

    def fnA(scalar):
        scalar.activation(out=msc[:, 0:32], in_=msc[:, 0:32], func=AF.Exp).then_inc(ae, 1)

    def fnAv(vector):
        vector.memset(msc[:, 256:384], 1.0).then_inc(ve, 1)
        vector.wait_ge(ae, a0 + 1)
        vector.tensor_scalar(out=msc[:, 0:32], in0=msc[:, 0:32], scalar1=-1.0,
                             scalar2=None, op0=ALU.mult).then_inc(ve, 1)
    ctx.emit(sync=fld)
    ctx.barrier()
    ctx.cnt["ae"] += 1
    ctx.cnt["ve"] += 2
    ctx.emit(scalar=fnA, vector=fnAv)
    ctx.barrier()

    stream_xnorm_transpose(
        ctx, sb, psum,
        lambda fc, tt: big1[:, fc * 2048 + tt * 128: fc * 2048 + (tt + 1) * 128],
        list(range(NT)), scale_fn=lambda fc: smal[:, 224 + fc:225 + fc])

    def xsrc(fc, ob, tb):
        return big1[:, fc * 2048 + tb * 128: fc * 2048 + (tb + 1) * 128]

    def xfm_w(fc, tb):
        return big1[:, fc * 2048 + tb * 512: fc * 2048 + (tb + 1) * 512]

    st0 = [ctx.cnt["st%d" % i] for i in range(4)]

    def ev_z(scalar, bank, ob, tb, abase, st0=st0):
        k = ob * NT + tb
        slot = k % 4
        scalar.wait_ge(stm[slot], st0[slot] + 16 * (k // 4))
        stg = big0[:, 1024 + slot * 512:1024 + (slot + 1) * 512]
        scalar.copy(out=stg, in_=psum[bank][:, 0:512]).then_inc(ae, 1)
        scalar.wait_ge(ae, abase + 1)
        scalar.dma_start(out=dram["d_z"][tb * 128:(tb + 1) * 128, ob * 512:(ob + 1) * 512],
                         in_=stg).then_inc(stm[slot], 16)
    for i in range(4):
        ctx.cnt["st%d" % i] += 16 * 16
    proj_xs(ctx, sb, psum, dram["win_t"], xsrc, NT, 2048, ev_z, None, 1, 0)

    st0 = [ctx.cnt["st%d" % i] for i in range(4)]

    def ev_xbc(scalar, bank, oc, tb, abase, st0=st0):
        k = oc * 4 + tb
        slot = k % 4
        scalar.wait_ge(stm[slot], st0[slot] + 16 * (k // 4))
        stg = big0[:, 1024 + slot * 512:1024 + (slot + 1) * 512]
        scalar.copy(out=stg, in_=psum[bank][:, 0:512]).then_inc(ae, 1)
        scalar.wait_ge(ae, abase + 1)
        scalar.dma_start(out=dram["d_xbc"][oc, :, 3 + tb * 512:3 + (tb + 1) * 512],
                         in_=stg).then_inc(stm[slot], 16)
    for i in range(4):
        ctx.cnt["st%d" % i] += 16 * 18
    proj_ws(ctx, sb, psum, dram["win_t"], xfm_w, 4, 18, ev_xbc, None, 1, 0, w_col0=2048)

    y0 = ctx.cnt["dsy"]

    def fwdt(sync):
        sync.dma_start(out=big0[:, 4096:4352].rearrange("p (a o) -> p a o", a=8),
                       in_=dram["win_t"][0:1024, 4352:4384]
                       .rearrange("(a p) o -> p a o", p=128)).then_inc(dsy, 16)
    ctx.cnt["dsy"] += 16
    ctx.emit(sync=fwdt)
    ctx.barrier()

    t0, v0, a0 = ctx.cnt["te"], ctx.cnt["ve"], ctx.cnt["ae"]

    tmp1 = big0[:, 6400:6432]
    tmp2 = big0[:, 6432:6464]

    def fdt_t(tensor):
        tensor.drain()
        for tt in range(NT):
            bank = tt % 2
            if tt >= 2:
                tensor.wait_ge(ve, v0 + 3 * (tt - 2) + 1)
            for fc in range(8):
                tensor.matmul(psum[bank][:, 0:32],
                              lhsT=big1[:, fc * 2048 + tt * 128:fc * 2048 + (tt + 1) * 128],
                              rhs=big0[:, 4096 + fc * 32:4096 + (fc + 1) * 32],
                              start=(fc == 0), stop=(fc == 7)).then_inc(te, 1)

    def fdt_v(vector):
        v = v0
        for tt in range(NT):
            bank = tt % 2
            sl = slice(tt * 32, (tt + 1) * 32)
            vector.wait_ge(te, t0 + 8 * (tt + 1))
            vector.tensor_tensor(out=dtb[:, sl], in0=psum[bank][:, 0:32],
                                 in1=msc[:, 32:64], op=ALU.add).then_inc(ve, 1); v += 1
            vector.wait_ge(ve, v)
            vector.tensor_scalar(out=tmp2, in0=dtb[:, sl], scalar1=0.0,
                                 scalar2=None, op0=ALU.max).then_inc(ve, 1); v += 1
            vector.wait_ge(ve, v)
            vector.wait_ge(ae, a0 + 3 * (tt + 1))
            vector.tensor_tensor(out=dtb[:, sl], in0=tmp2, in1=tmp1,
                                 op=ALU.add).then_inc(ve, 1); v += 1

    def fdt_a(scalar):
        a = a0
        for tt in range(NT):
            sl = slice(tt * 32, (tt + 1) * 32)
            scalar.wait_ge(ve, v0 + 3 * tt + 1)
            scalar.activation(out=tmp1, in_=dtb[:, sl], func=AF.Abs).then_inc(ae, 1); a += 1
            scalar.wait_ge(ae, a)
            scalar.activation(out=tmp1, in_=tmp1, func=AF.Exp,
                              scale=-1.0).then_inc(ae, 1); a += 1
            scalar.wait_ge(ae, a)
            scalar.activation(out=tmp1, in_=tmp1, func=AF.Ln,
                              bias=1.0).then_inc(ae, 1); a += 1
    ctx.cnt["te"] += 8 * NT
    ctx.cnt["ve"] += 3 * NT
    ctx.cnt["ae"] += 3 * NT
    ctx.emit(tensor=fdt_t, vector=fdt_v, scalar=fdt_a)
    ctx.barrier()

    y0 = ctx.cnt["dsy"]

    def fbx(sync):
        for oc in range(18):
            sync.dma_start(out=dram["ag_cv_i"][oc * 128:(oc + 1) * 128, 0:4],
                           in_=dram["d_xbc"][oc, :, 2047:2051]).then_inc(dsy, 16)
        for oc in range(18):
            sync.dma_start(out=sidx[:, 32 + oc:33 + oc],
                           in_=dram["cvidx2"][oc * 128:(oc + 1) * 128, :]).then_inc(dsy, 16)
    ctx.cnt["dsy"] += 16 * 36
    ctx.emit(sync=fbx)
    ctx.barrier()

    p0, c0, v0 = ctx.cnt["dgp"], ctx.cnt["cc"], ctx.cnt["ve"]

    def fcv_ag(gpsimd):
        gpsimd.collective_compute(
            "AllGather", ALU.bypass, replica_groups=[list(range(NCORES))],
            ins=[dram["ag_cv_i"].opt()], outs=[dram["ag_cv_o"].opt()]).then_inc(cc, 1)
        gpsimd.wait_ge(cc, c0 + 1)
        n = 0
        for oc in range(18):
            gpsimd.indirect_dma_start(
                out=stage[:, 0, oc * 4:(oc + 1) * 4], out_offset=None,
                in_=dram["ag_cv_o"][:, :],
                in_offset=bass.IndirectOffsetOnAxis(ap=sidx[:, 32 + oc:33 + oc], axis=0),
            ).then_inc(dgp, 16)
            n += 1
            gpsimd.wait_ge(dgp, p0 + 16 * n)
    ctx.cnt["cc"] += 1
    ctx.cnt["dgp"] += 16 * 18
    ctx.emit(gpsimd=fcv_ag)
    ctx.barrier()

    def fcv_fl(vector):
        vector.tensor_scalar(out=stage[:, 0, 0:72], in0=stage[:, 0, 0:72],
                             scalar1=smal[:, 0:1], scalar2=None,
                             op0=ALU.mult).then_inc(ve, 1)
    ctx.cnt["ve"] += 1
    ctx.emit(vector=fcv_fl)
    ctx.barrier()

    y0 = ctx.cnt["dsy"]

    def fbw(sync):
        for oc in range(18):
            sync.dma_start(out=dram["d_xbc"][oc, :, 0:3],
                           in_=stage[:, 0, oc * 4 + 1:oc * 4 + 4]).then_inc(dsy, 16)
    ctx.cnt["dsy"] += 16 * 18
    ctx.emit(sync=fbw)
    ctx.barrier()

    v0, a0 = ctx.cnt["ve"], ctx.cnt["st2"]
    ab0 = ctx.cnt["ae"]
    sb0 = [ctx.cnt["st0"], ctx.cnt["st1"]]
    OUT = big0[:, 4224:6272]

    def fcv_in(sync):
        for oc in range(18):
            slot = oc % 2
            if oc >= 2:
                sync.wait_ge(ve, v0 + 5 * (oc - 2) + 4)
            sync.dma_start(out=big0[:, slot * 2112:slot * 2112 + 2051],
                           in_=dram["d_xbc"][oc, :, 0:2051]).then_inc(stm[slot], 16)

    def fcv_v(vector):
        v = v0
        for oc in range(18):
            slot = oc % 2
            IN = big0[:, slot * 2112:slot * 2112 + 2051]
            vector.wait_ge(stm[slot], sb0[slot] + 16 * (oc // 2 + 1))
            vector.wait_ge(st2, a0 + 16 * oc)
            vector.tensor_scalar(out=OUT[:, 0:2048], in0=IN[:, 0:2048],
                                 scalar1=msc[:, 416 + oc * 4:417 + oc * 4],
                                 scalar2=None, op0=ALU.mult).then_inc(ve, 1); v += 1
            for k in range(1, 4):
                vector.wait_ge(ve, v)
                vector.scalar_tensor_tensor(
                    out=OUT[:, 0:2048], in0=IN[:, k:2048 + k],
                    scalar=msc[:, 416 + oc * 4 + k:417 + oc * 4 + k],
                    in1=OUT[:, 0:2048], op0=ALU.mult, op1=ALU.add).then_inc(ve, 1); v += 1
            vector.wait_ge(ve, v)
            vector.tensor_scalar(out=OUT[:, 0:2048], in0=OUT[:, 0:2048],
                                 scalar1=msc[:, 488 + oc:489 + oc], scalar2=None,
                                 op0=ALU.add).then_inc(ve, 1); v += 1

    def fcv_a(scalar):
        for oc in range(18):
            scalar.wait_ge(ve, v0 + 5 * (oc + 1))
            scalar.activation(out=OUT[:, 0:2048], in_=OUT[:, 0:2048],
                              func=AF.Silu).then_inc(ae, 1)
            scalar.wait_ge(ae, ab0 + oc + 1)
            scalar.dma_start(out=dram["d_xbc"][oc, :, 3:2051],
                             in_=OUT[:, 0:2048]).then_inc(st2, 16)
    st2 = ctx.sems["st2"]
    ctx.cnt["st0"] += 16 * 9
    ctx.cnt["st1"] += 16 * 9
    ctx.cnt["ve"] += 5 * 18
    ctx.cnt["ae"] += 18
    ctx.cnt["st2"] += 16 * 18
    ctx.emit(sync=fcv_in, vector=fcv_v, scalar=fcv_a)
    ctx.barrier()

    v0 = ctx.cnt["ve"]

    def fz0(vector):
        vector.memset(big0[:, 0:1024], 0.0).then_inc(ve, 1)
        vector.memset(big0[:, 1024:2048], 0.0).then_inc(ve, 1)
        vector.memset(msc[:, 160:192], 0.0).then_inc(ve, 1)
    ctx.cnt["ve"] += 3
    ctx.emit(vector=fz0)
    ctx.barrier()

    LB, PB, EB, CE, GT = 4096, 4224, 4352, 4480, 4608
    CB = [4736, 4864]
    YB = 5120

    for c in range(NT):
        y0 = ctx.cnt["dsy"]

        def fd(sync, c=c):
            for oc in range(16):
                sync.dma_start(out=big1[:, oc * 128:(oc + 1) * 128],
                               in_=dram["d_xbc"][oc, :, 3 + c * 128:3 + (c + 1) * 128]
                               ).then_inc(dsy, 16)
            sync.dma_start(out=big1[:, 2048:2176],
                           in_=dram["d_xbc"][16, :, 3 + c * 128:3 + (c + 1) * 128]).then_inc(dsy, 16)
            sync.dma_start(out=big1[:, 2176:2304],
                           in_=dram["d_xbc"][17, :, 3 + c * 128:3 + (c + 1) * 128]).then_inc(dsy, 16)
        ctx.cnt["dsy"] += 16 * 18
        ctx.emit(sync=fd)
        ctx.barrier()

        t0, v0, a0 = ctx.cnt["te"], ctx.cnt["ve"], ctx.cnt["ae"]

        def fpv(vector, c=c, t0=t0, v0=v0):
            v = v0
            vector.tensor_tensor(out=msc[:, 224:256], in0=dtb[:, c * 32:(c + 1) * 32],
                                 in1=msc[:, 0:32], op=ALU.mult).then_inc(ve, 1); v += 1
            vector.wait_ge(te, t0 + 1)
            vector.tensor_copy(msc[:, 64:96], psum[0][:, 0:32]).then_inc(ve, 1); v += 1
            vector.wait_ge(te, t0 + 2)
            vector.tensor_copy(big0[:, GT:GT + 128], psum[1][:, 0:128]).then_inc(ve, 1); v += 1
            for k in range(17):
                vector.wait_ge(te, t0 + 2 + k + 1)
                bank = 4 + k % 2
                if k == 0:
                    vector.tensor_copy(big1[:, 2304:2432], psum[bank][:, 0:128]).then_inc(ve, 1); v += 1
                else:
                    fc = k - 1
                    vector.tensor_copy(big1[:, 4096 + fc * 128:4096 + (fc + 1) * 128],
                                       psum[bank][:, 0:128]).then_inc(ve, 1); v += 1
                    for hh in range(2):
                        h = 2 * fc + hh
                        vector.tensor_scalar(
                            out=big1[:, 6144 + h * 64:6144 + (h + 1) * 64],
                            in0=psum[bank][:, hh * 64:(hh + 1) * 64],
                            scalar1=dtb[:, c * 32 + h:c * 32 + h + 1], scalar2=None,
                            op0=ALU.mult).then_inc(ve, 1); v += 1

        def fpt(tensor, c=c, t0=t0, v0=v0):
            tensor.drain()
            tensor.wait_ge(ve, v0 + 1)
            tensor.matmul(psum[0][:, 0:32], lhsT=cons[:, 512:640], rhs=msc[:, 224:256],
                          start=True, stop=True).then_inc(te, 1)
            tensor.matmul(psum[1][:, 0:128], lhsT=big1[:, 2048:2176], rhs=big1[:, 2176:2304],
                          start=True, stop=True).then_inc(te, 1)
            tensor.drain()
            for k in range(17):
                if k >= 2:
                    if k - 2 == 0:
                        tensor.wait_ge(ve, v0 + 4)
                    else:
                        tensor.wait_ge(ve, v0 + 4 + 3 * (k - 2))
                src = big1[:, 2048:2176] if k == 0 else big1[:, (k - 1) * 128:k * 128]
                tensor.transpose(out=psum[4 + k % 2][:, 0:128], in_=src,
                                 identity=cons[:, 0:128]).then_inc(te, 1)
        ctx.cnt["te"] += 2 + 17
        ctx.cnt["ve"] += 3 + 1 + 16 * 3
        ctx.emit(tensor=fpt, vector=fpv)
        ctx.barrier()

        t0, v0 = ctx.cnt["te"], ctx.cnt["ve"]

        def fc1v(vector, t0=t0, v0=v0):
            for h in range(32):
                slot = CB[h % 2]
                if h >= 2:
                    vector.wait_ge(te, t0 + h - 1)
                vector.tensor_scalar(out=big0[:, slot:slot + 128], in0=msc[:, 256:384],
                                     scalar1=msc[:, 64 + h:65 + h], scalar2=None,
                                     op0=ALU.mult).then_inc(ve, 1)
                if h >= 2:
                    j = h - 2
                    vector.wait_ge(te, t0 + j + 1)
                    vector.tensor_copy(big1[:, 10240 + j * 128:10240 + (j + 1) * 128],
                                       psum[4 + j % 2][:, 0:128]).then_inc(ve, 1)
            for j in (30, 31):
                vector.wait_ge(te, t0 + j + 1)
                vector.tensor_copy(big1[:, 10240 + j * 128:10240 + (j + 1) * 128],
                                   psum[4 + j % 2][:, 0:128]).then_inc(ve, 1)

        def fc1t(tensor, t0=t0, v0=v0):
            tensor.drain()
            for h in range(32):
                tensor.wait_ge(ve, v0 + (h + 1 if h < 2 else 2 * h))
                tensor.transpose(out=psum[4 + h % 2][:, 0:128],
                                 in_=big0[:, CB[h % 2]:CB[h % 2] + 128],
                                 identity=cons[:, 0:128]).then_inc(te, 1)
        ctx.cnt["te"] += 32
        ctx.cnt["ve"] += 64
        ctx.emit(tensor=fc1t, vector=fc1v)
        ctx.barrier()

        t0, v0, a0 = ctx.cnt["te"], ctx.cnt["ve"], ctx.cnt["ae"]

        def fhv(vector, c=c, t0=t0, v0=v0, a0=a0):
            v = v0
            for h in range(32):
                R = big1[:, 10240 + h * 128:10240 + (h + 1) * 128]
                if h >= 1:
                    vector.wait_ge(te, t0 + 2 * h - 1)
                vector.scalar_tensor_tensor(out=big0[:, LB:LB + 128], in0=R,
                                            scalar=msc[:, 64 + h:65 + h],
                                            in1=cons[:, 640:768],
                                            op0=ALU.subtract, op1=ALU.add).then_inc(ve, 1); v += 1
                vector.wait_ge(ae, a0 + 2 * h + 1)
                vector.tensor_copy(msc[:, 96 + h:97 + h],
                                   big0[:, LB + 127:LB + 128]).then_inc(ve, 1); v += 1
                vector.tensor_tensor(out=big0[:, PB:PB + 128], in0=big0[:, LB:LB + 128],
                                     in1=big0[:, GT:GT + 128], op=ALU.mult).then_inc(ve, 1); v += 1
                vector.wait_ge(ae, a0 + 2 * h + 2)
                vector.tensor_copy(msc[:, 128 + h:129 + h],
                                   big0[:, EB + 127:EB + 128]).then_inc(ve, 1); v += 1
                if h >= 1:
                    vector.wait_ge(te, t0 + 2 * h)
                vector.tensor_tensor(out=big0[:, CE:CE + 128], in0=big1[:, 2176:2304],
                                     in1=big0[:, EB:EB + 128], op=ALU.mult).then_inc(ve, 1); v += 1

        def fha(scalar, t0=t0, v0=v0, a0=a0):
            for h in range(32):
                scalar.wait_ge(ve, v0 + 5 * h + 1)
                scalar.activation(out=big0[:, LB:LB + 128], in_=big0[:, LB:LB + 128],
                                  func=AF.Exp).then_inc(ae, 1)
                if h >= 1:
                    scalar.wait_ge(ve, v0 + 5 * h)
                scalar.activation(out=big0[:, EB:EB + 128],
                                  in_=big1[:, 10240 + h * 128:10240 + (h + 1) * 128],
                                  func=AF.Exp).then_inc(ae, 1)

        def fht(tensor, t0=t0, v0=v0, a0=a0):
            tensor.drain()
            for h in range(32):
                bank = h // 8
                vector_pt_done = v0 + 5 * h + 3
                tensor.wait_ge(ve, vector_pt_done)
                tensor.matmul(psum[bank][:, (h % 8) * 64:(h % 8) * 64 + 64],
                              lhsT=big0[:, PB:PB + 128],
                              rhs=big1[:, 6144 + h * 64:6144 + (h + 1) * 64],
                              start=True, stop=False).then_inc(te, 1)
                tensor.wait_ge(ve, v0 + 5 * h + 5)
                tensor.matmul(psum[bank][:, (h % 8) * 64:(h % 8) * 64 + 64],
                              lhsT=big0[:, CE:CE + 128],
                              rhs=big0[:, h * 64:(h + 1) * 64],
                              start=False, stop=True).then_inc(te, 1)
        ctx.cnt["ve"] += 5 * 32
        ctx.cnt["ae"] += 2 * 32
        ctx.cnt["te"] += 2 * 32
        ctx.emit(vector=fhv, scalar=fha, tensor=fht)
        ctx.barrier()

        t0, v0 = ctx.cnt["te"], ctx.cnt["ve"]

        def fdv(vector, c=c, t0=t0, v0=v0):
            v = v0
            for h in range(32):
                vector.tensor_scalar(out=big1[:, 8192 + h * 64:8192 + (h + 1) * 64],
                                     in0=big1[:, 6144 + h * 64:6144 + (h + 1) * 64],
                                     scalar1=msc[:, 96 + h:97 + h], scalar2=None,
                                     op0=ALU.mult).then_inc(ve, 1); v += 1
            for h in range(32):
                vector.scalar_tensor_tensor(
                    out=big0[:, YB + h * 64:YB + (h + 1) * 64],
                    in0=big1[:, 4096 + h * 64:4096 + (h + 1) * 64],
                    scalar=msc[:, 384 + h:385 + h],
                    in1=psum[h // 8][:, (h % 8) * 64:(h % 8) * 64 + 64],
                    op0=ALU.mult, op1=ALU.add).then_inc(ve, 1); v += 1
            for h in range(32):
                vector.wait_ge(te, t0 + 4 * ((h * 64) // 512 + 1))
                sl = slice(h * 64, (h + 1) * 64)
                vector.scalar_tensor_tensor(
                    out=big0[:, sl], in0=big0[:, sl],
                    scalar=msc[:, 128 + h:129 + h],
                    in1=psum[4 + h // 8][:, (h % 8) * 64:(h % 8) * 64 + 64],
                    op0=ALU.mult, op1=ALU.add).then_inc(ve, 1); v += 1

        def fdt2(tensor, t0=t0, v0=v0):
            for ob in range(4):
                tensor.wait_ge(ve, v0 + 8 * (ob + 1))
                for r in range(4):
                    tensor.matmul(psum[4 + ob][:, r * 128:(r + 1) * 128],
                                  lhsT=big1[:, 2304:2432],
                                  rhs=big1[:, 8192 + ob * 512 + r * 128:8192 + ob * 512 + (r + 1) * 128],
                                  start=True, stop=True).then_inc(te, 1)

        def fdd(scalar, c=c, v0=v0):
            scalar.wait_ge(ve, v0 + 64)
            scalar.dma_start(out=dram["d_ypart"][c * 128:(c + 1) * 128, :],
                             in_=big0[:, YB:YB + 2048]).then_inc(dsc, 16)
        ctx.cnt["ve"] += 96
        ctx.cnt["te"] += 16
        ctx.cnt["dsc"] += 16
        ctx.emit(vector=fdv, tensor=fdt2, scalar=fdd)
        ctx.barrier()

    y0 = ctx.cnt["dsy"]

    def fse(sync):
        sync.dma_start(out=dram["ag_mb_i"][:, :], in_=big0[:, 0:2048]).then_inc(dsy, 16)
        sync.dma_start(out=sidx[:, 50:51], in_=dram["mbidx"][:, :]).then_inc(dsy, 16)
    ctx.cnt["dsy"] += 32
    ctx.emit(sync=fse)
    ctx.barrier()

    p0, c0 = ctx.cnt["dgp"], ctx.cnt["cc"]

    def fmb_ag(gpsimd):
        gpsimd.collective_compute(
            "AllGather", ALU.bypass, replica_groups=[list(range(NCORES))],
            ins=[dram["ag_mb_i"].opt()], outs=[dram["ag_mb_o"].opt()]).then_inc(cc, 1)
        gpsimd.wait_ge(cc, c0 + 1)
        gpsimd.indirect_dma_start(
            out=big0[:, 0:2048], out_offset=None, in_=dram["ag_mb_o"][:, :],
            in_offset=bass.IndirectOffsetOnAxis(ap=sidx[:, 50:51], axis=0),
        ).then_inc(dgp, 16)
    ctx.cnt["cc"] += 1
    ctx.cnt["dgp"] += 16
    ctx.emit(gpsimd=fmb_ag)
    ctx.barrier()

    y0 = ctx.cnt["dsy"]

    def fwm(sync):
        sync.dma_start(out=smal[:, 224:240], in_=dram["wmnorm_fm"][:, :]).then_inc(dsy, 16)
    ctx.cnt["dsy"] += 16
    ctx.emit(sync=fwm)
    ctx.barrier()

    for c in range(NT):
        y0 = ctx.cnt["dsy"]

        def fgd(sync, c=c):
            sync.dma_start(out=big0[:, YB:YB + 2048],
                           in_=dram["d_ypart"][c * 128:(c + 1) * 128, :]).then_inc(dsy, 16)
            sync.dma_start(out=big1[:, 0:2048],
                           in_=dram["d_z"][c * 128:(c + 1) * 128, :]).then_inc(dsy, 16)
            sync.dma_start(out=big1[:, 2048:2176],
                           in_=dram["d_xbc"][17, :, 3 + c * 128:3 + (c + 1) * 128]).then_inc(dsy, 16)
        ctx.cnt["dsy"] += 48
        ctx.emit(sync=fgd)
        ctx.barrier()

        t0, v0, a0 = ctx.cnt["te"], ctx.cnt["ve"], ctx.cnt["ae"]

        def fgv(vector, c=c, t0=t0, v0=v0, a0=a0):
            v = v0
            vector.tensor_tensor(out=msc[:, 224:256], in0=dtb[:, c * 32:(c + 1) * 32],
                                 in1=msc[:, 0:32], op=ALU.mult).then_inc(ve, 1); v += 1
            vector.wait_ge(te, t0 + 1)
            vector.tensor_copy(msc[:, 64:96], psum[0][:, 0:32]).then_inc(ve, 1); v += 1
            vector.wait_ge(ve, v)
            vector.tensor_tensor(out=msc[:, 192:224], in0=msc[:, 160:192],
                                 in1=msc[:, 64:96], op=ALU.add).then_inc(ve, 1); v += 1
            vector.wait_ge(ae, a0 + 1)
            vector.tensor_scalar(out=msc[:, 192:224], in0=msc[:, 192:224],
                                 scalar1=smal[:, 0:1], scalar2=None,
                                 op0=ALU.mult).then_inc(ve, 1); v += 1
            vector.wait_ge(te, t0 + 2)
            vector.tensor_tensor(out=msc[:, 160:192], in0=msc[:, 160:192],
                                 in1=psum[1][:, 0:32], op=ALU.add).then_inc(ve, 1); v += 1
            for h in range(32):
                vector.wait_ge(te, t0 + 2 + ((h * 64) // 512 + 1))
                vector.scalar_tensor_tensor(
                    out=big0[:, YB + h * 64:YB + (h + 1) * 64],
                    in0=psum[4 + h // 8][:, (h % 8) * 64:(h % 8) * 64 + 64],
                    scalar=msc[:, 192 + h:193 + h],
                    in1=big0[:, YB + h * 64:YB + (h + 1) * 64],
                    op0=ALU.mult, op1=ALU.add).then_inc(ve, 1); v += 1
            vector.wait_ge(ae, a0 + 2)
            for ob in range(4):
                vector.wait_ge(ve, v)
                vector.tensor_tensor(out=big0[:, YB + ob * 512:YB + (ob + 1) * 512],
                                     in0=big0[:, YB + ob * 512:YB + (ob + 1) * 512],
                                     in1=big1[:, ob * 512:(ob + 1) * 512],
                                     op=ALU.mult).then_inc(ve, 1); v += 1
            vector.wait_ge(ae, a0 + 3)
            vector.tensor_reduce(out=msc[:, 506:507], in_=big1[:, 4096:6144],
                                 axis=AX.X, op=ALU.add).then_inc(ve, 1); v += 1
            vector.wait_ge(ve, v)
            vector.tensor_scalar(out=msc[:, 507:508], in0=msc[:, 506:507],
                                 scalar1=1.0 / 2048.0, scalar2=None,
                                 op0=ALU.mult).then_inc(ve, 1); v += 1
            vector.wait_ge(ae, a0 + 4)
            vector.reciprocal(out=msc[:, 509:510], in_=msc[:, 508:509]).then_inc(ve, 1); v += 1
            vector.wait_ge(ve, v)
            for ob in range(4):
                vector.tensor_scalar(out=big0[:, YB + ob * 512:YB + (ob + 1) * 512],
                                     in0=big0[:, YB + ob * 512:YB + (ob + 1) * 512],
                                     scalar1=msc[:, 509:510], scalar2=None,
                                     op0=ALU.mult).then_inc(ve, 1); v += 1
            for fc in range(16):
                vector.wait_ge(te, t0 + 6 + fc + 1)
                vector.tensor_scalar(out=big1[:, 6144 + fc * 128:6144 + (fc + 1) * 128],
                                     in0=psum[4 + fc % 2][:, 0:128],
                                     scalar1=smal[:, 224 + fc:225 + fc], scalar2=None,
                                     op0=ALU.mult).then_inc(ve, 1); v += 1

        def fgt(tensor, c=c, t0=t0, v0=v0):
            tensor.drain()
            tensor.wait_ge(ve, v0 + 1)
            tensor.matmul(psum[0][:, 0:32], lhsT=cons[:, 512:640], rhs=msc[:, 224:256],
                          start=True, stop=True).then_inc(te, 1)
            tensor.matmul(psum[1][:, 0:32], lhsT=msc[:, 256:384], rhs=msc[:, 224:256],
                          start=True, stop=True).then_inc(te, 1)
            tensor.wait_ge(ve, v0 + 4)
            for ob in range(4):
                tensor.matmul(psum[4 + ob][:, 0:512], lhsT=big1[:, 2048:2176],
                              rhs=big0[:, ob * 512:(ob + 1) * 512],
                              start=True, stop=True).then_inc(te, 1)
            tensor.drain()
            tensor.wait_ge(ve, v0 + 48)
            for fc in range(16):
                if fc >= 2:
                    tensor.wait_ge(ve, v0 + 48 + fc - 1)
                tensor.transpose(out=psum[4 + fc % 2][:, 0:128],
                                 in_=big0[:, YB + fc * 128:YB + (fc + 1) * 128],
                                 identity=cons[:, 0:128]).then_inc(te, 1)

        def fga(scalar, c=c, t0=t0, v0=v0, a0=a0):
            scalar.wait_ge(ve, v0 + 3)
            scalar.activation(out=msc[:, 192:224], in_=msc[:, 192:224],
                              func=AF.Exp).then_inc(ae, 1)
            scalar.wait_ge(ve, v0 + 37)
            scalar.activation(out=big1[:, 0:2048], in_=big1[:, 0:2048],
                              func=AF.Silu).then_inc(ae, 1)
            scalar.wait_ge(ve, v0 + 41)
            scalar.activation(out=big1[:, 4096:6144], in_=big0[:, YB:YB + 2048],
                              func=AF.Square).then_inc(ae, 1)
            scalar.wait_ge(ve, v0 + 43)
            scalar.activation(out=msc[:, 508:509], in_=msc[:, 507:508],
                              func=AF.Sqrt, scale=1.0, bias=1e-5).then_inc(ae, 1)
            for fc in range(16):
                scalar.wait_ge(ve, v0 + 48 + fc + 1)
                scalar.dma_start(out=dram["d_act"][fc * 128:(fc + 1) * 128, c * 128:(c + 1) * 128],
                                 in_=big1[:, 6144 + fc * 128:6144 + (fc + 1) * 128]
                                 ).then_inc(st2, 16)
        ctx.cnt["ve"] += 5 + 32 + 4 + 3 + 4 + 16
        ctx.cnt["te"] += 6 + 16
        ctx.cnt["ae"] += 4
        ctx.cnt["st2"] += 16 * 16
        ctx.emit(vector=fgv, tensor=fgt, scalar=fga)
        ctx.barrier()

    for half in range(2):
        y0 = ctx.cnt["dsy"]

        def fgl(sync, half=half):
            for fc in range(16):
                sync.dma_start(out=big1[:, fc * 1024:(fc + 1) * 1024],
                               in_=dram["d_act"][fc * 128:(fc + 1) * 128,
                                                 half * 1024:(half + 1) * 1024]).then_inc(dsy, 16)
        ctx.cnt["dsy"] += 16 * 16
        ctx.emit(sync=fgl)
        ctx.barrier()

        def srcA(fc, ob, tb):
            return big1[:, fc * 1024 + tb * 128: fc * 1024 + (tb + 1) * 128]

        def srcB(fc, ob, tb):
            return big1[:, (8 + fc) * 1024 + tb * 128: (8 + fc) * 1024 + (tb + 1) * 128]

        def ev_wA(vector, bank, ob, tb, vbase):
            vector.tensor_copy(big0[:, tb * 1024 + ob * 512:tb * 1024 + (ob + 1) * 512],
                               psum[bank][:, 0:512]).then_inc(ve, 1)
        proj_xs(ctx, sb, psum, dram["wout_t"], srcA, 8, 1024, None, ev_wA, 0, 1)

        st0 = [ctx.cnt["st%d" % i] for i in range(4)]

        def ev_wBv(vector, bank, ob, tb, vbase):
            sl = slice(tb * 1024 + ob * 512, tb * 1024 + (ob + 1) * 512)
            vector.tensor_tensor(out=big0[:, sl], in0=big0[:, sl],
                                 in1=psum[bank][:, 0:512], op=ALU.add).then_inc(ve, 1)

        def ev_wBs(scalar, bank, ob, tb, abase, half=half, st0=st0):
            k = ob * 8 + tb
            slot = k % 4
            scalar.wait_ge(stm[slot], st0[slot] + 16 * (k // 4))
            stg = stage[:, slot // 2, (slot % 2) * 512:(slot % 2) * 512 + 512]
            scalar.copy(out=stg, in_=big0[:, tb * 1024 + ob * 512:tb * 1024 + (ob + 1) * 512]
                        ).then_inc(ae, 1)
            scalar.wait_ge(ae, abase + 1)
            scalar.dma_start(
                out=dram["d_aout"][(half * 8 + tb) * 128:(half * 8 + tb + 1) * 128,
                                   ob * 512:(ob + 1) * 512],
                in_=stg).then_inc(stm[slot], 16)
        for i in range(4):
            ctx.cnt["st%d" % i] += 16 * 4
        proj_xs(ctx, sb, psum, dram["wout_t"], srcB, 8, 1024, ev_wBs, ev_wBv, 1, 1,
                w_row0=1024, order="vs")

    y0 = ctx.cnt["dsy"]

    def fl2(sync):
        sync.dma_start(out=smal[:, 224:232], in_=dram["wln2_fm"][:, :]).then_inc(dsy, 16)
    ctx.cnt["dsy"] += 16
    ctx.emit(sync=fl2)
    ctx.barrier()

    for tt in range(NT):
        y0 = ctx.cnt["dsy"]

        def fh_in(sync, tt=tt):
            sync.dma_start(out=big0[:, 0:1024],
                           in_=dram["d_aout"][tt * 128:(tt + 1) * 128, :]).then_inc(dsy, 16)
        ctx.cnt["dsy"] += 16
        ctx.emit(sync=fh_in)
        ctx.barrier()

        t0, v0, a0 = ctx.cnt["te"], ctx.cnt["ve"], ctx.cnt["ae"]

        def fhv2(vector, tt=tt, t0=t0, v0=v0, a0=a0):
            v = v0
            vector.tensor_scalar(out=big0[:, 1024:2048], in0=xres[:, tt, :],
                                 scalar1=stat[:, tt, 0:1], scalar2=stat[:, tt, 6:7],
                                 op0=ALU.subtract, op1=ALU.mult).then_inc(ve, 1); v += 1
            vector.wait_ge(ve, v)
            vector.tensor_tensor(out=big0[:, 2048:3072], in0=big0[:, 0:1024],
                                 in1=big0[:, 1024:2048], op=ALU.add).then_inc(ve, 1); v += 1
            vector.wait_ge(ae, a0 + 1)
            vector.tensor_reduce(out=msc[:, 506:507], in_=big0[:, 3072:4096],
                                 axis=AX.X, op=ALU.add).then_inc(ve, 1); v += 1
            vector.wait_ge(ve, v)
            vector.tensor_scalar(out=msc[:, 507:508], in0=msc[:, 506:507],
                                 scalar1=1.0 / 1024.0, scalar2=None,
                                 op0=ALU.mult).then_inc(ve, 1); v += 1
            vector.wait_ge(ae, a0 + 2)
            vector.reciprocal(out=msc[:, 509:510], in_=msc[:, 508:509]).then_inc(ve, 1); v += 1
            vector.wait_ge(ve, v)
            vector.tensor_scalar(out=big0[:, 3072:4096], in0=big0[:, 2048:3072],
                                 scalar1=msc[:, 509:510], scalar2=None,
                                 op0=ALU.mult).then_inc(ve, 1); v += 1
            for fc in range(8):
                vector.wait_ge(te, t0 + fc + 1)
                vector.tensor_scalar(out=big1[:, fc * 2048 + tt * 128:fc * 2048 + (tt + 1) * 128],
                                     in0=psum[4 + fc % 2][:, 0:128],
                                     scalar1=smal[:, 224 + fc:225 + fc], scalar2=None,
                                     op0=ALU.mult).then_inc(ve, 1); v += 1

        def fha2(scalar, tt=tt, v0=v0, a0=a0):
            scalar.wait_ge(ve, v0 + 2)
            scalar.activation(out=big0[:, 3072:4096], in_=big0[:, 2048:3072],
                              func=AF.Square).then_inc(ae, 1)
            scalar.wait_ge(ve, v0 + 4)
            scalar.activation(out=msc[:, 508:509], in_=msc[:, 507:508],
                              func=AF.Sqrt, scale=1.0, bias=1e-6).then_inc(ae, 1)
            scalar.wait_ge(ve, v0 + 2)
            scalar.dma_start(out=dram["d_aout"][tt * 128:(tt + 1) * 128, :],
                             in_=big0[:, 2048:3072]).then_inc(st2, 16)

        def fht2(tensor, tt=tt, v0=v0):
            tensor.drain()
            tensor.wait_ge(ve, v0 + 6)
            for fc in range(8):
                if fc >= 2:
                    tensor.wait_ge(ve, v0 + 6 + fc - 1)
                tensor.transpose(out=psum[4 + fc % 2][:, 0:128],
                                 in_=big0[:, 3072 + fc * 128:3072 + (fc + 1) * 128],
                                 identity=cons[:, 0:128]).then_inc(te, 1)
        ctx.cnt["ve"] += 6 + 8
        ctx.cnt["ae"] += 2
        ctx.cnt["st2"] += 16
        ctx.cnt["te"] += 8
        ctx.emit(vector=fhv2, scalar=fha2, tensor=fht2)
        ctx.barrier()

    wl = [ctx.sems["wl0"], ctx.sems["wl1"]]
    w0 = [ctx.cnt["wl0"], ctx.cnt["wl1"]]
    t0, v0, a0 = ctx.cnt["te"], ctx.cnt["ve"], ctx.cnt["ae"]
    oslots = [ctx.sems["st3"], ctx.sems["st2"]]
    ob0 = [ctx.cnt["st3"], ctx.cnt["st2"]]

    def fgu_sy(sync):
        sync.wait_ge(te, t0)
        for oc2 in range(32):
            if oc2 >= 1:
                sync.wait_ge(te, t0 + 64 * oc2)
            for r in range(2):
                sync.dma_start(
                    out=wbuf_ap(sb)[:, r, 0:1024].rearrange("p (a o) -> p a o", o=128),
                    in_=dram["wgu_t"][0:1024, (2 * oc2 + r) * 128:(2 * oc2 + r + 1) * 128]
                    .rearrange("(a p) o -> p a o", p=128)).then_inc(wl[r], 16)

    def fgu_t(tensor):
        tensor.drain()
        for oc2 in range(32):
            tensor.wait_ge(wl[0], w0[0] + 16 * (oc2 + 1))
            tensor.wait_ge(wl[1], w0[1] + 16 * (oc2 + 1))
            for tb in range(4):
                k = oc2 * 4 + tb
                if k >= 2:
                    tensor.wait_ge(ve, v0 + (k - 1))
                for fc in range(8):
                    tensor.matmul(psum[tb % 2][:, 0:512],
                                  lhsT=wbuf_ap(sb)[:, 0, fc * 128:(fc + 1) * 128],
                                  rhs=big1[:, fc * 2048 + tb * 512:fc * 2048 + (tb + 1) * 512],
                                  start=(fc == 0), stop=(fc == 7)).then_inc(te, 1)
                for fc in range(8):
                    tensor.matmul(psum[2 + tb % 2][:, 0:512],
                                  lhsT=wbuf_ap(sb)[:, 1, fc * 128:(fc + 1) * 128],
                                  rhs=big1[:, fc * 2048 + tb * 512:fc * 2048 + (tb + 1) * 512],
                                  start=(fc == 0), stop=(fc == 7)).then_inc(te, 1)

    def fgu_a(scalar):
        for k in range(128):
            oc2, tb = divmod(k, 4)
            scalar.wait_ge(te, t0 + 16 * (k + 1))
            if k >= 2:
                scalar.wait_ge(ve, v0 + k - 1)
            slot = k % 2
            scalar.activation(out=big0[:, slot * 512:(slot + 1) * 512],
                              in_=psum[2 + tb % 2][:, 0:512], func=AF.Silu).then_inc(ae, 1)
            if k >= 1:
                j = k - 1
                oj, tj = divmod(j, 4)
                scalar.wait_ge(ve, v0 + j + 1)
                scalar.dma_start(out=dram["d_act"][oj * 128:(oj + 1) * 128,
                                                   tj * 512:(tj + 1) * 512],
                                 in_=big0[:, 1024 + (j % 2) * 512:1024 + (j % 2 + 1) * 512]
                                 ).then_inc(oslots[j % 2], 16)
        j = 127
        oj, tj = divmod(j, 4)
        scalar.wait_ge(ve, v0 + 128)
        scalar.dma_start(out=dram["d_act"][oj * 128:(oj + 1) * 128,
                                           tj * 512:(tj + 1) * 512],
                         in_=big0[:, 1024 + (j % 2) * 512:1024 + (j % 2 + 1) * 512]
                         ).then_inc(oslots[j % 2], 16)

    def fgu_v(vector):
        v = v0
        for k in range(128):
            oc2, tb = divmod(k, 4)
            slot = k % 2
            vector.wait_ge(ae, a0 + k + 1)
            if k >= 2:
                vector.wait_ge(oslots[k % 2], ob0[k % 2] + 16 * (k // 2))
            vector.tensor_tensor(out=big0[:, 1024 + slot * 512:1024 + (slot + 1) * 512],
                                 in0=big0[:, slot * 512:(slot + 1) * 512],
                                 in1=psum[tb % 2][:, 0:512], op=ALU.mult).then_inc(ve, 1); v += 1

    ctx.cnt["wl0"] += 16 * 32
    ctx.cnt["wl1"] += 16 * 32
    ctx.cnt["te"] += 16 * 128
    ctx.cnt["ae"] += 128
    ctx.cnt["ve"] += 128
    ctx.cnt["st3"] += 16 * 64
    ctx.cnt["st2"] += 16 * 64
    ctx.emit(sync=fgu_sy, tensor=fgu_t, scalar=fgu_a, vector=fgu_v)
    ctx.barrier()

    for g in range(4):
        y0 = ctx.cnt["dsy"]

        def fdl(sync, g=g):
            for oc in range(32):
                sync.dma_start(out=big1[:, oc * 512:(oc + 1) * 512],
                               in_=dram["d_act"][oc * 128:(oc + 1) * 128,
                                                 g * 512:(g + 1) * 512]).then_inc(dsy, 16)
            for tb in range(4):
                sync.dma_start(out=big0[:, 4096 + tb * 1024:4096 + (tb + 1) * 1024],
                               in_=dram["d_aout"][(g * 4 + tb) * 128:(g * 4 + tb + 1) * 128, :]
                               ).then_inc(dsy, 16)
        ctx.cnt["dsy"] += 16 * 36
        ctx.emit(sync=fdl)
        ctx.barrier()

        for q in range(4):
            def srcq(fc, ob, tb, q=q):
                return big1[:, (q * 8 + fc) * 512 + tb * 128:(q * 8 + fc) * 512 + (tb + 1) * 128]

            if q == 0:
                def ev_d0(vector, bank, ob, tb, vbase):
                    vector.tensor_copy(big0[:, tb * 1024 + ob * 512:tb * 1024 + (ob + 1) * 512],
                                       psum[bank][:, 0:512]).then_inc(ve, 1)
                proj_xs(ctx, sb, psum, dram["wd_t"], srcq, 4, 1024, None, ev_d0, 0, 1)
            elif q < 3:
                def ev_dq(vector, bank, ob, tb, vbase):
                    sl = slice(tb * 1024 + ob * 512, tb * 1024 + (ob + 1) * 512)
                    vector.tensor_tensor(out=big0[:, sl], in0=big0[:, sl],
                                         in1=psum[bank][:, 0:512], op=ALU.add).then_inc(ve, 1)
                proj_xs(ctx, sb, psum, dram["wd_t"], srcq, 4, 1024, None, ev_dq, 0, 1,
                        w_row0=q * 1024)
            else:
                def ev_d3(vector, bank, ob, tb, vbase, g=g):
                    sl = slice(tb * 1024 + ob * 512, tb * 1024 + (ob + 1) * 512)
                    vector.tensor_tensor(out=big0[:, sl], in0=big0[:, sl],
                                         in1=psum[bank][:, 0:512], op=ALU.add).then_inc(ve, 1)
                    vector.wait_ge(ve, vbase + 1)
                    vector.tensor_tensor(out=big0[:, sl], in0=big0[:, sl],
                                         in1=big0[:, 4096 + tb * 1024 + ob * 512:
                                                  4096 + tb * 1024 + (ob + 1) * 512],
                                         op=ALU.add).then_inc(ve, 1)
                    vector.wait_ge(ve, vbase + 2)
                    vector.tensor_tensor(out=xres[:, g * 4 + tb, ob * 512:(ob + 1) * 512],
                                         in0=xres[:, g * 4 + tb, ob * 512:(ob + 1) * 512],
                                         in1=big0[:, sl], op=ALU.add).then_inc(ve, 1)
                proj_xs(ctx, sb, psum, dram["wd_t"], srcq, 4, 1024, None, ev_d3, 0, 3,
                        w_row0=3 * 1024)


def wbuf_ap(sb):
    return sb["wbuf"]



def emit_retention(ctx, dram, sb, psum):
    xres, big0, big1, stat, smal, cons, stage = (
        sb["xres"], sb["big0"], sb["big1"], sb["stat"], sb["smal"], sb["cons"], sb["stage"])
    dsy, dsc, dgp = ctx.sems["dsy"], ctx.sems["dsc"], ctx.sems["dgp"]
    te, ve, ae, ge, cc = (ctx.sems[k] for k in ("te", "ve", "ae", "ge", "cc"))
    stm = [ctx.sems["st%d" % i] for i in range(4)]
    GQ = float(GAMMA) ** 128

    ln_stats(ctx, sb)
    stream_xnorm_transpose(ctx, sb, psum,
                           lambda fc, tt: big1[:, fc * 2048 + tt * 128: fc * 2048 + (tt + 1) * 128],
                           list(range(NT)))

    def xfm_w(fc, tb):
        return big1[:, fc * 2048 + tb * 512: fc * 2048 + (tb + 1) * 512]

    def xsrc(fc, ob, tb):
        return big1[:, fc * 2048 + tb * 128: fc * 2048 + (tb + 1) * 128]

    for which in ("q", "k"):
        wt = dram["retq" if which == "q" else "retk"]
        dst = dram["d_qfm" if which == "q" else "d_kfm"]
        st0 = [ctx.cnt["st%d" % i] for i in range(4)]

        def ev_qk7(scalar, bank, oc, tb, abase, dst=dst, st0=st0):
            k = oc * 4 + tb
            slot = k % 4
            scalar.wait_ge(stm[slot], st0[slot] + 16 * (k // 4))
            stg = big0[:, 1024 + slot * 512:1024 + (slot + 1) * 512]
            scalar.copy(out=stg, in_=psum[bank][:, 0:512]).then_inc(ae, 1)
            scalar.wait_ge(ae, abase + 1)
            scalar.dma_start(out=dst[oc, :, tb * 512:(tb + 1) * 512],
                             in_=stg).then_inc(stm[slot], 16)
        for i in range(4):
            ctx.cnt["st%d" % i] += 16 * 8
        proj_ws(ctx, sb, psum, wt, xfm_w, 4, 8, ev_qk7, None, 1, 0)

    st0 = [ctx.cnt["st%d" % i] for i in range(4)]

    def ev_v7(scalar, bank, ob, tb, abase, st0=st0):
        k = ob * NT + tb
        slot = k % 4
        scalar.wait_ge(stm[slot], st0[slot] + 16 * (k // 4))
        stg = big0[:, 1024 + slot * 512:1024 + (slot + 1) * 512]
        scalar.copy(out=stg, in_=psum[bank][:, 0:512]).then_inc(ae, 1)
        scalar.wait_ge(ae, abase + 1)
        scalar.dma_start(out=dram["d_vtm"][tb * 128:(tb + 1) * 128, ob * 512:(ob + 1) * 512],
                         in_=stg).then_inc(stm[slot], 16)
    for i in range(4):
        ctx.cnt["st%d" % i] += 16 * 8
    proj_xs(ctx, sb, psum, dram["retv"], xsrc, NT, 1024, ev_v7, None, 1, 0)

    v0 = ctx.cnt["ve"]

    def fz(vector):
        for i in range(8):
            vector.memset(big0[:, i * 1024:(i + 1) * 1024], 0.0).then_inc(ve, 1)
    ctx.cnt["ve"] += 8
    ctx.emit(vector=fz)
    ctx.barrier()

    for c in range(NT):
        y0 = ctx.cnt["dsy"]

        def fdma(sync, c=c):
            for fc in range(8):
                sync.dma_start(out=big1[:, fc * 128:(fc + 1) * 128],
                               in_=dram["d_qfm"][fc, :, c * 128:(c + 1) * 128]).then_inc(dsy, 16)
                sync.dma_start(out=big1[:, 1024 + fc * 128:1024 + (fc + 1) * 128],
                               in_=dram["d_kfm"][fc, :, c * 128:(c + 1) * 128]).then_inc(dsy, 16)
            sync.dma_start(out=big1[:, 2048:3072],
                           in_=dram["d_vtm"][c * 128:(c + 1) * 128, :]).then_inc(dsy, 16)
        ctx.cnt["dsy"] += 16 * 17
        ctx.emit(sync=fdma)
        ctx.barrier()

        t0, v0, a0 = ctx.cnt["te"], ctx.cnt["ve"], ctx.cnt["ae"]

        def fb1t(tensor, t0=t0, v0=v0, a0=a0):
            tensor.drain()
            for fc in range(8):
                tensor.matmul(psum[0][:, 0:128], lhsT=big1[:, 1024 + fc * 128:1024 + (fc + 1) * 128],
                              rhs=big1[:, fc * 128:(fc + 1) * 128],
                              start=(fc == 0), stop=(fc == 7)).then_inc(te, 1)
            tensor.drain()
            for fc in range(8):
                if fc >= 2:
                    j = fc - 2
                    if j % 2 == 0:
                        tensor.wait_ge(ve, v0 + j // 2 + 1)
                    else:
                        tensor.wait_ge(ae, a0 + (j + 1) // 2)
                tensor.transpose(out=psum[4 + fc % 2][:, 0:128],
                                 in_=big1[:, 1024 + fc * 128:1024 + (fc + 1) * 128],
                                 identity=cons[:, 0:128]).then_inc(te, 1)

        def fb1v(vector, t0=t0):
            for fc in range(0, 8, 2):
                vector.wait_ge(te, t0 + 8 + fc + 1)
                vector.tensor_scalar(out=big1[:, 3072 + fc * 128:3072 + (fc + 1) * 128],
                                     in0=psum[4 + fc % 2][:, 0:128],
                                     scalar1=smal[:, 5:6], scalar2=None,
                                     op0=ALU.mult).then_inc(ve, 1)

        def fb1a(scalar, t0=t0):
            for fc in range(1, 8, 2):
                scalar.wait_ge(te, t0 + 8 + fc + 1)
                scalar.activation(out=big1[:, 3072 + fc * 128:3072 + (fc + 1) * 128],
                                  in_=psum[4 + fc % 2][:, 0:128], func=AF.Copy,
                                  scale=smal[:, 5:6]).then_inc(ae, 1)
        ctx.cnt["te"] += 16
        ctx.cnt["ve"] += 4
        ctx.cnt["ae"] += 4
        ctx.emit(tensor=fb1t, vector=fb1v, scalar=fb1a)
        ctx.barrier()

        t0, v0 = ctx.cnt["te"], ctx.cnt["ve"]

        def fb2v(vector):
            vector.tensor_tensor(out=big1[:, 4096:4224], in0=psum[0][:, 0:128],
                                 in1=cons[:, 384:512], op=ALU.mult).then_inc(ve, 1)

        def fb2t(tensor, t0=t0, v0=v0):
            tensor.wait_ge(ve, v0 + 1)
            for ob in range(2):
                tensor.matmul(psum[1 + ob][:, 0:512], lhsT=big1[:, 4096:4224],
                              rhs=big1[:, 2048 + ob * 512:2048 + (ob + 1) * 512],
                              start=True, stop=True).then_inc(te, 1)
            for ob in range(2):
                bank = 3 if ob == 0 else 6
                for fc in range(8):
                    tensor.matmul(psum[bank][:, 0:512], lhsT=big1[:, fc * 128:(fc + 1) * 128],
                                  rhs=big0[:, fc * 1024 + ob * 512:fc * 1024 + (ob + 1) * 512],
                                  start=(fc == 0), stop=(fc == 7)).then_inc(te, 1)
        ctx.cnt["ve"] += 1
        ctx.cnt["te"] += 18
        ctx.emit(vector=fb2v, tensor=fb2t)
        ctx.barrier()

        v0 = ctx.cnt["ve"]

        def fb3v(vector, c=c, v0=v0):
            v = v0
            for ob in range(2):
                bank = 3 if ob == 0 else 6
                vector.tensor_copy(stage[:, 1, ob * 512:(ob + 1) * 512],
                                   psum[1 + ob][:, 0:512]).then_inc(ve, 1); v += 1
                vector.wait_ge(ve, v)
                vector.scalar_tensor_tensor(
                    out=stage[:, 0, ob * 512:(ob + 1) * 512], in0=psum[bank][:, 0:512],
                    scalar=smal[:, 4:5], in1=stage[:, 1, ob * 512:(ob + 1) * 512],
                    op0=ALU.mult, op1=ALU.add).then_inc(ve, 1); v += 1
                vector.wait_ge(ve, v)
                vector.tensor_tensor(out=xres[:, c, ob * 512:(ob + 1) * 512],
                                     in0=xres[:, c, ob * 512:(ob + 1) * 512],
                                     in1=stage[:, 0, ob * 512:(ob + 1) * 512],
                                     op=ALU.add).then_inc(ve, 1); v += 1
        ctx.cnt["ve"] += 6
        ctx.emit(vector=fb3v)
        ctx.barrier()

        t0, v0 = ctx.cnt["te"], ctx.cnt["ve"]
        sbanks = (0, 4, 5, 7)

        def fb4t(tensor, t0=t0, v0=v0):
            for k in range(16):
                dkc, ob = divmod(k, 2)
                if k >= 4:
                    tensor.wait_ge(ve, v0 + k - 3)
                tensor.matmul(psum[sbanks[k % 4]][:, 0:512],
                              lhsT=big1[:, 3072 + dkc * 128:3072 + (dkc + 1) * 128],
                              rhs=big1[:, 2048 + ob * 512:2048 + (ob + 1) * 512],
                              start=True, stop=True).then_inc(te, 1)

        def fb4v(vector, t0=t0):
            for k in range(16):
                dkc, ob = divmod(k, 2)
                vector.wait_ge(te, t0 + k + 1)
                sl = slice(dkc * 1024 + ob * 512, dkc * 1024 + (ob + 1) * 512)
                vector.scalar_tensor_tensor(out=big0[:, sl], in0=big0[:, sl],
                                            scalar=GQ, in1=psum[sbanks[k % 4]][:, 0:512],
                                            op0=ALU.mult, op1=ALU.add).then_inc(ve, 1)
        ctx.cnt["te"] += 16
        ctx.cnt["ve"] += 16
        ctx.emit(tensor=fb4t, vector=fb4v)
        ctx.barrier()

    y0 = ctx.cnt["dsy"]

    def fexp(sync):
        for fc in range(8):
            sync.dma_start(out=dram["ag_ret_i"][fc * 128:(fc + 1) * 128, :],
                           in_=big0[:, fc * 1024:(fc + 1) * 1024]).then_inc(dsy, 16)
        for fc in range(8):
            sync.dma_start(out=sb["sidx"][:, 24 + fc:25 + fc],
                           in_=dram["retidx"][fc * 128:(fc + 1) * 128, :]).then_inc(dsy, 16)
        sync.dma_start(out=smal[:, 140:156], in_=dram["gtcol"][:, :]).then_inc(dsy, 16)
    ctx.cnt["dsy"] += 16 * 17
    ctx.emit(sync=fexp)
    ctx.barrier()

    p0, c0 = ctx.cnt["dgp"], ctx.cnt["cc"]

    def fag(gpsimd, p0=p0, c0=c0):
        gpsimd.collective_compute(
            "AllGather", ALU.bypass, replica_groups=[list(range(NCORES))],
            ins=[dram["ag_ret_i"].opt()], outs=[dram["ag_ret_o"].opt()]).then_inc(cc, 1)
        gpsimd.wait_ge(cc, c0 + 1)
        n = 0
        for fc in range(8):
            gpsimd.indirect_dma_start(
                out=big0[:, fc * 1024:(fc + 1) * 1024], out_offset=None,
                in_=dram["ag_ret_o"][:, :],
                in_offset=bass.IndirectOffsetOnAxis(ap=sb["sidx"][:, 24 + fc:25 + fc], axis=0),
            ).then_inc(dgp, 16)
            n += 1
            gpsimd.wait_ge(dgp, p0 + 16 * n)
    ctx.cnt["cc"] += 1
    ctx.cnt["dgp"] += 16 * 8
    ctx.emit(gpsimd=fag)
    ctx.barrier()

    for tt in range(NT):
        y0 = ctx.cnt["dsy"]

        def fq(sync, tt=tt):
            for fc in range(8):
                sync.dma_start(out=big1[:, fc * 128:(fc + 1) * 128],
                               in_=dram["d_qfm"][fc, :, tt * 128:(tt + 1) * 128]).then_inc(dsy, 16)
        ctx.cnt["dsy"] += 16 * 8
        ctx.emit(sync=fq)
        ctx.barrier()

        t0, v0 = ctx.cnt["te"], ctx.cnt["ve"]

        def fct(tensor, t0=t0):
            for ob in range(2):
                for fc in range(8):
                    tensor.matmul(psum[ob][:, 0:512], lhsT=big1[:, fc * 128:(fc + 1) * 128],
                                  rhs=big0[:, fc * 1024 + ob * 512:fc * 1024 + (ob + 1) * 512],
                                  start=(fc == 0), stop=(fc == 7)).then_inc(te, 1)

        def fcv(vector, tt=tt, t0=t0, v0=v0):
            v = v0
            for ob in range(2):
                vector.wait_ge(te, t0 + 8 * (ob + 1))
                vector.tensor_scalar(out=stage[:, 0, ob * 512:(ob + 1) * 512],
                                     in0=psum[ob][:, 0:512],
                                     scalar1=smal[:, 140 + tt:141 + tt], scalar2=None,
                                     op0=ALU.mult).then_inc(ve, 1); v += 1
                vector.wait_ge(ve, v)
                vector.tensor_tensor(out=xres[:, tt, ob * 512:(ob + 1) * 512],
                                     in0=xres[:, tt, ob * 512:(ob + 1) * 512],
                                     in1=stage[:, 0, ob * 512:(ob + 1) * 512],
                                     op=ALU.add).then_inc(ve, 1); v += 1
        ctx.cnt["te"] += 16
        ctx.cnt["ve"] += 4
        ctx.emit(tensor=fct, vector=fcv)
        ctx.barrier()



def emit_mha(ctx, dram, sb, psum, rope=False):
    xres, big0, big1, stat, smal, cons, sidx, stage = (
        sb["xres"], sb["big0"], sb["big1"], sb["stat"], sb["smal"], sb["cons"],
        sb["sidx"], sb["stage"])
    dsy, dsc, dgp = ctx.sems["dsy"], ctx.sems["dsc"], ctx.sems["dgp"]
    te, ve, ae, ge, cc = (ctx.sems[k] for k in ("te", "ve", "ae", "ge", "cc"))
    stm = [ctx.sems["st%d" % i] for i in range(4)]
    agi = dram["ag_kv5_i"]
    ago = dram["ag_kv5_o"]
    NKT = 33 if rope else 32
    SW = NKT * 128

    ln_stats(ctx, sb)

    stream_xnorm_transpose(ctx, sb, psum,
                           lambda fc, tt: big1[:, fc * 2048 + tt * 128: fc * 2048 + (tt + 1) * 128],
                           list(range(NT)))

    def xsrc(fc, ob, tb):
        return big1[:, fc * 2048 + tb * 128: fc * 2048 + (tb + 1) * 128]

    def xfm_w(fc, tb):
        return big1[:, fc * 2048 + tb * 512: fc * 2048 + (tb + 1) * 512]

    kv_view = agi.rearrange("(f j) c -> f j c", j=2)

    for which in ("q", "k"):
        if rope:
            wt = dram["wq_t" if which == "q" else "wk_t"]
            bias_col = None
        else:
            wt = dram["ww_t"][:, 0:1024] if which == "q" else dram["ww_t"][:, 1024:2048]
        st0 = [ctx.cnt["st%d" % i] for i in range(4)]
        biasoff = 0 if which == "q" else 8

        def ev_qk(scalar, bank, oc, tb2, abase, which=which, st0=st0, biasoff=biasoff):
            k = oc * 4 + tb2
            slot = k % 4
            scalar.wait_ge(stm[slot], st0[slot] + 16 * (k // 4))
            if rope:
                bcol = sb["qkbias"][:, (0 if which == "q" else 1) * 8 + oc:
                                    (0 if which == "q" else 1) * 8 + oc + 1]
            else:
                bcol = sb["qkbias"][:, biasoff + oc:biasoff + oc + 1]
            stg = big0[:, 1024 + slot * 512:1024 + (slot + 1) * 512]
            scalar.activation(out=stg, in_=psum[bank][:, :], func=AF.Identity,
                              bias=bcol).then_inc(ae, 1)
            scalar.wait_ge(ae, abase + 1)
            if which == "q":
                dst = dram["d_qfm"][oc, :, tb2 * 512:(tb2 + 1) * 512]
            else:
                j, cb = divmod(tb2 * 512, 1024)
                dst = kv_view[oc * 128:(oc + 1) * 128, j, cb:cb + 512]
            scalar.dma_start(out=dst, in_=stg).then_inc(stm[slot], 16)

        if which == "q":
            y0b = ctx.cnt["dsy"]

            def fbl(sync):
                if rope:
                    sync.dma_start(out=sb["qkbias"][:, 0:8], in_=dram["bq_fm"][:, :]).then_inc(dsy, 16)
                    sync.dma_start(out=sb["qkbias"][:, 8:16], in_=dram["bk_fm"][:, :]).then_inc(dsy, 16)
                else:
                    sync.dma_start(out=sb["qkbias"][:, 0:16], in_=dram["wb_fm"][:, 0:16]).then_inc(dsy, 16)
            ctx.cnt["dsy"] += 32 if rope else 16
            ctx.emit(sync=fbl)
            ctx.barrier()

        for i in range(4):
            ctx.cnt["st%d" % i] += 16 * 8
        proj_ws(ctx, sb, psum, wt, xfm_w, 4, 8, ev_qk, None, 1, 0)

    st0 = [ctx.cnt["st%d" % i] for i in range(4)]
    y0 = ctx.cnt["dsc"]

    def fvb(scalar):
        src_ap = dram["bv_b"][:, :] if rope else dram["wb_b"][:, 2048:3072]
        scalar.dma_start(out=big0[:, 0:1024], in_=src_ap).then_inc(dsc, 16)
    ctx.cnt["dsc"] += 16
    ctx.emit(scalar=fvb)
    ctx.barrier()

    def ev_v_v(vector, bank, ob, tb, vbase):
        k = ob * NT + tb
        slot = k % 4
        vector.wait_ge(stm[slot], st0[slot] + 16 * (k // 4))
        stg = big0[:, 1024 + slot * 512:1024 + (slot + 1) * 512]
        vector.tensor_tensor(out=stg, in0=psum[bank][:, :],
                             in1=big0[:, ob * 512:(ob + 1) * 512], op=ALU.add).then_inc(ve, 1)

    def ev_v_s(scalar, bank, ob, tb, abase):
        k = ob * NT + tb
        slot = k % 4
        stg = big0[:, 1024 + slot * 512:1024 + (slot + 1) * 512]
        dst = agi[2048 + tb * 128:2048 + (tb + 1) * 128, ob * 512:(ob + 1) * 512]
        scalar.dma_start(out=dst, in_=stg).then_inc(stm[slot], 16)
        scalar.nop().then_inc(ae, 1)

    vwt = dram["wv_t"] if rope else dram["ww_t"][:, 2048:3072]
    for i in range(4):
        ctx.cnt["st%d" % i] += 16 * 8
    proj_xs(ctx, sb, psum, vwt, xsrc, NT, 1024, ev_v_s, ev_v_v, 1, 1, order="vs")

    if rope:
        y0 = ctx.cnt["dsy"]

        def fcs(sync):
            sync.dma_start(out=big1[:, 0:2048], in_=dram["cosfm"][:, :]).then_inc(dsy, 16)
            sync.dma_start(out=big1[:, 2048:4096], in_=dram["sinfm"][:, :]).then_inc(dsy, 16)
            sync.dma_start(out=big1[:, 4096:4224], in_=dram["ropePT"][:, :]).then_inc(dsy, 16)
        ctx.cnt["dsy"] += 48
        ctx.emit(sync=fcs)
        ctx.barrier()

        for which in ("q", "k"):
            t0, v0, a0 = ctx.cnt["te"], ctx.cnt["ve"], ctx.cnt["ae"]
            st0 = [ctx.cnt["st0"], ctx.cnt["st1"]]

            def blk_ap(k):
                return big1[:, 4352 + (k % 2) * 512:4352 + (k % 2 + 1) * 512]

            def dram_ap(k, which=which):
                oc, tb = divmod(k, 4)
                if which == "q":
                    return dram["d_qfm"][oc, :, tb * 512:(tb + 1) * 512]
                j, cb = divmod(tb * 512, 1024)
                return kv_view[oc * 128:(oc + 1) * 128, j, cb:cb + 512]

            def frs(sync, st0=st0):
                for k in range(32):
                    s = k % 2
                    if k >= 2:
                        sync.wait_ge(stm[s], st0[s] + 32 * (k // 2))
                    sync.dma_start(out=blk_ap(k), in_=dram_ap(k)).then_inc(stm[s], 16)

            def frt(tensor, t0=t0, v0=v0, st0=st0):
                for k in range(32):
                    s = k % 2
                    tensor.wait_ge(stm[s], st0[s] + 32 * (k // 2) + 16)
                    if k >= 4:
                        tensor.wait_ge(ve, v0 + 3 * (k - 4) + 2)
                    tensor.matmul(psum[k % 4][:, 0:512], lhsT=big1[:, 4096:4224],
                                  rhs=blk_ap(k), start=True, stop=True).then_inc(te, 1)

            def frv(vector, t0=t0, v0=v0):
                v = v0
                for k in range(32):
                    s = k % 2
                    tb = k % 4
                    cosb = big1[:, tb * 512:(tb + 1) * 512]
                    sinb = big1[:, 2048 + tb * 512:2048 + (tb + 1) * 512]
                    vector.wait_ge(te, t0 + k + 1)
                    vector.tensor_tensor(out=blk_ap(k), in0=blk_ap(k), in1=cosb,
                                         op=ALU.mult).then_inc(ve, 1); v += 1
                    vector.tensor_tensor(out=stage[:, s, 0:512], in0=psum[k % 4][:, 0:512],
                                         in1=sinb, op=ALU.mult).then_inc(ve, 1); v += 1
                    vector.wait_ge(ve, v)
                    vector.tensor_tensor(out=blk_ap(k), in0=blk_ap(k),
                                         in1=stage[:, s, 0:512], op=ALU.add).then_inc(ve, 1); v += 1

            def fra(scalar, v0=v0, st0=st0):
                for k in range(32):
                    s = k % 2
                    scalar.wait_ge(ve, v0 + 3 * k + 3)
                    scalar.dma_start(out=dram_ap(k), in_=blk_ap(k)).then_inc(stm[s], 16)
            ctx.cnt["te"] += 32
            ctx.cnt["ve"] += 96
            ctx.cnt["st0"] += 32 * 16
            ctx.cnt["st1"] += 32 * 16
            ctx.emit(sync=frs, tensor=frt, vector=frv, scalar=fra)
            ctx.barrier()

    p0, c0 = ctx.cnt["dgp"], ctx.cnt["cc"]

    def fag(gpsimd):
        gpsimd.collective_compute(
            "AllGather", ALU.bypass, replica_groups=[list(range(NCORES))],
            ins=[agi.opt()], outs=[ago.opt()]).then_inc(cc, 1)
    ctx.cnt["cc"] += 1
    ctx.emit(gpsimd=fag)
    ctx.barrier()

    y0 = ctx.cnt["dsy"]
    vtab = "v5idx"
    ktab = "k5idx"

    def fvidx(sync):
        for k32 in range(32):
            sync.dma_start(out=sidx[:, 16 + k32:17 + k32],
                           in_=dram[vtab][k32 * 128:(k32 + 1) * 128, :]).then_inc(dsy, 16)
        if rope:
            sync.dma_start(out=big0[:, 5888:6400], in_=dram["sadd"][:, :]).then_inc(dsy, 16)
    ctx.cnt["dsy"] += 16 * (33 if rope else 32)
    ctx.emit(sync=fvidx)
    ctx.barrier()

    for hg in range(4):
        y0 = ctx.cnt["dsy"]
        p0 = ctx.cnt["dgp"]

        def fkidx(sync, hg=hg):
            for i in range(8):
                sync.dma_start(out=sidx[:, 8 + i:9 + i],
                               in_=dram[ktab][(hg * 8 + i) * 128:(hg * 8 + i + 1) * 128, :]
                               ).then_inc(dsy, 16)
        ctx.cnt["dsy"] += 16 * 8
        ctx.emit(sync=fkidx)
        ctx.barrier()

        def fgat(gpsimd, hg=hg, p0=p0):
            n = 0
            for c2 in range(2):
                for half in range(2):
                    for j in range(2):
                        i = c2 * 4 + half * 2 + j
                        gpsimd.indirect_dma_start(
                            out=big1[:, c2 * 4096 + half * 2048 + j * 1024:
                                     c2 * 4096 + half * 2048 + (j + 1) * 1024],
                            out_offset=None, in_=ago[:, :],
                            in_offset=bass.IndirectOffsetOnAxis(ap=sidx[:, 8 + i:9 + i], axis=0),
                        ).then_inc(dgp, 16)
                        n += 1
                        gpsimd.wait_ge(dgp, p0 + 16 * n)
            for k32 in range(32):
                gpsimd.indirect_dma_start(
                    out=big1[:, 8192 + k32 * 256:8192 + (k32 + 1) * 256],
                    out_offset=None, in_=ago[:, :],
                    in_offset=bass.IndirectOffsetOnAxis(ap=sidx[:, 16 + k32:17 + k32], axis=0),
                    element_offset=256 * hg,
                ).then_inc(dgp, 16)
                n += 1
                gpsimd.wait_ge(dgp, p0 + 16 * n)
        ctx.cnt["dgp"] += 16 * 40
        ctx.emit(gpsimd=fgat)
        ctx.barrier()

        emit_attn_core(ctx, dram, sb, psum, hg, rope)

    stream_load_transpose(ctx, sb, psum,
                          lambda tt: dram["d_aout"][tt * 128:(tt + 1) * 128, :],
                          lambda fc, tt: big1[:, fc * 2048 + tt * 128: fc * 2048 + (tt + 1) * 128],
                          NT)
    y0 = ctx.cnt["dsc"]

    def fob(scalar):
        scalar.dma_start(out=big0[:, 0:1024],
                         in_=dram["bo_b" if rope else "outb_b"][:, :]).then_inc(dsc, 16)
    ctx.cnt["dsc"] += 16
    ctx.emit(scalar=fob)
    ctx.barrier()

    def ev_o_v(vector, bank, ob, tb, vbase):
        vector.tensor_tensor(out=psum[bank][:, :], in0=psum[bank][:, :],
                             in1=big0[:, ob * 512:(ob + 1) * 512], op=ALU.add).then_inc(ve, 1)
        vector.wait_ge(ve, vbase + 1)
        vector.tensor_tensor(out=xres[:, tb, ob * 512:(ob + 1) * 512],
                             in0=xres[:, tb, ob * 512:(ob + 1) * 512],
                             in1=psum[bank][:, :], op=ALU.add).then_inc(ve, 1)
    owt = dram["wo_t"] if rope else dram["outw_t"]
    proj_xs(ctx, sb, psum, owt, xsrc, NT, 1024, None, ev_o_v, 0, 2)


def emit_attn_core(ctx, dram, sb, psum, hg, rope):
    xres, big0, big1, stat, smal, cons, sidx = (
        sb["xres"], sb["big0"], sb["big1"], sb["stat"], sb["smal"], sb["cons"], sb["sidx"])
    dsy, dsc = ctx.sems["dsy"], ctx.sems["dsc"]
    te, ve, ae = ctx.sems["te"], ctx.sems["ve"], ctx.sems["ae"]
    stm = [ctx.sems["st%d" % i] for i in range(4)]
    NKT = 33 if rope else 32
    SW = NKT * 128
    NB = (SW + 511) // 512
    S_off, Q_off, AO_off = 0, 4352, 4864
    DEN_off, SAD_off = 5632, 5888

    def fz(vector):
        vector.memset(big0[:, Q_off:Q_off + 512], 0.0).then_inc(ve, 1)
    ctx.cnt["ve"] += 1
    ctx.emit(vector=fz)
    ctx.barrier()

    agi = dram["ag_kv5_i"] if rope else None
    kv_view = agi.rearrange("(f j) c -> f j c", j=2) if rope else None
    SC = 17 if rope else 8
    VB = 14 if rope else 11
    K33_off, V33_off = 5120, 5376

    for s in range(NT):
        y0 = ctx.cnt["dsc"]

        def fql(scalar, s=s, hg=hg, tq0=ctx.cnt["te"]):
            scalar.wait_ge(te, tq0)
            for ho in range(4):
                c2, rsel = divmod(ho, 2)
                scalar.dma_start(
                    out=big0[64 * rsel:64 * rsel + 64,
                             Q_off + ho * 128:Q_off + (ho + 1) * 128],
                    in_=dram["d_qfm"][2 * hg + c2, 64 * rsel:64 * rsel + 64,
                                      s * 128:(s + 1) * 128]).then_inc(dsc, 16)
            if rope:
                for c2 in range(2):
                    scalar.dma_start(
                        out=big0[:, K33_off + c2 * 128:K33_off + (c2 + 1) * 128],
                        in_=kv_view[(2 * hg + c2) * 128:(2 * hg + c2 + 1) * 128,
                                    s // 8, (s % 8) * 128:(s % 8) * 128 + 128]
                    ).then_inc(dsc, 16)
                scalar.dma_start(
                    out=big0[:, V33_off:V33_off + 256],
                    in_=agi[2048 + s * 128:2048 + (s + 1) * 128,
                            hg * 256:(hg + 1) * 256]).then_inc(dsc, 16)
        ctx.cnt["dsc"] += 112 if rope else 64
        ctx.emit(scalar=fql)
        ctx.barrier()

        for ho in range(4):
            c2, rsel = divmod(ho, 2)
            t0, v0, a0 = ctx.cnt["te"], ctx.cnt["ve"], ctx.cnt["ae"]
            st0 = [ctx.cnt["st%d" % i] for i in range(4)]
            qap64 = big0[64 * rsel:64 * rsel + 64,
                         Q_off + ho * 128:Q_off + (ho + 1) * 128]

            def kT_lhs(kt, c2=c2, rsel=rsel):
                if kt == 32:
                    return big0[64 * rsel:64 * rsel + 64,
                                K33_off + c2 * 128:K33_off + (c2 + 1) * 128]
                return big1[64 * rsel:64 * rsel + 64,
                            c2 * 4096 + kt * 128:c2 * 4096 + (kt + 1) * 128]

            def pv_rhs(j, ho=ho):
                if j == 32:
                    return big0[:, V33_off + 64 * ho:V33_off + 64 * (ho + 1)]
                return big1[:, 8192 + j * 256 + 64 * ho:8192 + j * 256 + 64 * (ho + 1)]

            def evac_pos(j):
                return 1 if j == 0 else 2 * j

            def ft(tensor, s=s, ho=ho, t0=t0, v0=v0, a0=a0):
                tensor.wait_ge(ve, v0)
                tensor.wait_ge(ae, a0)
                for kt in range(NKT):
                    if kt >= 4:
                        tensor.wait_ge(ve, v0 + evac_pos(kt - 4))
                    tensor.matmul(psum[kt % 4][:, 0:128], lhsT=kT_lhs(kt), rhs=qap64,
                                  start=True, stop=True).then_inc(te, 1)
                    if kt >= 1:
                        j = kt - 1
                        tensor.wait_ge(ae, a0 + j + 1)
                        tensor.matmul(psum[7][:, 0:64],
                                      lhsT=big0[:, S_off + j * 128:S_off + (j + 1) * 128],
                                      rhs=pv_rhs(j),
                                      start=(j == 0), stop=False).then_inc(te, 1)
                j = NKT - 1
                tensor.wait_ge(ae, a0 + j + 1)
                tensor.matmul(psum[7][:, 0:64],
                              lhsT=big0[:, S_off + j * 128:S_off + (j + 1) * 128],
                              rhs=pv_rhs(j),
                              start=False, stop=True).then_inc(te, 1)
                tensor.wait_ge(ve, v0 + 2 * NKT)
                tensor.matmul(psum[6][:, 0:1], lhsT=big0[:, DEN_off:DEN_off + 128],
                              rhs=smal[:, 6:7], start=True, stop=True).then_inc(te, 1)

            def fv(vector, s=s, ho=ho, t0=t0, v0=v0, a0=a0, st0=st0):
                v = v0
                for kt in range(NKT):
                    vector.wait_ge(te, t0 + (1 if kt == 0 else 2 * kt))
                    dst = big0[:, S_off + kt * 128:S_off + (kt + 1) * 128]
                    if rope and kt == 32:
                        vector.scalar_tensor_tensor(
                            out=dst, in0=psum[kt % 4][:, 0:128], scalar=0.125,
                            in1=cons[:, 640:768], op0=ALU.mult,
                            op1=ALU.add).then_inc(ve, 1); v += 1
                    elif rope:
                        vector.tensor_scalar(
                            out=dst, in0=psum[kt % 4][:, 0:128], scalar1=0.125,
                            scalar2=big0[:, SAD_off + s * 32 + kt:SAD_off + s * 32 + kt + 1],
                            op0=ALU.mult, op1=ALU.add).then_inc(ve, 1); v += 1
                    else:
                        vector.tensor_scalar(
                            out=dst, in0=psum[kt % 4][:, 0:128], scalar1=0.125,
                            scalar2=None, op0=ALU.mult).then_inc(ve, 1); v += 1
                    if kt >= 1:
                        j = kt - 1
                        vector.wait_ge(ae, a0 + j + 1)
                        vector.wait_ge(ve, v)
                        if j == 0:
                            vector.tensor_copy(big0[:, DEN_off:DEN_off + 128],
                                               big0[:, S_off:S_off + 128]).then_inc(ve, 1); v += 1
                        else:
                            vector.tensor_tensor(
                                out=big0[:, DEN_off:DEN_off + 128],
                                in0=big0[:, DEN_off:DEN_off + 128],
                                in1=big0[:, S_off + j * 128:S_off + (j + 1) * 128],
                                op=ALU.add).then_inc(ve, 1); v += 1
                j = NKT - 1
                vector.wait_ge(ae, a0 + j + 1)
                vector.wait_ge(ve, v)
                vector.tensor_tensor(out=big0[:, DEN_off:DEN_off + 128],
                                     in0=big0[:, DEN_off:DEN_off + 128],
                                     in1=big0[:, S_off + j * 128:S_off + (j + 1) * 128],
                                     op=ALU.add).then_inc(ve, 1); v += 1
                vector.wait_ge(te, t0 + 2 * NKT + 1)
                vector.tensor_copy(smal[:, 218:219], psum[6][:, 0:1]).then_inc(ve, 1); v += 1
                vector.wait_ge(ve, v)
                vector.reciprocal(out=smal[:, 219:220], in_=smal[:, 218:219]).then_inc(ve, 1); v += 1
                vector.wait_ge(te, t0 + 2 * NKT)
                slot = (s * 4 + ho) % 2
                vector.wait_ge(stm[slot], st0[slot])
                vector.wait_ge(ve, v)
                vector.tensor_scalar(out=big0[:, AO_off + slot * 64:AO_off + (slot + 1) * 64],
                                     in0=psum[7][:, 0:64], scalar1=smal[:, 219:220],
                                     scalar2=None, op0=ALU.mult).then_inc(ve, 1); v += 1

            def fa(scalar, s=s, ho=ho, t0=t0, v0=v0, a0=a0, st0=st0):
                for kt in range(NKT):
                    scalar.wait_ge(ve, v0 + evac_pos(kt))
                    scalar.activation(out=big0[:, S_off + kt * 128:S_off + (kt + 1) * 128],
                                      in_=big0[:, S_off + kt * 128:S_off + (kt + 1) * 128],
                                      func=AF.Exp).then_inc(ae, 1)
                slot = (s * 4 + ho) % 2
                scalar.wait_ge(ve, v0 + 2 * NKT + 3)
                scalar.dma_start(out=dram["d_aout"][s * 128:(s + 1) * 128,
                                                    hg * 256 + ho * 64:hg * 256 + (ho + 1) * 64],
                                 in_=big0[:, AO_off + slot * 64:AO_off + (slot + 1) * 64]
                                 ).then_inc(stm[slot], 16)

            ctx.cnt["te"] += 2 * NKT + 1
            ctx.cnt["ve"] += 2 * NKT + 3
            ctx.cnt["ae"] += NKT
            ctx.cnt["st%d" % ((s * 4 + ho) % 2)] += 16
            ctx.emit(tensor=ft, scalar=fa, vector=fv)
        ctx.barrier()


def emit_whole_program(ctx, dram, sb, psum, y_out):
    ctx.dram = dram
    load_consts(ctx, dram, sb)
    load_x0(ctx, dram, sb)

    for s in BUILD_SUBLAYERS:
        if s == 1:
            emit_mamba(ctx, dram, sb, psum)
        elif s == 2:
            emit_quant_linear(ctx, dram, sb, psum, use_idx=1)
        elif s == 4:
            emit_quant_linear(ctx, dram, sb, psum, use_idx=2)
        elif s == 3:
            emit_mha(ctx, dram, sb, psum, rope=True)
        elif s == 5:
            emit_mha(ctx, dram, sb, psum, rope=False)
            if DEBUG_TAP == "multi":
                dsy_ = ctx.sems["dsy"]

                def ftap2(sync):
                    for tt in range(NT):
                        sync.dma_start(out=dram["dbgy"][tt * 128:(tt + 1) * 128, :],
                                       in_=dram["d_aout"][tt * 128:(tt + 1) * 128, :]).then_inc(dsy_, 16)
                    for tt in range(NT):
                        sync.dma_start(out=dram["dbgy"][NTOK + tt * 128:NTOK + (tt + 1) * 128, :],
                                       in_=sb["big1"][:, tt * 1024:(tt + 1) * 1024]).then_inc(dsy_, 16)
                ctx.cnt["dsy"] += 32 * 16
                ctx.emit(sync=ftap2)
                ctx.barrier()
            if DEBUG_TAP in ("aout", "qfm", "agkv0", "agkv1"):
                dsy_ = ctx.sems["dsy"]

                def ftap(sync):
                    if DEBUG_TAP == "aout":
                        for tt in range(NT):
                            sync.dma_start(out=y_out[tt * 128:(tt + 1) * 128, :],
                                           in_=dram["d_aout"][tt * 128:(tt + 1) * 128, :]).then_inc(dsy_, 16)
                            ctx.cnt["dsy"] += 16
                    elif DEBUG_TAP == "qfm":
                        for a in range(8):
                            sync.dma_start(out=y_out[a * 256:a * 256 + 128, 0:1024],
                                           in_=dram["d_qfm"][a, :, 0:1024]).then_inc(dsy_, 16)
                            sync.dma_start(out=y_out[a * 256 + 128:a * 256 + 256, 0:1024],
                                           in_=dram["d_qfm"][a, :, 1024:2048]).then_inc(dsy_, 16)
                            ctx.cnt["dsy"] += 32
                    else:
                        r0 = 0 if DEBUG_TAP == "agkv0" else 2048
                        for tt in range(NT):
                            sync.dma_start(out=y_out[tt * 128:(tt + 1) * 128, :],
                                           in_=dram["ag_kv5_i"][r0 + tt * 128:r0 + (tt + 1) * 128, :]).then_inc(dsy_, 16)
                            ctx.cnt["dsy"] += 16
                ctx.emit(sync=ftap)
                ctx.barrier()
        elif s == 6:
            emit_rwkv(ctx, dram, sb, psum, use_idx=1)
        elif s == 7:
            emit_retention(ctx, dram, sb, psum)
        elif s == 8:
            emit_rwkv(ctx, dram, sb, psum, use_idx=2)

    xres = sb["xres"]
    dsy = ctx.sems["dsy"]
    y0 = ctx.cnt["dsy"]
    if DEBUG_TAP not in (None, "multi", "big0", "big1", "smal", "stat"):
        def fnone(sync):
            sync.wait_ge(dsy, y0)
        ctx.emit(sync=fnone)
        ctx.barrier()
        return

    def fout(sync):
        if DEBUG_TAP in (None, "multi"):
            for tt in range(NT):
                sync.dma_start(out=y_out[tt * 128:(tt + 1) * 128, :],
                               in_=xres[:, tt, :]).then_inc(dsy, 16)
        elif DEBUG_TAP in ("big0", "big1"):
            buf = sb[DEBUG_TAP]
            for tt in range(NT):
                sync.dma_start(out=y_out[tt * 128:(tt + 1) * 128, :],
                               in_=buf[:, tt * 1024:(tt + 1) * 1024]).then_inc(dsy, 16)
        elif DEBUG_TAP == "smal":
            sync.dma_start(out=y_out[0:128, 0:256], in_=sb["smal"][:, :]).then_inc(dsy, 16)
            for tt in range(1, NT):
                sync.dma_start(out=y_out[tt * 128:(tt + 1) * 128, :],
                               in_=xres[:, tt, :]).then_inc(dsy, 16)
        elif DEBUG_TAP == "stat":
            sync.dma_start(out=y_out[0:128, 0:NT * 8],
                           in_=sb["stat"][:, :, :].rearrange("p a b -> p (a b)")).then_inc(dsy, 16)
            for tt in range(1, NT):
                sync.dma_start(out=y_out[tt * 128:(tt + 1) * 128, :],
                               in_=xres[:, tt, :]).then_inc(dsy, 16)
        sync.wait_ge(dsy, y0 + 16 * NT)
    ctx.cnt["dsy"] += 16 * (NT if DEBUG_TAP is None or DEBUG_TAP in ("big0", "big1") else NT)
    ctx.emit(sync=fout)
    ctx.barrier()



_cache = {}


def _stage_core_inputs(inputs, c):
    import ml_dtypes
    b, h = divmod(c, 2)

    def t(a):
        return np.ascontiguousarray(np.asarray(a, np.float32).T)

    def bc(v):
        v = np.asarray(v, np.float32)
        return np.ascontiguousarray(np.broadcast_to(v[None, :], (128, v.shape[0])))

    m = {}
    x = np.asarray(inputs["L0_attn_a_hidden_states"], np.float32)
    m["x0"] = np.ascontiguousarray(x[b, h * NTOK:(h + 1) * NTOK, :])
    m["win_t"] = t(inputs["L0_attn_a_w_in"])
    m["wout_t"] = t(inputs["L0_attn_a_w_out"])
    m["wg_t"] = t(inputs["L0_attn_a_w_gate"])
    m["wu_t"] = t(inputs["L0_attn_a_w_up"])
    m["wd_t"] = t(inputs["L0_attn_a_w_down"])
    m["wq_t"] = t(inputs["L0_attn_b_wq"]); m["wk_t"] = t(inputs["L0_attn_b_wk"])
    m["wv_t"] = t(inputs["L0_attn_b_wv"]); m["wo_t"] = t(inputs["L0_attn_b_wo"])
    m["l0fw_t"] = t(inputs["L0_ffn_weight"])
    m["ww_t"] = t(inputs["L1_attn_a_W_w"]); m["outw_t"] = t(inputs["L1_attn_a_out_w"])
    m["retq"] = np.ascontiguousarray(np.asarray(inputs["L1_attn_b_weight_q"], np.float32))
    m["retk"] = np.ascontiguousarray(np.asarray(inputs["L1_attn_b_weight_k"], np.float32))
    m["retv"] = np.ascontiguousarray(np.asarray(inputs["L1_attn_b_weight_v"], np.float32))
    m["wr_t"] = t(inputs["L1_ffn_W_r"]); m["wk2_t"] = t(inputs["L1_ffn_W_k"])
    m["wv2_t"] = t(inputs["L1_ffn_W_v"])
    m["wln1_b"] = bc(inputs["L0_attn_a_w_ln1"])
    m["wmnorm_b"] = bc(inputs["L0_attn_a_w_mnorm"])
    m["wln2_b"] = bc(inputs["L0_attn_a_w_ln2"])
    m["bconv_fm"] = np.ascontiguousarray(
        np.asarray(inputs["L0_attn_a_b_conv"], np.float32).reshape(18, 128).T)
    m["wconv_fm"] = np.ascontiguousarray(
        np.asarray(inputs["L0_attn_a_w_conv"], np.float32).reshape(18, 128, 4)
        .transpose(1, 0, 2).reshape(128, 72))
    m["dtbias_fm"] = np.asarray(inputs["L0_attn_a_dt_bias"], np.float32).reshape(32, 1)
    m["alog_fm"] = np.asarray(inputs["L0_attn_a_A_log"], np.float32).reshape(32, 1)
    m["d_bcast"] = bc(inputs["L0_attn_a_D"])
    m["bq_b"] = bc(inputs["L0_attn_b_bq"]); m["bk_b"] = bc(inputs["L0_attn_b_bk"])
    m["bv_b"] = bc(inputs["L0_attn_b_bv"]); m["bo_b"] = bc(inputs["L0_attn_b_bo"])
    m["l0fb_b"] = bc(inputs["L0_ffn_bias"])
    m["wb_b"] = bc(inputs["L1_attn_a_W_b"]); m["outb_b"] = bc(inputs["L1_attn_a_out_b"])
    m["mur_b"] = bc(inputs["L1_ffn_mu_r"]); m["muk_b"] = bc(inputs["L1_ffn_mu_k"])
    m["mur_fm"] = np.ascontiguousarray(np.asarray(inputs["L1_ffn_mu_r"], np.float32).reshape(8, 128).T)
    m["wb_fm"] = np.ascontiguousarray(np.asarray(inputs["L1_attn_a_W_b"], np.float32).reshape(24, 128).T)
    m["bq_fm"] = np.ascontiguousarray(np.asarray(inputs["L0_attn_b_bq"], np.float32).reshape(8, 128).T)
    m["bk_fm"] = np.ascontiguousarray(np.asarray(inputs["L0_attn_b_bk"], np.float32).reshape(8, 128).T)
    m["muk_fm"] = np.ascontiguousarray(np.asarray(inputs["L1_ffn_mu_k"], np.float32).reshape(8, 128).T)
    m["ident"] = np.eye(128, dtype=np.float32)
    m["ones_row"] = np.ones((1, 128), np.float32)
    m["ones_col"] = np.ones((128, 1), np.float32)
    i = np.arange(128)
    m["tril01"] = (i[:, None] >= i[None, :]).astype(np.float32)
    m["triu_big"] = np.where(i[None, :] > i[:, None], np.float32(BIGNEG), 0.0).astype(np.float32)
    m["dtile"] = np.where(i[None, :] >= i[:, None],
                          np.float32(GAMMA) ** (i[None, :] - i[:, None]).astype(np.float32),
                          0.0).astype(np.float32)
    m["gp_col"] = (np.float32(GAMMA) ** np.arange(1, 129, dtype=np.float32)).reshape(128, 1)
    m["gk_col"] = (np.float32(GAMMA) ** np.arange(127, -1, -1, dtype=np.float32)).reshape(128, 1)
    pos = np.arange(h * NTOK, (h + 1) * NTOK, dtype=np.float32)
    inv = (10000.0 ** (np.arange(0, 64, 2, dtype=np.float32) / 64.0)) ** -1.0
    fr = pos[:, None] * inv[None, :]
    emb = np.concatenate([fr, fr], -1)
    m["rope_cos"] = np.cos(emb).astype(np.float32)
    m["rope_sin"] = np.sin(emb).astype(np.float32)
    P = np.zeros((128, 128), np.float32)
    for i in range(64):
        P[2 * i, 2 * i + 1] = -1.0
        P[2 * i + 1, 2 * i] = 1.0
    m["ropePT"] = np.ascontiguousarray(P.T)
    j = np.arange(128) % 64
    invj = inv[j % 32]
    m["cosfm"] = np.cos(pos[None, :] * invj[:, None]).astype(np.float32)
    m["sinfm"] = np.sin(pos[None, :] * invj[:, None]).astype(np.float32)
    g0 = 16 * h
    mr = np.zeros((16, 4224), np.float32)
    for s in range(16):
        mr[s, (g0 + s) * 128:4096] = BIGNEG
    m["mrow"] = mr.astype(ml_dtypes.bfloat16)
    m["ones_bf"] = np.ones((1, 128), ml_dtypes.bfloat16)
    m["constvals"] = np.ascontiguousarray(
        np.broadcast_to(np.array(CONST_VALUES, np.float32), (128, len(CONST_VALUES))))
    m["flags"] = np.ascontiguousarray(
        np.broadcast_to(np.array([float(h), 0.0, 0.0, 0.0], np.float32), (128, 4)))
    p = c ^ 1
    m["kvidx"] = (b * 4096 + np.arange(32 * 128, dtype=np.int32)).reshape(-1, 1)
    m["mbidx"] = (p * 128 + np.arange(128, dtype=np.int32)).reshape(-1, 1)
    m["cvidx"] = (p * 4 + np.arange(4, dtype=np.int32)).reshape(-1, 1)
    m["retidx"] = (p * 1024 + np.arange(1024, dtype=np.int32)).reshape(-1, 1)
    m["rwidx"] = (p * 128 + np.arange(128, dtype=np.int32)).reshape(-1, 1)
    m["gtcol"] = (float(h) * np.float32(GAMMA)
                  ** (np.arange(2048, dtype=np.float32).reshape(16, 128).T + 1.0)
                  ).astype(np.float32)
    ii = np.arange(128)
    m["triu01"] = (ii[:, None] <= ii[None, :]).astype(np.float32)
    m["tril_big"] = np.where(ii[None, :] < ii[:, None], np.float32(BIGNEG), 0.0).astype(np.float32)
    m["dtbias_b"] = bc(inputs["L0_attn_a_dt_bias"])
    m["alog_b"] = bc(inputs["L0_attn_a_A_log"])
    m["wln1_fm"] = np.ascontiguousarray(
        np.asarray(inputs["L0_attn_a_w_ln1"], np.float32).reshape(8, 128).T)
    m["wln2_fm"] = np.ascontiguousarray(
        np.asarray(inputs["L0_attn_a_w_ln2"], np.float32).reshape(8, 128).T)
    m["wmnorm_fm"] = np.ascontiguousarray(
        np.asarray(inputs["L0_attn_a_w_mnorm"], np.float32).reshape(16, 128).T)
    m["cvidx2"] = (p * 2304 + np.arange(2304, dtype=np.int32)).reshape(-1, 1)
    sadd = np.zeros((16, 32), np.float32)
    for s_ in range(16):
        sadd[s_, g0 // 128 * 0 + (16 * h + s_):] = np.float32(BIGNEG)
    m["sadd"] = np.ascontiguousarray(
        np.broadcast_to(sadd.reshape(1, 512), (128, 512)))
    wgu = np.empty((1024, 8192), np.float32)
    wu_t = t(inputs["L0_attn_a_w_up"])
    wg_t2 = t(inputs["L0_attn_a_w_gate"])
    for i2 in range(32):
        wgu[:, (2 * i2) * 128:(2 * i2 + 1) * 128] = wu_t[:, i2 * 128:(i2 + 1) * 128]
        wgu[:, (2 * i2 + 1) * 128:(2 * i2 + 2) * 128] = wg_t2[:, i2 * 128:(i2 + 1) * 128]
    m["wgu_t"] = wgu
    kt = np.zeros((4, 2, 2, 2, 128), np.int32)
    pp = np.arange(128, dtype=np.int32)
    for hg in range(4):
        for c2 in range(2):
            for half in range(2):
                for j in range(2):
                    kt[hg, c2, half, j] = (2 * b + half) * 4096 + 2 * (256 * hg + c2 * 128 + pp) + j
    m["k5idx"] = kt.reshape(-1, 1)[:32 * 128]
    m["k5idx"] = kt.reshape(-1, 1)
    vt = np.zeros((32, 128), np.int32)
    for k32 in range(32):
        t = k32 * 128 + pp
        vt[k32] = (2 * b + t // 2048) * 4096 + 2048 + (t % 2048)
    m["v5idx"] = vt.reshape(-1, 1)
    kt3 = np.zeros((4, 2, 2, 2, 128), np.int32)
    for hg in range(4):
        for c2 in range(2):
            for half in range(2):
                for j in range(2):
                    kt3[hg, c2, half, j] = ((2 * b + half) * 4352
                                            + 2 * (256 * hg + c2 * 128 + pp) + j)
    m["k3idx"] = kt3.reshape(-1, 1)
    vt3 = np.zeros((32, 128), np.int32)
    for k32 in range(32):
        t = k32 * 128 + pp
        vt3[k32] = (2 * b + t // 2048) * 4352 + 2048 + (t % 2048)
    m["v3idx"] = vt3.reshape(-1, 1)
    return m


def kernel(**inputs):
    key = ("prog", BUILD_SUBLAYERS, DEBUG_TAP, DEBUG_STOP)
    if key not in _cache:
        _cache[key] = build_program()
    nc = _cache[key]
    in_maps = [_stage_core_inputs(inputs, c) for c in range(NCORES)]
    res = run_bass_kernel_spmd(nc, in_maps, core_ids=list(range(NCORES)))
    out = np.empty((4, 4096, 1024), np.float32)
    for c in range(NCORES):
        b, h = divmod(c, 2)
        out[b, h * NTOK:(h + 1) * NTOK, :] = res.results[c]["y"]
    return out
